# version 08
# baseline (speedup 1.0000x reference)
import numpy as np
import concourse.bass as bass
import concourse.mybir as mybir
from concourse.bass_utils import run_bass_kernel_spmd
from contextlib import ExitStack

F32 = mybir.dt.float32
BF16 = mybir.dt.bfloat16
AF = mybir.ActivationFunctionType
ALU = mybir.AluOpType
AX = mybir.AxisListType

B, L, D = 4, 8192, 1024
H, DK, NE = 16, 64, 16
P = 128
KC = D // P
TPC = L // 2
NT = 4
QT = NT * P
TH = max(1, QT // 512)
EPS_LN = 1e-5
EPS_L2 = 1e-12

ENGINES = ("sync", "tensor", "vector", "scalar", "gpsimd")
SEMS = ("s_x", "s_stg", "s_wld", "s_wfree", "s_ln", "s_lnv",
        "s_mm", "s_ev", "s_dv", "s_gp", "s_st", "s_cache", "s_gx", "s_cv")

N_PREF_Q = TPC // QT
N_MAIN_Q = TPC // QT
N_EXP = NE


class EM:
    def __init__(self):
        self.q = {e: [] for e in ENGINES}
        self.cnt = {}
        self.sems = {}

    def sem(self, name, handle):
        self.sems[name] = handle
        self.cnt[name] = 0

    def inc(self, name, n=1):
        self.cnt[name] += n
        return self.cnt[name]

    def emit(self, eng, fn, inc=None, n=1):
        if inc is None:
            self.q[eng].append(fn)
        else:
            s = self.sems[inc]
            self.inc(inc, n)
            self.q[eng].append(lambda E, fn=fn, s=s, n=n: fn(E).then_inc(s, n))

    def wait(self, eng, name, val=None):
        s = self.sems[name]
        v = self.cnt[name] if val is None else val
        if v > 0:
            self.q[eng].append(lambda E, s=s, v=v: E.wait_ge(s, v))

    def barrier(self, engines=ENGINES, sems=("s_mm", "s_ev", "s_dv", "s_lnv",
                                             "s_ln", "s_gp")):
        for e in engines:
            for s in sems:
                self.wait(e, s)


def build_nc(n_pref_q=None, n_main_q=None, n_exp=None, dbg=False, ffn_stage=99):
    n_pref_q = N_PREF_Q if n_pref_q is None else n_pref_q
    n_main_q = N_MAIN_Q if n_main_q is None else n_main_q
    n_exp = N_EXP if n_exp is None else n_exp

    nc = bass.Bass("TRN2", target_bir_lowering=False, debug=False)
    em = EM()

    xm = nc.dram_tensor("xm", [TPC, D], F32, kind="ExternalInput").ap()
    xp = nc.dram_tensor("xp", [TPC, D], F32, kind="ExternalInput").ap()
    wT = {s: nc.dram_tensor(f"w{s}T", [D, D], F32, kind="ExternalInput").ap()
          for s in ("q", "k", "v", "g", "o")}
    cw_d = {s: nc.dram_tensor(f"cw{s}", [P, KC * 4], F32, kind="ExternalInput").ap()
            for s in ("q", "k", "v")}
    winit = nc.dram_tensor("winit", [DK, H, DK], F32, kind="ExternalInput").ap()
    gamma_t = nc.dram_tensor("gamma_t", [D], F32, kind="ExternalInput").ap()
    beta_t = nc.dram_tensor("beta_t", [D], F32, kind="ExternalInput").ap()
    wr = nc.dram_tensor("wr", [D, NE], F32, kind="ExternalInput").ap()
    br = nc.dram_tensor("br", [NE], F32, kind="ExternalInput").ap()
    w1 = nc.dram_tensor("w1", [NE, D, D], F32, kind="ExternalInput").ap()
    b1 = nc.dram_tensor("b1", [NE, D], F32, kind="ExternalInput").ap()
    g1 = nc.dram_tensor("g1", [NE, D], F32, kind="ExternalInput").ap()
    be1 = nc.dram_tensor("be1", [NE, D], F32, kind="ExternalInput").ap()
    w2 = nc.dram_tensor("w2", [NE, D, D], F32, kind="ExternalInput").ap()
    b2 = nc.dram_tensor("b2", [NE, D], F32, kind="ExternalInput").ap()
    out = nc.dram_tensor("out", [TPC, D], F32, kind="ExternalOutput").ap()
    out_t = out.rearrange("(i p) d -> i p d", p=P)

    dbg_t = {}
    if dbg:
        for nm, shape, dt in (("d_xT", [P, KC, QT], BF16),
                              ("d_qt", [P, NT, D], BF16),
                              ("d_kt", [P, NT, D], BF16),
                              ("d_vt", [P, NT, D], BF16),
                              ("d_gt", [P, NT, D], BF16),
                              ("d_wq", [P, NT, NE], F32),
                              ("d_osb", [P, D], F32),
                              ("d_ya", [P, NT, D], F32),
                              ("d_eL", [P, NE], F32),
                              ("d_sums", [P, 8], F32),
                              ("d_pb", [P, QT + 3], F32),
                              ("d_cb", [P, QT], F32),
                              ("d_qkf", [DK, KC, P], BF16),
                              ("d_qkm", [P, 4, P], BF16),
                              ("d_stb", [DK, H, DK], BF16)):
            dbg_t[nm] = nc.dram_tensor(nm, shape, dt, kind="ExternalOutput").ap()

    def dump(nm, src_ap):
        if dbg:
            em.barrier()
            em.emit("gpsimd", lambda g, o=dbg_t[nm], s=src_ap:
                    g.dma_start(out=o, in_=s), inc="s_st", n=16)

    snap_qkf = snap_qkm = snap_stb = snap_osb = None

    g1bf_t = nc.dram_tensor("g1bf", [NE, D], BF16, kind="Internal")
    be1bf_t = nc.dram_tensor("be1bf", [NE, D], BF16, kind="Internal")
    b1bf_t = nc.dram_tensor("b1bf", [NE, D], BF16, kind="Internal")
    g1bf, be1bf, b1bf = g1bf_t.ap(), be1bf_t.ap(), b1bf_t.ap()
    cache_proj = nc.dram_tensor("cache_proj", [5, P, KC, D], BF16, kind="Internal").ap()
    cache_w1 = nc.dram_tensor("cache_w1", [NE, P, KC, D], BF16, kind="Internal").ap()
    cache_w2 = nc.dram_tensor("cache_w2", [NE, P, KC, D], BF16, kind="Internal").ap()

    PROJ_IDX = {"q": 0, "k": 1, "v": 2, "g": 3, "o": 4}

    with ExitStack() as top:
        E = top.enter_context
        for s in SEMS:
            em.sem(s, E(nc.semaphore(s)))
        if dbg:
            snap_qkf = E(nc.sbuf_tensor("snap_qkf", [DK, KC, P], BF16))
            snap_qkm = E(nc.sbuf_tensor("snap_qkm", [P, 4, P], BF16))
            snap_stb = E(nc.sbuf_tensor("snap_stb", [DK, H, DK], BF16))
            snap_osb = E(nc.sbuf_tensor("snap_osb", [P, D], F32))

        identb = E(nc.sbuf_tensor("identb", [P, P], BF16, side="right"))
        identf = E(nc.sbuf_tensor("identf", [P, P], F32, side="right"))
        maskT = E(nc.sbuf_tensor("maskT", [P, P], F32, side="right"))
        ones1f = E(nc.sbuf_tensor("ones1f", [1, P], F32, side="right"))
        ones1b = E(nc.sbuf_tensor("ones1b", [1, P], BF16, side="right"))
        onecol = E(nc.sbuf_tensor("onecol", [P, 1], F32, side="right"))
        epsln = E(nc.sbuf_tensor("epsln", [P, 1], F32, side="right"))
        epsl2 = E(nc.sbuf_tensor("epsl2", [P, 1], F32, side="right"))
        gamB = E(nc.sbuf_tensor("gamB", [P, D], F32, side="right"))
        betB = E(nc.sbuf_tensor("betB", [P, D], F32, side="right"))
        winitB = E(nc.sbuf_tensor("winitB", [DK, H, DK], F32, side="right"))
        cw_sb = {s: E(nc.sbuf_tensor(f"cw_{s}", [P, KC * 4], F32, side="right"))
                 for s in ("q", "k", "v")}
        wr_sb = E(nc.sbuf_tensor("wr_sb", [P, KC, NE], BF16, side="right"))
        brb = E(nc.sbuf_tensor("brb", [1, NE], BF16, side="right"))
        b2sb = E(nc.sbuf_tensor("b2sb", [NE, D], BF16, side="right"))
        setup_ctx = ExitStack()
        b2f = setup_ctx.enter_context(nc.sbuf_tensor("b2f", [NE, D], F32))
        rowf = setup_ctx.enter_context(nc.sbuf_tensor("rowf", [NE, D], F32))
        rowb = setup_ctx.enter_context(nc.sbuf_tensor("rowb", [NE, D], BF16))
        grow = setup_ctx.enter_context(nc.sbuf_tensor("grow", [1, D], F32))
        grow2 = setup_ctx.enter_context(nc.sbuf_tensor("grow2", [1, D], F32))
        sums = E(nc.sbuf_tensor("sums", [P, 8], F32, side="right"))
        s16a = E(nc.sbuf_tensor("s16a", [P, NE], F32, side="right"))
        s16b = E(nc.sbuf_tensor("s16b", [P, NE], F32, side="right"))
        s16c = E(nc.sbuf_tensor("s16c", [P, NE], F32, side="right"))
        eL = E(nc.sbuf_tensor("eL", [P, NE], F32, side="right"))
        rtmp = E(nc.sbuf_tensor("rtmp", [P, NE], F32, side="right"))
        scr1 = E(nc.sbuf_tensor("scr1", [P, D], F32, side="right"))
        halo = {s: E(nc.sbuf_tensor(f"halo_{s}", [P, KC * 3], F32, side="right"))
                for s in ("q", "k", "v")}
        bstage = [E(nc.sbuf_tensor(f"bstg{j}", [P, D], F32, side="right"))
                  for j in range(2)]
        bconv = [E(nc.sbuf_tensor(f"bcv{j}", [P, D], BF16, side="right"))
                 for j in range(2)]
        A_sb = [E(nc.sbuf_tensor(f"A_sb{j}", [DK, H, DK], F32, side="right"))
                for j in range(2)]

        def gp_setup(g):
            g.memset(identb[:], 0.0)
            g.memset(identf[:], 0.0)
            g.memset(maskT[:], 1.0)
            g.memset(ones1f[:], 1.0)
            g.memset(ones1b[:], 1.0)
            g.memset(onecol[:], 1.0)
            g.memset(epsln[:], EPS_LN)
            g.memset(epsl2[:], EPS_L2)
            for s in ("q", "k", "v"):
                g.memset(halo[s][:], 0.0)
            g.memset(sums[:], 0.0)
            g.memset(s16a[:], 0.0)
            g.memset(s16b[:], 0.0)
            g.memset(s16c[:], 0.0)
            g.memset(eL[:], 0.0)
            g.memset(rtmp[:], 0.0)
            g.memset(scr1[:], 0.0)
            g.affine_select(out=identb[:], in_=identb[:],
                            compare_op=ALU.not_equal, fill=1.0, base=0,
                            pattern=[[-1, P]], channel_multiplier=1)
            g.affine_select(out=identf[:], in_=identf[:],
                            compare_op=ALU.not_equal, fill=1.0, base=0,
                            pattern=[[-1, P]], channel_multiplier=1)
            return g.affine_select(out=maskT[:], in_=maskT[:],
                                   compare_op=ALU.is_ge, fill=0.0, base=0,
                                   pattern=[[1, P]], channel_multiplier=-1)
        em.emit("gpsimd", gp_setup, inc="s_gp")

        setup_loads = [
            (winitB[:], winit),
            (cw_sb["q"][:], cw_d["q"]),
            (cw_sb["k"][:], cw_d["k"]),
            (cw_sb["v"][:], cw_d["v"]),
            (grow[:], gamma_t[None, :]),
            (grow2[:], beta_t[None, :]),
            (b2f[:], b2),
            (scr1[0:1, 0:NE], br[None, :]),
            (scr1[:, 128:128 + KC * NE].rearrange("p (c e) -> p c e", c=KC),
             wr.rearrange("(c p) e -> p c e", p=P)),
        ]
        for o_, i_ in setup_loads:
            em.emit("sync", lambda sy, o=o_, i=i_: sy.dma_start(out=o, in_=i),
                    inc="s_x", n=16)

        em.wait("vector", "s_x")
        def v_setup0(v):
            return v.tensor_copy(out=A_sb[0][:], in_=winitB[:])
        em.emit("vector", v_setup0, inc="s_dv")
        for srcd, dstd in ((g1, g1bf), (be1, be1bf), (b1, b1bf)):
            em.emit("sync", lambda sy, s=srcd: sy.dma_start(out=rowf[:], in_=s),
                    inc="s_x", n=16)
            em.wait("vector", "s_x")
            em.emit("vector", lambda v: v.tensor_copy(out=rowb[:], in_=rowf[:]),
                    inc="s_dv")
            em.wait("gpsimd", "s_dv")
            em.emit("gpsimd", lambda g, d=dstd: g.dma_start(out=d, in_=rowb[:]),
                    inc="s_cache", n=16)
            em.wait("sync", "s_cache")
        def v_setup1(v):
            for si_, st in enumerate(("q", "k", "v")):
                for mc in range(KC):
                    sl = slice(mc * 4 + 3, mc * 4 + 4)
                    v.tensor_scalar(out=scr1[:, 32 + si_ * KC + mc: 33 + si_ * KC + mc],
                                    in0=cw_sb[st][:, sl],
                                    scalar1=1.0, scalar2=None, op0=ALU.add)
            v.tensor_copy(out=b2sb[:], in_=b2f[:])
            v.tensor_copy(out=brb[:], in_=scr1[0:1, 0:NE])
            return v.tensor_copy(
                out=wr_sb[:], in_=scr1[:, 128:128 + KC * NE].rearrange(
                    "p (c e) -> p c e", c=KC))
        em.emit("vector", v_setup1, inc="s_dv")
        em.wait("vector", "s_dv")
        def v_setup2(v):
            for si_, st in enumerate(("q", "k", "v")):
                for mc in range(KC):
                    sl = slice(mc * 4 + 3, mc * 4 + 4)
                    v.tensor_copy(out=cw_sb[st][:, sl],
                                  in_=scr1[:, 32 + si_ * KC + mc: 33 + si_ * KC + mc])
            return v.tensor_copy(out=scr1[:, 32:33], in_=scr1[:, 33:34])
        em.emit("vector", v_setup2, inc="s_dv")

        with nc.psum_tensor("ps_gb", [P, 512], F32) as ps_gb, \
             nc.psum_tensor("ps_gb2", [P, 512], F32) as ps_gb2:
            em.wait("tensor", "s_x")
            em.wait("tensor", "s_gp")
            for hf in range(2):
                sl = slice(hf * 512, (hf + 1) * 512)
                em.emit("tensor", lambda t, sl=sl: t.matmul(
                    ps_gb[:], ones1f[:], grow[:, sl], start=True, stop=True))
                em.emit("tensor", lambda t, sl=sl: t.matmul(
                    ps_gb2[:], ones1f[:], grow2[:, sl], start=True, stop=True),
                    inc="s_mm")
                em.wait("scalar", "s_mm")
                em.emit("scalar", lambda s, sl=sl: s.copy(
                    out=gamB[:, sl], in_=ps_gb[:]))
                em.emit("scalar", lambda s, sl=sl: s.copy(
                    out=betB[:, sl], in_=ps_gb2[:]), inc="s_ev")
                em.wait("tensor", "s_ev")

        em.barrier()
        setup_ctx.close()

        state = {"proj_cached": set(), "exp_cached": False,
                 "nstream": 0, "A_par": 0, "A_upd": 0,
                 "cache_done": {}, "bpar": 0}

        def emit_cache_build(name, src_fp32, cache_ap):
            for kc in range(KC):
                j = state["bpar"] & 1
                state["bpar"] += 1
                em.wait("gpsimd", "s_cv")
                em.emit("gpsimd", lambda g, o=bstage[j][:], s=src_fp32[:, kc, :]:
                        g.dma_start(out=o, in_=s), inc="s_stg", n=16)
                em.wait("vector", "s_stg")
                em.wait("vector", "s_cache")
                em.emit("vector", lambda v, i=bstage[j][:], o=bconv[j][:]:
                        v.tensor_copy(out=o, in_=i), inc="s_cv")
                em.wait("gpsimd", "s_cv")
                em.emit("gpsimd", lambda g, i=bconv[j][:], o=cache_ap[:, kc, :]:
                        g.dma_start(out=o, in_=i), inc="s_cache", n=16)
            state["cache_done"][name] = em.cnt["s_cache"]

        def emit_ln_quarter(ctx_unused, src_ap, uid, xT):
            lctx = ExitStack()
            xld = [lctx.enter_context(nc.sbuf_tensor(f"xld{j}_{uid}", [P, D], F32))
                   for j in range(2)]
            xnt = [lctx.enter_context(nc.sbuf_tensor(f"xnt{j}_{uid}", [P, D], BF16))
                   for j in range(2)]
            ps_tp = [lctx.enter_context(nc.psum_tensor(f"lntp{j}_{uid}",
                                                      [P, KC, P], BF16))
                     for j in range(2)]
            src_t = src_ap.rearrange("(i p) d -> i p d", p=P)
            for i in range(NT):
                j = i % 2
                if i >= 2:
                    em.wait("sync", "s_lnv")
                    em.wait("sync", "s_ev")
                em.emit("sync", lambda sy, o=xld[j][:], s=src_t[i]:
                        sy.dma_start(out=o, in_=s), inc="s_x", n=16)
                em.wait("vector", "s_x")
                em.wait("vector", "s_ev")
                em.emit("vector", lambda v, x=xld[j][:]: v.tensor_reduce(
                    out=sums[:, 0:1], in_=x, axis=AX.X, op=ALU.add), inc="s_lnv")
                em.wait("scalar", "s_lnv")
                em.wait("scalar", "s_dv")
                em.emit("scalar", lambda s, x=xld[j][:]: s.activation(
                    out=scr1[:], in_=x, func=AF.Square, accum_out=sums[:, 1:2]),
                    inc="s_ev")
                em.wait("vector", "s_ev")
                em.emit("vector", lambda v: v.tensor_scalar(
                    out=sums[:, 2:3], in0=sums[:, 0:1], scalar1=1.0 / D,
                    scalar2=None, op0=ALU.mult), inc="s_dv")
                em.wait("vector", "s_dv")
                em.emit("vector", lambda v: v.tensor_tensor(
                    out=sums[:, 3:4], in0=sums[:, 2:3], in1=sums[:, 2:3],
                    op=ALU.mult), inc="s_dv")
                em.wait("vector", "s_dv")
                em.emit("vector", lambda v: v.scalar_tensor_tensor(
                    out=sums[:, 4:5], in0=sums[:, 1:2], scalar=1.0 / D,
                    in1=sums[:, 3:4], op0=ALU.mult, op1=ALU.subtract),
                    inc="s_dv")
                em.wait("scalar", "s_dv")
                em.emit("scalar", lambda s: s.activation(
                    out=sums[:, 5:6], in_=sums[:, 4:5], func=AF.Sqrt,
                    bias=epsln[:], scale=1.0), inc="s_ev")
                em.wait("vector", "s_ev")
                em.emit("vector", lambda v: v.reciprocal(
                    out=sums[:, 6:7], in_=sums[:, 5:6]), inc="s_dv")
                em.wait("vector", "s_dv")
                em.emit("vector", lambda v, x=xld[j][:], o=xnt[j][:]:
                        v.tensor_scalar(out=o, in0=x, scalar1=sums[:, 2:3],
                                        scalar2=sums[:, 6:7], op0=ALU.subtract,
                                        op1=ALU.mult), inc="s_lnv")
                em.wait("tensor", "s_lnv")
                if i >= 2:
                    em.wait("tensor", "s_ln")
                for c in range(KC):
                    em.emit("tensor", lambda t, c=c, ps=ps_tp[j],
                            src=xnt[j][:, c * P:(c + 1) * P]:
                            t.transpose(ps[:, c, :], src, identb[:]),
                            inc=("s_mm" if c == KC - 1 else None))
                em.wait("scalar", "s_mm")
                em.emit("scalar", lambda s, ps=ps_tp[j],
                        o=xT[:, :, i * P:(i + 1) * P]:
                        s.copy(out=o, in_=ps[:]), inc="s_ln")
            em.barrier()
            lctx.close()

        def emit_weight_load(dst_sb, cache_ap, name, wait_free_val):
            em.wait("sync", "s_wfree", wait_free_val)
            em.wait("sync", "s_cache", state["cache_done"][name])
            em.emit("sync", lambda sy, o=dst_sb[:], s=cache_ap:
                    sy.dma_start(out=o, in_=s), inc="s_wld", n=16)
            return em.cnt["s_wld"]

        def emit_proj_stream(sname, xT, dest, uid, do_conv, do_norm, wp):
            si = PROJ_IDX[sname]
            with ExitStack() as ctx:
                psP = [ctx.enter_context(nc.psum_tensor(
                    f"psP{j}_{uid}_{sname}", [P, min(512, QT)], F32)) for j in range(2)]
                ps_tp = [ctx.enter_context(nc.psum_tensor(
                    f"ptp{j}_{uid}_{sname}", [P, KC, P], BF16)) for j in range(2)]
                pbuf = [ctx.enter_context(nc.sbuf_tensor(
                    f"pb{j}_{uid}_{sname}", [P, QT + 3], F32)) for j in range(2)]
                cA = ctx.enter_context(nc.sbuf_tensor(f"cA_{uid}_{sname}", [P, QT], F32))
                cB = ctx.enter_context(nc.sbuf_tensor(f"cB_{uid}_{sname}", [P, QT], F32))
                csb = ctx.enter_context(nc.sbuf_tensor(f"csb_{uid}_{sname}",
                                                       [P, KC, QT], BF16))
                knr = ctx.enter_context(nc.sbuf_tensor(
                    f"knr_{uid}_{sname}", [P, D], BF16)) if do_norm else None

                w_ready = emit_weight_load(
                    wp, cache_proj[si], f"proj_{sname}", state["nstream"])
                state["nstream"] += 1

                em.wait("tensor", "s_wld", w_ready)
                em.wait("tensor", "s_ev")
                for u in range(TH * KC):
                    mc, th = u // TH, u % TH
                    j = u % 2
                    if u >= 2:
                        em.wait("tensor", "s_ev")
                    for kc in range(KC):
                        em.emit("tensor", lambda t, ps=psP[j],
                                l=wp[:, kc, mc * P:(mc + 1) * P],
                                r=xT[:, kc, th * 512:min(QT, (th + 1) * 512)],
                                st=(kc == 0), sp=(kc == KC - 1):
                                t.matmul(ps[:], l, r, start=st, stop=sp,
                                         skip_group_check=True),
                                inc=("s_mm" if kc == KC - 1 else None))
                    em.wait("scalar", "s_mm")
                    if do_conv:
                        em.wait("scalar", "s_dv")
                        em.emit("scalar", lambda s, ps=psP[j],
                                o=pbuf[mc % 2][:, 3 + th * 512: 3 + min(QT, (th + 1) * 512)]:
                                s.copy(out=o, in_=ps[:]), inc="s_ev")
                    else:
                        em.emit("scalar", lambda s, ps=psP[j],
                                o=csb[:, mc, th * 512:min(QT, (th + 1) * 512)]:
                                s.copy(out=o, in_=ps[:]), inc="s_ev")
                    if th == TH - 1 and do_conv:
                        jj = mc % 2
                        cw = cw_sb[sname]
                        em.wait("vector", "s_ev")
                        em.emit("vector", lambda v, o=pbuf[jj][:, 0:3],
                                i=halo[sname][:, mc * 3:(mc + 1) * 3]:
                                v.tensor_copy(out=o, in_=i))
                        em.emit("vector", lambda v, p=pbuf[jj][:, 3:3 + QT],
                                w3=cw[:, mc * 4 + 3: mc * 4 + 4]:
                                v.tensor_scalar(out=cA[:], in0=p, scalar1=w3,
                                                scalar2=None, op0=ALU.mult))
                        chain = [(cA, cB), (cB, cA), (cA, cB)]
                        for d in (1, 2, 3):
                            ai, ao = chain[d - 1]
                            em.emit("vector", lambda v, d=d,
                                    p=pbuf[jj][:, 3 - d:3 + QT - d],
                                    w=cw[:, mc * 4 + (3 - d): mc * 4 + (4 - d)],
                                    ai=ai, ao=ao:
                                    v.scalar_tensor_tensor(
                                        out=ao[:], in0=p, scalar=w, in1=ai[:],
                                        op0=ALU.mult, op1=ALU.add))
                        em.emit("vector", lambda v, i=pbuf[jj][:, QT:QT + 3],
                                o=halo[sname][:, mc * 3:(mc + 1) * 3]:
                                v.tensor_copy(out=o, in_=i), inc="s_dv")
                        em.wait("scalar", "s_dv")
                        em.emit("scalar", lambda s, o=csb[:, mc, :]:
                                s.activation(out=o, in_=cB[:], func=AF.Silu),
                                inc="s_ev")
                        if dbg and sname == "v" and uid == "m0" and mc == KC - 1:
                            em.wait("gpsimd", "s_ev")
                            em.emit("gpsimd", lambda g, pb=pbuf[jj]: g.dma_start(
                                out=dbg_t["d_pb"], in_=pb[:]), inc="s_st", n=16)
                            em.emit("gpsimd", lambda g: g.dma_start(
                                out=dbg_t["d_cb"], in_=cB[:]), inc="s_st", n=16)
                            em.wait("scalar", "s_st")
                            em.wait("vector", "s_st")
                if dest is not None:
                    for i in range(NT):
                        j = i % 2
                        em.wait("tensor", "s_ev")
                        if i >= 2:
                            em.wait("tensor", "s_ln")
                        if do_norm:
                            em.wait("tensor", "s_lnv")
                        for c in range(KC):
                            em.emit("tensor", lambda t, c=c, ps=ps_tp[j],
                                    src=csb[:, c, i * P:(i + 1) * P]:
                                    t.transpose(ps[:, c, :], src, identb[:]),
                                    inc=("s_mm" if c == KC - 1 else None))
                        em.wait("scalar", "s_mm")
                        tgt = knr[:] if do_norm else dest[:, i, :]
                        em.emit("scalar", lambda s, ps=ps_tp[j], o=tgt:
                                s.copy(out=o, in_=ps[:]), inc="s_ln")
                        if do_norm:
                            em.wait("scalar", "s_lnv")
                            em.wait("scalar", "s_dv")
                            em.emit("scalar", lambda s: s.activation(
                                out=scr1[:], in_=knr[:], func=AF.Square),
                                inc="s_ev")
                            em.wait("vector", "s_ev")
                            em.emit("vector", lambda v: v.tensor_reduce(
                                out=s16a[:],
                                in_=scr1[:].rearrange("p (h k) -> p h k", k=DK),
                                axis=AX.X, op=ALU.add), inc="s_dv")
                            em.wait("scalar", "s_dv")
                            em.emit("scalar", lambda s: s.activation(
                                out=s16b[:], in_=s16a[:], func=AF.Sqrt,
                                bias=epsl2[:], scale=1.0), inc="s_ev")
                            em.wait("vector", "s_ev")
                            em.emit("vector", lambda v: v.reciprocal(
                                out=s16c[:], in_=s16b[:]), inc="s_dv")
                            em.wait("vector", "s_dv")
                            for h in range(H):
                                em.emit("vector", lambda v, h=h, i=i:
                                        v.tensor_scalar(
                                            out=dest[:, i, h * DK:(h + 1) * DK],
                                            in0=knr[:, h * DK:(h + 1) * DK],
                                            scalar1=s16c[:, h:h + 1],
                                            scalar2=None, op0=ALU.mult),
                                        inc=("s_lnv" if h == H - 1 else None))
                em.emit("tensor", lambda t: t.nop(), inc="s_wfree")
                em.barrier()

        def emit_A_update(A_tmp, k_t, v_t, i):
            em.wait("tensor", "s_dv", state["A_upd"])
            for h in range(H):
                em.emit("tensor", lambda t, h=h,
                        st=(h == 0 or h == 8),
                        l=k_t[:, i, h * DK:(h + 1) * DK],
                        r=v_t[:, i, h * DK:(h + 1) * DK]:
                        t.matmul(A_tmp[:, h, :], l, r, start=st, stop=True,
                                 skip_group_check=True),
                        inc=("s_mm" if h == H - 1 else None))
            em.wait("vector", "s_mm")
            par = state["A_par"]
            em.emit("vector", lambda v, a=A_sb[par][:], b=A_sb[par ^ 1][:],
                    ps=A_tmp[:]:
                    v.tensor_tensor(out=b, in0=a, in1=ps, op=ALU.add),
                    inc="s_dv")
            state["A_par"] = par ^ 1
            state["A_upd"] = em.cnt["s_dv"]

        def emit_scan_quarter(ctx, q, uid, q_t, k_t, v_t, gate_t, y_acc,
                              wo_sb, xsrc):
            ps_sm = [ctx.enter_context(nc.psum_tensor(f"pssm{j}_{uid}",
                                                      [P, KC, P], BF16))
                     for j in range(2)]
            ps_qk = [ctx.enter_context(nc.psum_tensor(f"psqk{j}_{uid}",
                                                      [P, 4, P], F32))
                     for j in range(2)]
            ps_o = ctx.enter_context(nc.psum_tensor(f"pso_{uid}", [P, H, DK], F32))
            A_tmp = ctx.enter_context(nc.psum_tensor(f"Atmp_{uid}", [DK, H, DK], F32))
            qkf = [ctx.enter_context(nc.sbuf_tensor(f"qkf{j}_{uid}",
                                                    [DK, KC, P], BF16))
                   for j in range(2)]
            qkm = [ctx.enter_context(nc.sbuf_tensor(f"qkm{j}_{uid}",
                                                    [P, 4, P], BF16))
                   for j in range(2)]
            st_b = ctx.enter_context(nc.sbuf_tensor(f"stb_{uid}", [DK, H, DK], BF16))
            o_sb = ctx.enter_context(nc.sbuf_tensor(f"osb_{uid}", [P, D], F32))
            scr2 = ctx.enter_context(nc.sbuf_tensor(f"oscr_{uid}", [P, D], F32))
            on_sb = scr1
            og_sb = ctx.enter_context(nc.sbuf_tensor(f"ogsb_{uid}", [P, D], F32))
            og2 = scr2
            ogb = ctx.enter_context(nc.sbuf_tensor(f"ogb_{uid}", [P, D], BF16))
            ogT = ctx.enter_context(nc.sbuf_tensor(f"ogT_{uid}", [P, KC, P], BF16))
            xld2 = [ctx.enter_context(nc.sbuf_tensor(f"x2{j}_{uid}", [P, D], F32))
                    for j in range(2)]
            xsrc_t = xsrc.rearrange("(i p) d -> i p d", p=P)

            for i in range(NT):
                em.wait("vector", "s_dv", state["A_upd"])
                em.emit("vector", lambda v, a=A_sb[state["A_par"]][:]:
                        v.tensor_copy(out=st_b[:], in_=a), inc="s_dv")
                st_ready = em.cnt["s_dv"]
                if i >= 2:
                    em.wait("gpsimd", "s_ev")
                em.emit("gpsimd", lambda g, o=xld2[i % 2][:], s=xsrc_t[i]:
                        g.dma_start(out=o, in_=s), inc="s_gx", n=16)
                for grp in range(4):
                    j = grp % 2
                    em.wait("tensor", "s_ev")
                    for hh in range(4):
                        h = grp * 4 + hh
                        em.emit("tensor", lambda t, hh=hh, ps=ps_sm[j],
                                src=q_t[:, i, h * DK:(h + 1) * DK]:
                                t.transpose(ps[0:DK, hh * 2, :], src,
                                            identb[:]))
                        em.emit("tensor", lambda t, hh=hh, ps=ps_sm[j],
                                src=k_t[:, i, h * DK:(h + 1) * DK]:
                                t.transpose(ps[0:DK, hh * 2 + 1, :], src,
                                            identb[:]),
                                inc=("s_mm" if hh == 3 else None))
                    em.wait("scalar", "s_mm")
                    em.emit("scalar", lambda s, ps=ps_sm[j], o=qkf[j][:]:
                            s.copy(out=o, in_=ps[0:DK]), inc="s_ev")
                    em.wait("tensor", "s_ev")
                    em.wait("tensor", "s_dv")
                    for hh in range(4):
                        em.emit("tensor", lambda t, hh=hh, ps=ps_qk[j], f=qkf[j]:
                                t.matmul(ps[:, hh, :], f[:, hh * 2 + 1, :],
                                         f[:, hh * 2, :], start=(hh == 0),
                                         stop=True, skip_group_check=True),
                                inc=("s_mm" if hh == 3 else None))
                    em.wait("vector", "s_mm")
                    em.wait("vector", "s_dv", st_ready)
                    for hh in range(4):
                        em.emit("vector", lambda v, hh=hh, ps=ps_qk[j], m=qkm[j]:
                                v.tensor_tensor(out=m[:, hh, :],
                                                in0=ps[:, hh, :], in1=maskT[:],
                                                op=ALU.mult),
                                inc=("s_dv" if hh == 3 else None))
                    if dbg and uid == "m0" and i == 0 and grp == 0:
                        em.emit("scalar", lambda s: s.copy(
                            out=snap_qkf[:], in_=qkf[0][:]))
                        em.wait("vector", "s_dv")
                        em.emit("vector", lambda v: v.tensor_copy(
                            out=snap_qkm[:], in_=qkm[0][:]))
                        em.emit("vector", lambda v: v.tensor_copy(
                            out=snap_stb[:], in_=st_b[:]), inc="s_dv")
                    em.wait("tensor", "s_dv")
                    for hh in range(4):
                        h = grp * 4 + hh
                        em.emit("tensor", lambda t, hh=hh, h=h, m=qkm[j],
                                vsrc=v_t[:, i, h * DK:(h + 1) * DK]:
                                t.matmul(ps_o[:, h, :], m[:, hh, :], vsrc,
                                         start=(h == 0 or h == 8), stop=False,
                                         skip_group_check=True))
                        em.emit("tensor", lambda t, hh=hh, h=h, f=qkf[j]:
                                t.matmul(ps_o[:, h, :], f[:, hh * 2, :],
                                         st_b[:, h, :], start=False, stop=True,
                                         skip_group_check=True),
                                inc=("s_mm" if hh == 3 else None))
                emit_A_update(A_tmp, k_t, v_t, i)
                em.wait("scalar", "s_mm")
                em.wait("scalar", "s_dv")
                em.emit("scalar", lambda s: s.copy(out=o_sb[:], in_=ps_o[:]),
                        inc="s_ev")
                if dbg and q == 0 and i == 0:
                    em.emit("scalar", lambda s: s.copy(
                        out=snap_osb[:], in_=o_sb[:]), inc="s_ev")
                em.emit("scalar", lambda s: s.activation(
                    out=scr2[:], in_=o_sb[:], func=AF.Square), inc="s_ev")
                em.wait("vector", "s_ev")
                em.emit("vector", lambda v: v.tensor_reduce(
                    out=s16a[:], in_=o_sb[:].rearrange("p (h k) -> p h k", k=DK),
                    axis=AX.X, op=ALU.add))
                em.emit("vector", lambda v: v.tensor_reduce(
                    out=s16b[:], in_=scr2[:].rearrange("p (h k) -> p h k", k=DK),
                    axis=AX.X, op=ALU.add), inc="s_dv")
                em.wait("vector", "s_dv")
                em.emit("vector", lambda v: v.tensor_scalar(
                    out=rtmp[:], in0=s16a[:], scalar1=1.0 / DK, scalar2=None,
                    op0=ALU.mult), inc="s_dv")
                em.wait("vector", "s_dv")
                em.emit("vector", lambda v: v.tensor_tensor(
                    out=s16a[:], in0=rtmp[:], in1=rtmp[:], op=ALU.mult),
                    inc="s_dv")
                em.wait("vector", "s_dv")
                em.emit("vector", lambda v: v.scalar_tensor_tensor(
                    out=eL[:], in0=s16b[:], scalar=1.0 / DK, in1=s16a[:],
                    op0=ALU.mult, op1=ALU.subtract), inc="s_dv")
                em.wait("scalar", "s_dv")
                em.emit("scalar", lambda s: s.activation(
                    out=s16b[:], in_=eL[:], func=AF.Sqrt, bias=epsln[:],
                    scale=1.0), inc="s_ev")
                em.wait("vector", "s_ev")
                em.emit("vector", lambda v: v.reciprocal(
                    out=s16a[:], in_=s16b[:]), inc="s_dv")
                em.wait("vector", "s_dv")
                for h in range(H):
                    em.emit("vector", lambda v, h=h: v.tensor_scalar(
                        out=on_sb[:, h * DK:(h + 1) * DK],
                        in0=o_sb[:, h * DK:(h + 1) * DK],
                        scalar1=rtmp[:, h:h + 1], scalar2=s16a[:, h:h + 1],
                        op0=ALU.subtract, op1=ALU.mult))
                em.emit("vector", lambda v: v.scalar_tensor_tensor(
                    out=og_sb[:], in0=on_sb[:], scalar=1.0, in1=gamB[:],
                    op0=ALU.mult, op1=ALU.mult), inc="s_dv")
                em.wait("gpsimd", "s_dv")
                em.emit("gpsimd", lambda g: g.tensor_tensor(
                    out=og2[:], in0=og_sb[:], in1=betB[:], op=ALU.add),
                    inc="s_gp")
                em.wait("vector", "s_gp")
                em.wait("vector", "s_gx")
                em.emit("vector", lambda v, i=i: v.tensor_tensor(
                    out=ogb[:], in0=og2[:], in1=gate_t[:, i, :], op=ALU.mult),
                    inc="s_dv")
                em.wait("tensor", "s_dv")
                em.wait("tensor", "s_ev")
                for c in range(KC):
                    em.emit("tensor", lambda t, c=c: t.transpose(
                        ps_sm[0][:, c, :], ogb[:, c * P:(c + 1) * P], identb[:]),
                        inc=("s_mm" if c == KC - 1 else None))
                em.wait("scalar", "s_mm")
                em.emit("scalar", lambda s: s.copy(out=ogT[:], in_=ps_sm[0][:]),
                        inc="s_ev")
                em.wait("tensor", "s_ev")
                po = ps_o[:].rearrange("p h k -> p (h k)")
                for th in range(2):
                    for kc in range(KC):
                        em.emit("tensor", lambda t, th=th, kc=kc, po=po:
                                t.matmul(po[:, th * 512:(th + 1) * 512],
                                         ogT[:, kc, :],
                                         wo_sb[:, kc, th * 512:(th + 1) * 512],
                                         start=(kc == 0), stop=(kc == KC - 1),
                                         skip_group_check=True),
                                inc=("s_mm" if th == 1 and kc == KC - 1
                                     else None))
                em.wait("vector", "s_mm")
                em.emit("vector", lambda v, i=i, po=po: v.tensor_tensor(
                    out=y_acc[:, i, :], in0=po, in1=xld2[i % 2][:], op=ALU.add),
                    inc="s_ev")

        def emit_router_quarter(xT, w_q, wTq, uid):
            with ExitStack() as rctx:
                ps_r_raw = rctx.enter_context(nc.psum_tensor(f"psr_{uid}",
                                                             [P, 512], F32))
                ps_r = ps_r_raw[:].rearrange("p (i e) -> p i e", e=NE)
                ps_wt = rctx.enter_context(nc.psum_tensor(f"pswt_{uid}",
                                                          [NE, 4, P], F32))
                em.wait("tensor", "s_ev")
                for i in range(NT):
                    for kc in range(KC):
                        em.emit("tensor", lambda t, i=i, kc=kc: t.matmul(
                            ps_r[:, i, :], xT[:, kc, i * P:(i + 1) * P],
                            wr_sb[:, kc, :], start=(i == 0 and kc == 0),
                            stop=False, skip_group_check=True))
                    em.emit("tensor", lambda t, i=i: t.matmul(
                        ps_r[:, i, :], ones1b[:], brb[:], start=False,
                        stop=True, skip_group_check=True), inc="s_mm")
                    em.wait("vector", "s_mm")
                    em.wait("vector", "s_ev")
                    em.emit("vector", lambda v, i=i: v.reduce_max(
                        out=sums[:, 0:1], in_=ps_r[:, i, :], axis=AX.X,
                        negate=True), inc="s_dv")
                    em.wait("scalar", "s_dv")
                    em.emit("scalar", lambda s, i=i: s.activation(
                        out=eL[:], in_=ps_r[:, i, :], func=AF.Exp,
                        bias=sums[:, 0:1], scale=1.0), inc="s_ev")
                    em.wait("vector", "s_ev")
                    em.emit("vector", lambda v: v.reduce_max(
                        out=sums[:, 1:2], in_=eL[:], axis=AX.X), inc="s_dv")
                    em.wait("vector", "s_dv")
                    em.emit("vector", lambda v: v.tensor_scalar(
                        out=rtmp[:], in0=eL[:], scalar1=sums[:, 1:2],
                        scalar2=0.0, op0=ALU.subtract, op1=ALU.is_ge),
                        inc="s_dv")
                    em.wait("vector", "s_dv")
                    em.emit("vector", lambda v: v.tensor_tensor(
                        out=s16a[:], in0=eL[:], in1=rtmp[:], op=ALU.mult),
                        inc="s_dv")
                    em.wait("vector", "s_dv")
                    em.emit("vector", lambda v: v.tensor_tensor(
                        out=s16b[:], in0=eL[:], in1=s16a[:], op=ALU.subtract),
                        inc="s_dv")
                    em.wait("vector", "s_dv")
                    em.emit("vector", lambda v: v.reduce_max(
                        out=sums[:, 2:3], in_=s16b[:], axis=AX.X), inc="s_dv")
                    em.wait("vector", "s_dv")
                    em.emit("vector", lambda v: v.tensor_scalar(
                        out=rtmp[:], in0=eL[:], scalar1=sums[:, 2:3],
                        scalar2=0.0, op0=ALU.subtract, op1=ALU.is_ge),
                        inc="s_dv")
                    em.wait("vector", "s_dv")
                    em.emit("vector", lambda v: v.tensor_tensor(
                        out=s16a[:], in0=eL[:], in1=rtmp[:], op=ALU.mult))
                    em.emit("vector", lambda v: v.tensor_tensor(
                        out=sums[:, 3:4], in0=sums[:, 1:2], in1=sums[:, 2:3],
                        op=ALU.add), inc="s_dv")
                    em.wait("vector", "s_dv")
                    em.emit("vector", lambda v: v.reciprocal(
                        out=sums[:, 4:5], in_=sums[:, 3:4]), inc="s_dv")
                    em.wait("vector", "s_dv")
                    em.emit("vector", lambda v, i=i: v.tensor_scalar(
                        out=w_q[:, i, :], in0=s16a[:], scalar1=sums[:, 4:5],
                        scalar2=None, op0=ALU.mult), inc="s_dv")
                    if dbg and i == 0 and uid == "m0":
                        em.wait("gpsimd", "s_dv")
                        em.emit("gpsimd", lambda g: g.dma_start(
                            out=dbg_t["d_eL"], in_=eL[:]), inc="s_st", n=16)
                        em.emit("gpsimd", lambda g: g.dma_start(
                            out=dbg_t["d_sums"], in_=sums[:]), inc="s_st", n=16)
                        em.wait("vector", "s_st")
                    em.wait("tensor", "s_dv")
                    if i >= 2:
                        em.wait("tensor", "s_ev")
                    em.emit("tensor", lambda t, i=i: t.transpose(
                        ps_wt[:, i % 2, :], w_q[:, i, :], identf[:]),
                        inc="s_mm")
                    em.wait("scalar", "s_mm")
                    em.emit("scalar", lambda s, i=i: s.copy(
                        out=wTq[:, i, :], in_=ps_wt[:, i % 2, :]), inc="s_ev")
                em.barrier()

        def emit_ffn_quarter(ctx, q, uid, xT, y_acc, w_q, wTq):
            build = False
            nbuf = 2
            w1sb = [ctx.enter_context(nc.sbuf_tensor(f"w1s{j}_{uid}", [P, KC, D], BF16))
                    for j in range(nbuf)]
            w2sb = [ctx.enter_context(nc.sbuf_tensor(f"w2s{j}_{uid}", [P, KC, D], BF16))
                    for j in range(nbuf)]
            b1rb = [ctx.enter_context(nc.sbuf_tensor(f"b1rb{j}_{uid}", [1, D], BF16))
                    for j in range(2)]
            G1B = [ctx.enter_context(nc.sbuf_tensor(f"G1B{j}_{uid}", [P, D], BF16))
                   for j in range(2)]
            BE1B = [ctx.enter_context(nc.sbuf_tensor(f"BE1B{j}_{uid}", [P, D], BF16))
                    for j in range(2)]
            hn = [ctx.enter_context(nc.sbuf_tensor(f"hn{j}_{uid}", [P, D], F32))
                  for j in range(2)]
            hg = [ctx.enter_context(nc.sbuf_tensor(f"hg{j}_{uid}", [P, D], F32))
                  for j in range(2)]
            hgl = [ctx.enter_context(nc.sbuf_tensor(f"hgl{j}_{uid}", [P, D], BF16))
                   for j in range(2)]
            hw = [ctx.enter_context(nc.sbuf_tensor(f"hw{j}_{uid}", [P, D], BF16))
                  for j in range(2)]
            hT = [ctx.enter_context(nc.sbuf_tensor(f"hT{j}_{uid}", [P, KC, P], BF16))
                  for j in range(2)]
            ytmp = [ctx.enter_context(nc.sbuf_tensor(f"ytmp{j}_{uid}", [P, D], F32))
                    for j in range(2)]
            sums2 = ctx.enter_context(nc.sbuf_tensor(f"sums2_{uid}", [P, 16], F32))
            ps_h = [ctx.enter_context(nc.psum_tensor(f"psh{j}_{uid}", [P, D], F32))
                    for j in range(2)]
            ps_hT = ctx.enter_context(nc.psum_tensor(f"pshT_{uid}", [P, KC, P], BF16))
            ps_y = ctx.enter_context(nc.psum_tensor(f"psy_{uid}", [P, D], F32))
            em.emit("gpsimd", lambda g: g.memset(sums2[:], 0.0), inc="s_gp")
            em.wait("vector", "s_gp")
            em.wait("scalar", "s_gp")

            T = {}
            base_wfree = em.cnt["s_wfree"]

            def hdr(e):
                j = e % nbuf
                p = e % 2
                wf = base_wfree + max(0, e - 1)
                if e >= 2:
                    em.wait("sync", "s_mm", T[(e - 2, "last_mm")])
                    if T.get((e - 2, "last_stt")):
                        em.wait("sync", "s_dv", T[(e - 2, "last_stt")])
                em.emit("sync", lambda sy, e=e, o=b1rb[p][:]: sy.dma_start(
                    out=o, in_=b1bf[e][None, :]), inc="s_x", n=16)
                em.emit("sync", lambda sy, e=e, o=G1B[p][:]: sy.dma_start(
                    out=o, in_=bass.AP(g1bf_t, e * D, [[0, P], [1, D]])),
                    inc="s_x", n=16)
                em.emit("sync", lambda sy, e=e, o=BE1B[p][:]: sy.dma_start(
                    out=o, in_=bass.AP(be1bf_t, e * D, [[0, P], [1, D]])),
                    inc="s_x", n=16)
                rows_ld = em.cnt["s_x"]
                T[(e, "b1")] = 0
                T[(e, "gbx")] = rows_ld
                emit_weight_load(w1sb[j], cache_w1[e], f"w1_{e}", wf)
                w_ready = emit_weight_load(w2sb[j], cache_w2[e], f"w2_{e}", wf)
                T[(e, "gb")] = 0
                T[(e, "wld")] = w_ready
                T[(e, "rows")] = rows_ld

            def w1_mm(e, i):
                j, p, k = e % nbuf, e % 2, i % 2
                pk = (e, i - 2) if i >= 2 else (e - 1, NT - 2 + i)
                if pk[0] >= 0:
                    t1 = T.get((pk[0], pk[1], "hn"))
                    t2 = T.get((pk[0], pk[1], "sq"))
                    if t1:
                        em.wait("tensor", "s_dv", t1)
                    if t2:
                        em.wait("tensor", "s_ev", t2)
                em.wait("tensor", "s_wld", T[(e, "wld")])
                em.wait("tensor", "s_x", T[(e, "rows")])
                for th in range(2):
                    for kc in range(KC):
                        em.emit("tensor", lambda t, th=th, kc=kc, j=j, i=i, k=k:
                                t.matmul(ps_h[k][:, th * 512:(th + 1) * 512],
                                         xT[:, kc, i * P:(i + 1) * P],
                                         w1sb[j][:, kc, th * 512:(th + 1) * 512],
                                         start=(kc == 0), stop=False,
                                         skip_group_check=True))
                    em.emit("tensor", lambda t, th=th, p=p, k=k: t.matmul(
                        ps_h[k][:, th * 512:(th + 1) * 512], ones1b[:],
                        b1rb[p][:, th * 512:(th + 1) * 512], start=False,
                        stop=True, skip_group_check=True),
                        inc=("s_mm" if th == 1 else None))
                T[(e, i, "w1")] = em.cnt["s_mm"]

            def epiA(e, i):
                if ffn_stage < 21:
                    return
                j, p, k = e % nbuf, e % 2, i % 2
                so = 8 * (i % 2)
                C = lambda c: sums2[:, so + c: so + c + 1]
                em.wait("scalar", "s_mm", T[(e, i, "w1")])
                pk = (e, i - 2) if i >= 2 else (e - 1, NT - 2 + i)
                t2 = T.get((pk[0], pk[1], "v1r")) if pk[0] >= 0 else None
                if t2:
                    em.wait("scalar", "s_dv", t2)
                em.emit("scalar", lambda s, k=k, C=C: s.activation(
                    out=scr1[:], in_=ps_h[k][:], func=AF.Square,
                    accum_out=C(1)), inc="s_ev")
                T[(e, i, "sq")] = em.cnt["s_ev"]
                em.wait("vector", "s_mm", T[(e, i, "w1")])
                em.wait("vector", "s_ev", T[(e, i, "sq")])
                em.emit("vector", lambda v, k=k, C=C: v.tensor_reduce(
                    out=C(0), in_=ps_h[k][:], axis=AX.X, op=ALU.add),
                    inc="s_dv")
                T[(e, i, "rd")] = em.cnt["s_dv"]
                em.wait("vector", "s_dv")
                em.emit("vector", lambda v, C=C: v.tensor_tensor(
                    out=C(3), in0=C(0), in1=C(0), op=ALU.mult))
                em.emit("vector", lambda v, C=C: v.tensor_scalar(
                    out=C(2), in0=C(0), scalar1=1.0 / D, scalar2=None,
                    op0=ALU.mult), inc="s_dv")
                em.wait("vector", "s_dv")
                em.emit("vector", lambda v, C=C: v.scalar_tensor_tensor(
                    out=C(4), in0=C(1), scalar=float(D), in1=C(3),
                    op0=ALU.mult, op1=ALU.subtract), inc="s_dv")
                T[(e, i, "v1r")] = em.cnt["s_dv"]
                em.wait("scalar", "s_dv", T[(e, i, "v1r")])
                em.emit("scalar", lambda s, C=C: s.activation(
                    out=C(5), in_=C(4), func=AF.Sqrt, bias=epsln[:],
                    scale=1.0 / (D * D)), inc="s_ev")
                T[(e, i, "sig")] = em.cnt["s_ev"]
                em.wait("vector", "s_ev", T[(e, i, "sig")])
                em.emit("vector", lambda v, C=C: v.reciprocal(
                    out=C(6), in_=C(5)), inc="s_dv")
                em.wait("vector", "s_dv")
                pk2 = (e, i - 2) if i >= 2 else (e - 1, NT - 2 + i)
                t3 = T.get((pk2[0], pk2[1], "gel")) if pk2[0] >= 0 else None
                if t3:
                    em.wait("vector", "s_ev", t3)
                em.emit("vector", lambda v, k=k, C=C: v.tensor_scalar(
                    out=hn[k][:], in0=ps_h[k][:], scalar1=C(2), scalar2=C(6),
                    op0=ALU.subtract, op1=ALU.mult), inc="s_dv")
                T[(e, i, "hn")] = em.cnt["s_dv"]

            def epiB(e, i):
                if ffn_stage < 26:
                    T[(e, i, "hw")] = 0
                    T[(e, i, "stt")] = 0
                    T[(e, i, "add")] = 0
                    return
                j, p, k = e % nbuf, e % 2, i % 2
                em.wait("vector", "s_x", T[(e, "rows")])
                em.emit("vector", lambda v, k=k, p=p: v.scalar_tensor_tensor(
                    out=hg[k][:], in0=hn[k][:], scalar=1.0, in1=G1B[p][:],
                    op0=ALU.mult, op1=ALU.mult), inc="s_dv")
                T[(e, i, "stt")] = em.cnt["s_dv"]
                em.wait("gpsimd", "s_dv", T[(e, i, "stt")])
                em.emit("gpsimd", lambda g, k=k, p=p: g.tensor_tensor(
                    out=hg[k][:], in0=hg[k][:], in1=BE1B[p][:], op=ALU.add),
                    inc="s_gp")
                T[(e, i, "add")] = em.cnt["s_gp"]
                em.wait("scalar", "s_gp", T[(e, i, "add")])
                em.emit("scalar", lambda s, k=k: s.activation(
                    out=hgl[k][:], in_=hg[k][:], func=AF.Gelu), inc="s_ev")
                T[(e, i, "gel")] = em.cnt["s_ev"]
                em.wait("vector", "s_ev", T[(e, i, "gel")])
                em.emit("vector", lambda v, k=k, e=e, i=i: v.tensor_scalar(
                    out=hw[k][:], in0=hgl[k][:], scalar1=w_q[:, i, e:e + 1],
                    scalar2=None, op0=ALU.mult), inc="s_dv")
                T[(e, i, "hw")] = em.cnt["s_dv"]

            def t_w2(e, i):
                j, p, k = e % nbuf, e % 2, i % 2
                if ffn_stage < 27:
                    if i == NT - 1:
                        T[(e, "last_mm")] = em.cnt["s_mm"]
                        T[(e, "last_stt")] = T.get((e, i, "stt"), 0)
                        T[(e, "last_add")] = T.get((e, i, "add"), 0)
                        if e == n_exp - 1:
                            for i2 in range(NT):
                                em.emit("gpsimd", lambda g, i2=i2,
                                        o=out_t[q * NT + i2],
                                        src=y_acc[:, i2, :]:
                                        g.dma_start(out=o, in_=src),
                                        inc="s_st", n=16)
                        em.emit("tensor", lambda t: t.nop(), inc="s_wfree")
                    return
                em.wait("tensor", "s_dv", T.get((e, i, "hw"), 0))
                prev = T.get((e, i - 1, "hTe")) or T.get((e - 1, NT - 1, "hTe"))
                if prev:
                    em.wait("tensor", "s_ev", prev)
                for c in range(KC):
                    em.emit("tensor", lambda t, c=c, k=k: t.transpose(
                        ps_hT[:, c, :], hw[k][:, c * P:(c + 1) * P], identb[:]),
                        inc=("s_mm" if c == KC - 1 else None))
                T[(e, i, "tp")] = em.cnt["s_mm"]
                em.wait("scalar", "s_mm", T[(e, i, "tp")])
                em.emit("scalar", lambda s, k=k: s.copy(out=hT[k][:],
                                                        in_=ps_hT[:]),
                        inc="s_ev")
                T[(e, i, "hTe")] = em.cnt["s_ev"]
                if ffn_stage < 28:
                    if i == NT - 1:
                        T[(e, "last_mm")] = em.cnt["s_mm"]
                        T[(e, "last_stt")] = T.get((e, i, "stt"), 0)
                        T[(e, "last_add")] = T.get((e, i, "add"), 0)
                        if e == n_exp - 1:
                            for i2 in range(NT):
                                em.emit("gpsimd", lambda g, i2=i2,
                                        o=out_t[q * NT + i2],
                                        src=y_acc[:, i2, :]:
                                        g.dma_start(out=o, in_=src),
                                        inc="s_st", n=16)
                        em.emit("tensor", lambda t: t.nop(), inc="s_wfree")
                    return
                em.wait("tensor", "s_ev", T[(e, i, "hTe")])
                prev = T.get((e, i - 1, "yt")) or T.get((e - 1, NT - 1, "yt"))
                if prev:
                    em.wait("tensor", "s_ev", prev)
                for th in range(2):
                    first = (e != 0)
                    if e == 0:
                        em.emit("tensor", lambda t, th=th, i=i: t.matmul(
                            ps_y[:, th * 512:(th + 1) * 512], wTq[:, i, :],
                            b2sb[:, th * 512:(th + 1) * 512], start=True,
                            stop=False, skip_group_check=True))
                    for kc in range(KC):
                        em.emit("tensor", lambda t, th=th, kc=kc, j=j, k=k,
                                st=(first and kc == 0):
                                t.matmul(ps_y[:, th * 512:(th + 1) * 512],
                                         hT[k][:, kc, :],
                                         w2sb[j][:, kc, th * 512:(th + 1) * 512],
                                         start=st, stop=(kc == KC - 1),
                                         skip_group_check=True),
                                inc=("s_mm" if th == 1 and kc == KC - 1
                                     else None))
                T[(e, i, "w2")] = em.cnt["s_mm"]
                T[(e, "last_mm")] = em.cnt["s_mm"]
                if i == NT - 1:
                    T[(e, "last_stt")] = T[(e, i, "stt")]
                    T[(e, "last_add")] = T[(e, i, "add")]
                em.wait("scalar", "s_mm", T[(e, i, "w2")])
                prev = T.get((e, i - 2, "ya")) or (T.get((e - 1, NT - 2 + i, "ya"))
                                                   if i < 2 else None)
                if prev:
                    em.wait("scalar", "s_gp", prev)
                em.emit("scalar", lambda s, k=k: s.copy(out=ytmp[k][:],
                                                        in_=ps_y[:]),
                        inc="s_ev")
                T[(e, i, "yt")] = em.cnt["s_ev"]
                em.wait("gpsimd", "s_ev", T[(e, i, "yt")])
                prev = T.get((e - 1, i, "ya"))
                if prev:
                    em.wait("gpsimd", "s_gp", prev)
                em.emit("gpsimd", lambda g, i=i, k=k: g.tensor_tensor(
                    out=y_acc[:, i, :], in0=y_acc[:, i, :], in1=ytmp[k][:],
                    op=ALU.add), inc="s_gp")
                T[(e, i, "ya")] = em.cnt["s_gp"]
                if e == n_exp - 1:
                    em.emit("gpsimd", lambda g, i=i,
                            o=out_t[q * NT + i], src=y_acc[:, i, :]:
                            g.dma_start(out=o, in_=src), inc="s_st", n=16)
                if i == NT - 1:
                    em.emit("tensor", lambda t: t.nop(), inc="s_wfree")

            if n_exp == 0:
                for i in range(NT):
                    em.emit("gpsimd", lambda g, i=i, o=out_t[q * NT + i],
                            src=y_acc[:, i, :]:
                            g.dma_start(out=o, in_=src), inc="s_st", n=16)
                return
            hdr(0)
            for e in range(n_exp):
                if e + 1 < n_exp:
                    hdr(e + 1)
                w1_mm(e, 0)
                epiA(e, 0)
                for i in range(1, NT):
                    w1_mm(e, i)
                    epiB(e, i - 1)
                    epiA(e, i)
                    t_w2(e, i - 1)
                epiB(e, NT - 1)
                t_w2(e, NT - 1)

        build_list = [("proj_k", wT["k"], cache_proj[1]),
                      ("proj_v", wT["v"], cache_proj[2]),
                      ("proj_q", wT["q"], cache_proj[0]),
                      ("proj_g", wT["g"], cache_proj[3]),
                      ("proj_o", wT["o"], cache_proj[4])]
        for e_ in range(NE):
            build_list.append((f"w1_{e_}", w1[e_], cache_w1[e_]))
            build_list.append((f"w2_{e_}", w2[e_], cache_w2[e_]))
        for name, s_, c_ in build_list[:3]:
            emit_cache_build(name, s_.rearrange("(c p) d -> p c d", p=P), c_)
        bpos = 3
        nq_total = max(1, n_pref_q)
        per_q = (len(build_list) - 3 + nq_total - 1) // nq_total

        for pq in range(n_pref_q):
            with ExitStack() as ctx:
                xT = ctx.enter_context(nc.sbuf_tensor(f"xTp_{pq}", [P, KC, QT], BF16))
                k_t = ctx.enter_context(nc.sbuf_tensor(f"ktp_{pq}", [P, NT, D], BF16))
                v_t = ctx.enter_context(nc.sbuf_tensor(f"vtp_{pq}", [P, NT, D], BF16))
                wp = ctx.enter_context(nc.sbuf_tensor(f"wpp_{pq}",
                                                       [P, KC, D], BF16))
                emit_ln_quarter(ctx, xp[pq * QT:(pq + 1) * QT], f"p{pq}", xT)
                emit_proj_stream("k", xT, k_t, f"p{pq}", True, True, wp)
                emit_proj_stream("v", xT, v_t, f"p{pq}", True, False, wp)
                if pq == n_pref_q - 1:
                    emit_proj_stream("q", xT, None, f"p{pq}", True, False, wp)
                with nc.psum_tensor(f"Atmp_p{pq}", [DK, H, DK], F32) as A_tmp:
                    for i in range(NT):
                        emit_A_update(A_tmp, k_t, v_t, i)
                    em.barrier()
            for name, s_, c_ in build_list[bpos:bpos + per_q]:
                emit_cache_build(name, s_.rearrange("(c p) d -> p c d", p=P), c_)
            bpos += per_q
        for name, s_, c_ in build_list[bpos:]:
            emit_cache_build(name, s_.rearrange("(c p) d -> p c d", p=P), c_)

        for q in range(n_main_q):
            with ExitStack() as ctx:
                uid = f"m{q}"
                xT = ctx.enter_context(nc.sbuf_tensor(f"xTm_{q}", [P, KC, QT], BF16))
                w_q = ctx.enter_context(nc.sbuf_tensor(f"wq_{q}", [P, NT, NE], F32))
                wTq = ctx.enter_context(nc.sbuf_tensor(f"wTq_{q}", [NE, NT, P], BF16))
                emit_ln_quarter(ctx, xm[q * QT:(q + 1) * QT], uid, xT)
                em.barrier()
                if q == 0:
                    dump("d_xT", xT[:])
                emit_router_quarter(xT, w_q, wTq, uid)
                if q == 0:
                    dump("d_wq", w_q[:])
                y_acc = ctx.enter_context(nc.sbuf_tensor(
                    f"ya_{q}", [P, NT, D], F32))
                with ExitStack() as pctx:
                    q_t = pctx.enter_context(nc.sbuf_tensor(f"qt_{q}", [P, NT, D], BF16))
                    k_t = pctx.enter_context(nc.sbuf_tensor(f"kt_{q}", [P, NT, D], BF16))
                    v_t = pctx.enter_context(nc.sbuf_tensor(f"vt_{q}", [P, NT, D], BF16))
                    gate_t = pctx.enter_context(nc.sbuf_tensor(f"gt_{q}", [P, NT, D], BF16))
                    with ExitStack() as wctx:
                        wp = wctx.enter_context(nc.sbuf_tensor(
                            f"wpm_{q}", [P, KC, D], BF16))
                        emit_proj_stream("q", xT, q_t, uid, True, True, wp)
                        emit_proj_stream("k", xT, k_t, uid, True, True, wp)
                        emit_proj_stream("v", xT, v_t, uid, True, False, wp)
                        emit_proj_stream("g", xT, gate_t, uid, False, False,
                                         wp)
                    if q == 0:
                        dump("d_qt", q_t[:])
                        dump("d_kt", k_t[:])
                        dump("d_vt", v_t[:])
                        dump("d_gt", gate_t[:])
                    with ExitStack() as sctx:
                        wo_sb = sctx.enter_context(nc.sbuf_tensor(
                            f"wo_{q}", [P, KC, D], BF16))
                        emit_weight_load(wo_sb, cache_proj[PROJ_IDX["o"]],
                                         "proj_o", state["nstream"])
                        state["nstream"] += 1
                        emit_scan_quarter(sctx, q, uid, q_t, k_t, v_t, gate_t,
                                          y_acc, wo_sb,
                                          xm[q * QT:(q + 1) * QT])
                        em.emit("tensor", lambda t: t.nop(), inc="s_wfree")
                        em.barrier()
                    if q == 0:
                        dump("d_ya", y_acc[:])
                with ExitStack() as fctx:
                    emit_ffn_quarter(fctx, q, uid, xT, y_acc, w_q, wTq)
                    em.barrier()
                    em.wait("gpsimd", "s_st")

        if dbg:
            em.barrier()
            for nm, sb in (("d_qkf", snap_qkf), ("d_qkm", snap_qkm),
                           ("d_stb", snap_stb), ("d_osb", snap_osb)):
                em.emit("gpsimd", lambda g, o=dbg_t[nm], s=sb[:]:
                        g.dma_start(out=o, in_=s), inc="s_st", n=16)
        em.wait("gpsimd", "s_st")
        em.wait("gpsimd", "s_cache")

        block = E(nc.Block())
        for eng, meth in (("sync", block.sync), ("tensor", block.tensor),
                          ("vector", block.vector), ("scalar", block.scalar),
                          ("gpsimd", block.gpsimd)):
            def body(E_, _q=em.q[eng]):
                for fn in _q:
                    fn(E_)
            meth(body)

    return nc


def make_inmaps(inputs):
    x = np.asarray(inputs["L0_attn_hidden_states"], np.float32)
    zeros = np.zeros((TPC, D), np.float32)

    def T(a):
        return np.ascontiguousarray(np.asarray(a, np.float32).T)

    def cw_layout(a):
        return np.ascontiguousarray(
            np.asarray(a, np.float32).reshape(KC, P, 4).transpose(1, 0, 2)
            .reshape(P, KC * 4))

    shared = {
        "wqT": T(inputs["L0_attn_Wq"]),
        "wkT": T(inputs["L0_attn_Wk"]),
        "wvT": T(inputs["L0_attn_Wv"]),
        "wgT": T(inputs["L0_attn_Wg"]),
        "woT": T(inputs["L0_attn_Wo"]),
        "cwq": cw_layout(inputs["L0_attn_conv_q"]),
        "cwk": cw_layout(inputs["L0_attn_conv_k"]),
        "cwv": cw_layout(inputs["L0_attn_conv_v"]),
        "winit": np.ascontiguousarray(
            np.asarray(inputs["L0_attn_W_init"], np.float32)[0].transpose(1, 0, 2)),
        "gamma_t": np.ascontiguousarray(
            np.tile(np.asarray(inputs["L0_attn_ln_gamma"], np.float32), H)),
        "beta_t": np.ascontiguousarray(
            np.tile(np.asarray(inputs["L0_attn_ln_beta"], np.float32), H)),
        "wr": np.ascontiguousarray(np.asarray(inputs["L0_ffn_Wr"], np.float32)),
        "br": np.ascontiguousarray(np.asarray(inputs["L0_ffn_br"], np.float32)),
        "w1": np.ascontiguousarray(np.asarray(inputs["L0_ffn_W1"], np.float32)),
        "b1": np.ascontiguousarray(np.asarray(inputs["L0_ffn_b1"], np.float32)),
        "g1": np.ascontiguousarray(np.asarray(inputs["L0_ffn_g1"], np.float32)),
        "be1": np.ascontiguousarray(np.asarray(inputs["L0_ffn_be1"], np.float32)),
        "w2": np.ascontiguousarray(np.asarray(inputs["L0_ffn_W2"], np.float32)),
        "b2": np.ascontiguousarray(np.asarray(inputs["L0_ffn_b2"], np.float32)),
    }
    in_maps = []
    for c in range(8):
        b, hf = c // 2, c % 2
        m = dict(shared)
        m["xm"] = np.ascontiguousarray(x[b, hf * TPC:(hf + 1) * TPC])
        m["xp"] = np.ascontiguousarray(x[b, 0:TPC]) if hf == 1 else zeros
        in_maps.append(m)
    return in_maps


def kernel(**inputs):
    in_maps = make_inmaps(inputs)
    nc = build_nc()
    res = run_bass_kernel_spmd(nc, in_maps, core_ids=list(range(8)))
    outp = np.empty((B, L, D), np.float32)
    for c in range(8):
        b, hf = c // 2, c % 2
        outp[b, hf * TPC:(hf + 1) * TPC] = res.results[c]["out"]
    return outp


# version 12
# speedup vs baseline: 1.1000x; 1.1000x over previous
import numpy as np
import concourse.bass as bass
import concourse.mybir as mybir
from concourse.bass_utils import run_bass_kernel_spmd
from contextlib import ExitStack

F32 = mybir.dt.float32
BF16 = mybir.dt.bfloat16
AF = mybir.ActivationFunctionType
ALU = mybir.AluOpType
AX = mybir.AxisListType

B, L, D = 4, 8192, 1024
H, DK, NE = 16, 64, 16
P = 128
KC = D // P
TPC = L // 2
NT = 4
QT = NT * P
TH = max(1, QT // 512)
EPS_LN = 1e-5
EPS_L2 = 1e-12

ENGINES = ("sync", "tensor", "vector", "scalar", "gpsimd")
SEMS = ("s_x", "s_stg", "s_wld", "s_wfree", "s_ln", "s_lnv",
        "s_mm", "s_ev", "s_dv", "s_gp", "s_st", "s_cache", "s_gx", "s_cv")

N_PREF_Q = TPC // QT
N_MAIN_Q = TPC // QT
N_EXP = NE


class EM:
    def __init__(self):
        self.q = {e: [] for e in ENGINES}
        self.cnt = {}
        self.sems = {}

    def sem(self, name, handle):
        self.sems[name] = handle
        self.cnt[name] = 0

    def inc(self, name, n=1):
        self.cnt[name] += n
        return self.cnt[name]

    def emit(self, eng, fn, inc=None, n=1):
        if inc is None:
            self.q[eng].append(fn)
        else:
            s = self.sems[inc]
            self.inc(inc, n)
            self.q[eng].append(lambda E, fn=fn, s=s, n=n: fn(E).then_inc(s, n))

    def wait(self, eng, name, val=None):
        s = self.sems[name]
        v = self.cnt[name] if val is None else val
        if v > 0:
            self.q[eng].append(lambda E, s=s, v=v: E.wait_ge(s, v))

    def barrier(self, engines=ENGINES, sems=("s_mm", "s_ev", "s_dv", "s_lnv",
                                             "s_ln", "s_gp")):
        for e in engines:
            for s in sems:
                self.wait(e, s)


def build_nc(n_pref_q=None, n_main_q=None, n_exp=None, dbg=False, ffn_stage=99):
    n_pref_q = N_PREF_Q if n_pref_q is None else n_pref_q
    n_main_q = N_MAIN_Q if n_main_q is None else n_main_q
    n_exp = N_EXP if n_exp is None else n_exp

    nc = bass.Bass("TRN2", target_bir_lowering=False, debug=False)
    em = EM()

    xm = nc.dram_tensor("xm", [TPC, D], F32, kind="ExternalInput").ap()
    xp = nc.dram_tensor("xp", [TPC, D], F32, kind="ExternalInput").ap()
    wT = {s: nc.dram_tensor(f"w{s}T", [D, D], F32, kind="ExternalInput").ap()
          for s in ("q", "k", "v", "g", "o")}
    cw_d = {s: nc.dram_tensor(f"cw{s}", [P, KC * 4], F32, kind="ExternalInput").ap()
            for s in ("q", "k", "v")}
    winit = nc.dram_tensor("winit", [DK, H, DK], F32, kind="ExternalInput").ap()
    gamma_t = nc.dram_tensor("gamma_t", [D], F32, kind="ExternalInput").ap()
    beta_t = nc.dram_tensor("beta_t", [D], F32, kind="ExternalInput").ap()
    wr = nc.dram_tensor("wr", [D, NE], F32, kind="ExternalInput").ap()
    br = nc.dram_tensor("br", [NE], F32, kind="ExternalInput").ap()
    w1 = nc.dram_tensor("w1", [NE, D, D], F32, kind="ExternalInput").ap()
    b1 = nc.dram_tensor("b1", [NE, D], F32, kind="ExternalInput").ap()
    g1 = nc.dram_tensor("g1", [NE, D], F32, kind="ExternalInput").ap()
    be1 = nc.dram_tensor("be1", [NE, D], F32, kind="ExternalInput").ap()
    w2 = nc.dram_tensor("w2", [NE, D, D], F32, kind="ExternalInput").ap()
    b2 = nc.dram_tensor("b2", [NE, D], F32, kind="ExternalInput").ap()
    out = nc.dram_tensor("out", [TPC, D], F32, kind="ExternalOutput").ap()
    out_t = out.rearrange("(i p) d -> i p d", p=P)

    dbg_t = {}
    if dbg:
        for nm, shape, dt in (("d_xT", [P, KC, QT], BF16),
                              ("d_qt", [P, NT, D], BF16),
                              ("d_kt", [P, NT, D], BF16),
                              ("d_vt", [P, NT, D], BF16),
                              ("d_gt", [P, NT, D], BF16),
                              ("d_wq", [P, NT, NE], F32),
                              ("d_osb", [P, D], F32),
                              ("d_ya", [P, NT, D], F32),
                              ("d_eL", [P, NE], F32),
                              ("d_sums", [P, 8], F32),
                              ("d_pb", [P, QT + 3], F32),
                              ("d_cb", [P, QT], F32),
                              ("d_qkf", [DK, KC, P], BF16),
                              ("d_qkm", [P, 4, P], BF16),
                              ("d_stb", [DK, H, DK], BF16)):
            dbg_t[nm] = nc.dram_tensor(nm, shape, dt, kind="ExternalOutput").ap()

    def dump(nm, src_ap):
        if dbg:
            em.barrier()
            em.emit("gpsimd", lambda g, o=dbg_t[nm], s=src_ap:
                    g.dma_start(out=o, in_=s), inc="s_st", n=16)

    snap_qkf = snap_qkm = snap_stb = snap_osb = None

    g1bf_t = nc.dram_tensor("g1bf", [NE, D], BF16, kind="Internal")
    be1bf_t = nc.dram_tensor("be1bf", [NE, D], BF16, kind="Internal")
    b1bf_t = nc.dram_tensor("b1bf", [NE, D], BF16, kind="Internal")
    g1bf, be1bf, b1bf = g1bf_t.ap(), be1bf_t.ap(), b1bf_t.ap()
    cache_proj = nc.dram_tensor("cache_proj", [5, P, KC, D], BF16, kind="Internal").ap()
    cache_w1 = nc.dram_tensor("cache_w1", [NE, P, KC, D], BF16, kind="Internal").ap()
    cache_w2 = nc.dram_tensor("cache_w2", [NE, P, KC, D], BF16, kind="Internal").ap()

    PROJ_IDX = {"q": 0, "k": 1, "v": 2, "g": 3, "o": 4}

    with ExitStack() as top:
        E = top.enter_context
        for s in SEMS:
            em.sem(s, E(nc.semaphore(s)))
        if dbg:
            snap_qkf = E(nc.sbuf_tensor("snap_qkf", [DK, KC, P], BF16))
            snap_qkm = E(nc.sbuf_tensor("snap_qkm", [P, 4, P], BF16))
            snap_stb = E(nc.sbuf_tensor("snap_stb", [DK, H, DK], BF16))
            snap_osb = E(nc.sbuf_tensor("snap_osb", [P, D], F32))

        identb = E(nc.sbuf_tensor("identb", [P, P], BF16, side="right"))
        identf = E(nc.sbuf_tensor("identf", [P, P], F32, side="right"))
        maskT = E(nc.sbuf_tensor("maskT", [P, P], F32, side="right"))
        ones1f = E(nc.sbuf_tensor("ones1f", [1, P], F32, side="right"))
        ones1b = E(nc.sbuf_tensor("ones1b", [1, P], BF16, side="right"))
        onecol = E(nc.sbuf_tensor("onecol", [P, 1], F32, side="right"))
        epsln = E(nc.sbuf_tensor("epsln", [P, 1], F32, side="right"))
        epsl2 = E(nc.sbuf_tensor("epsl2", [P, 1], F32, side="right"))
        gamB = E(nc.sbuf_tensor("gamB", [P, D], F32, side="right"))
        betB = E(nc.sbuf_tensor("betB", [P, D], F32, side="right"))
        winitB = E(nc.sbuf_tensor("winitB", [DK, H, DK], F32, side="right"))
        cw_sb = {s: E(nc.sbuf_tensor(f"cw_{s}", [P, KC * 4], F32, side="right"))
                 for s in ("q", "k", "v")}
        wr_sb = E(nc.sbuf_tensor("wr_sb", [P, KC, NE], BF16, side="right"))
        brb = E(nc.sbuf_tensor("brb", [1, NE], BF16, side="right"))
        b2sb = E(nc.sbuf_tensor("b2sb", [NE, D], BF16, side="right"))
        setup_ctx = ExitStack()
        b2f = setup_ctx.enter_context(nc.sbuf_tensor("b2f", [NE, D], F32))
        rowf = setup_ctx.enter_context(nc.sbuf_tensor("rowf", [NE, D], F32))
        rowb = setup_ctx.enter_context(nc.sbuf_tensor("rowb", [NE, D], BF16))
        grow = setup_ctx.enter_context(nc.sbuf_tensor("grow", [1, D], F32))
        grow2 = setup_ctx.enter_context(nc.sbuf_tensor("grow2", [1, D], F32))
        sums = E(nc.sbuf_tensor("sums", [P, 8], F32, side="right"))
        s16a = E(nc.sbuf_tensor("s16a", [P, NE], F32, side="right"))
        s16b = E(nc.sbuf_tensor("s16b", [P, NE], F32, side="right"))
        s16c = E(nc.sbuf_tensor("s16c", [P, NE], F32, side="right"))
        eL = E(nc.sbuf_tensor("eL", [P, NE], F32, side="right"))
        rtmp = E(nc.sbuf_tensor("rtmp", [P, NE], F32, side="right"))
        scr1 = E(nc.sbuf_tensor("scr1", [P, D], F32, side="right"))
        halo = {s: E(nc.sbuf_tensor(f"halo_{s}", [P, KC * 3], F32, side="right"))
                for s in ("q", "k", "v")}
        bstage = [E(nc.sbuf_tensor(f"bstg{j}", [P, D], F32, side="right"))
                  for j in range(4)]
        bconv = [E(nc.sbuf_tensor(f"bcv{j}", [P, D], BF16, side="right"))
                 for j in range(4)]
        A_sb = [E(nc.sbuf_tensor(f"A_sb{j}", [DK, H, DK], F32, side="right"))
                for j in range(2)]

        def gp_setup(g):
            g.memset(identb[:], 0.0)
            g.memset(identf[:], 0.0)
            g.memset(maskT[:], 1.0)
            g.memset(ones1f[:], 1.0)
            g.memset(ones1b[:], 1.0)
            g.memset(onecol[:], 1.0)
            g.memset(epsln[:], EPS_LN)
            g.memset(epsl2[:], EPS_L2)
            for s in ("q", "k", "v"):
                g.memset(halo[s][:], 0.0)
            g.memset(sums[:], 0.0)
            g.memset(s16a[:], 0.0)
            g.memset(s16b[:], 0.0)
            g.memset(s16c[:], 0.0)
            g.memset(eL[:], 0.0)
            g.memset(rtmp[:], 0.0)
            return g.memset(scr1[:], 0.0)
        em.emit("gpsimd", gp_setup, inc="s_gp")
        em.wait("gpsimd", "s_gp")

        def gp_setup2(g):
            g.affine_select(out=identb[:], in_=identb[:],
                            compare_op=ALU.not_equal, fill=1.0, base=0,
                            pattern=[[-1, P]], channel_multiplier=1)
            g.affine_select(out=identf[:], in_=identf[:],
                            compare_op=ALU.not_equal, fill=1.0, base=0,
                            pattern=[[-1, P]], channel_multiplier=1)
            return g.affine_select(out=maskT[:], in_=maskT[:],
                                   compare_op=ALU.is_ge, fill=0.0, base=0,
                                   pattern=[[1, P]], channel_multiplier=-1)
        em.emit("gpsimd", gp_setup2, inc="s_gp")

        setup_loads = [
            (winitB[:], winit),
            (cw_sb["q"][:], cw_d["q"]),
            (cw_sb["k"][:], cw_d["k"]),
            (cw_sb["v"][:], cw_d["v"]),
            (grow[:], gamma_t[None, :]),
            (grow2[:], beta_t[None, :]),
            (b2f[:], b2),
            (scr1[0:1, 0:NE], br[None, :]),
            (scr1[:, 128:128 + KC * NE].rearrange("p (c e) -> p c e", c=KC),
             wr.rearrange("(c p) e -> p c e", p=P)),
        ]
        for o_, i_ in setup_loads:
            em.emit("sync", lambda sy, o=o_, i=i_: sy.dma_start(out=o, in_=i),
                    inc="s_x", n=16)

        em.wait("vector", "s_x")
        def v_setup0(v):
            return v.tensor_copy(out=A_sb[0][:], in_=winitB[:])
        em.emit("vector", v_setup0, inc="s_dv")
        for srcd, dstd in ((g1, g1bf), (be1, be1bf), (b1, b1bf)):
            em.emit("sync", lambda sy, s=srcd: sy.dma_start(out=rowf[:], in_=s),
                    inc="s_x", n=16)
            em.wait("vector", "s_x")
            em.emit("vector", lambda v: v.tensor_copy(out=rowb[:], in_=rowf[:]),
                    inc="s_dv")
            em.wait("gpsimd", "s_dv")
            em.emit("gpsimd", lambda g, d=dstd: g.dma_start(out=d, in_=rowb[:]),
                    inc="s_cache", n=16)
            em.wait("sync", "s_cache")
        def v_setup1(v):
            for si_, st in enumerate(("q", "k", "v")):
                for mc in range(KC):
                    sl = slice(mc * 4 + 3, mc * 4 + 4)
                    v.tensor_scalar(out=scr1[:, 32 + si_ * KC + mc: 33 + si_ * KC + mc],
                                    in0=cw_sb[st][:, sl],
                                    scalar1=1.0, scalar2=None, op0=ALU.add)
            v.tensor_copy(out=b2sb[:], in_=b2f[:])
            v.tensor_copy(out=brb[:], in_=scr1[0:1, 0:NE])
            return v.tensor_copy(
                out=wr_sb[:], in_=scr1[:, 128:128 + KC * NE].rearrange(
                    "p (c e) -> p c e", c=KC))
        em.emit("vector", v_setup1, inc="s_dv")
        em.wait("vector", "s_dv")
        def v_setup2(v):
            for si_, st in enumerate(("q", "k", "v")):
                for mc in range(KC):
                    sl = slice(mc * 4 + 3, mc * 4 + 4)
                    v.tensor_copy(out=cw_sb[st][:, sl],
                                  in_=scr1[:, 32 + si_ * KC + mc: 33 + si_ * KC + mc])
            return v.tensor_copy(out=scr1[:, 32:33], in_=scr1[:, 33:34])
        em.emit("vector", v_setup2, inc="s_dv")

        with nc.psum_tensor("ps_gb", [P, 512], F32) as ps_gb, \
             nc.psum_tensor("ps_gb2", [P, 512], F32) as ps_gb2:
            em.wait("tensor", "s_x")
            em.wait("tensor", "s_gp")
            for hf in range(2):
                sl = slice(hf * 512, (hf + 1) * 512)
                em.emit("tensor", lambda t, sl=sl: t.matmul(
                    ps_gb[:], ones1f[:], grow[:, sl], start=True, stop=True))
                em.emit("tensor", lambda t, sl=sl: t.matmul(
                    ps_gb2[:], ones1f[:], grow2[:, sl], start=True, stop=True),
                    inc="s_mm")
                em.wait("scalar", "s_mm")
                em.emit("scalar", lambda s, sl=sl: s.copy(
                    out=gamB[:, sl], in_=ps_gb[:]))
                em.emit("scalar", lambda s, sl=sl: s.copy(
                    out=betB[:, sl], in_=ps_gb2[:]), inc="s_ev")
                em.wait("tensor", "s_ev")

        em.barrier()
        setup_ctx.close()

        state = {"proj_cached": set(), "exp_cached": False,
                 "wfree_t": {}, "A_par": 0, "A_upd": 0,
                 "cache_done": {}, "bpar": 0}

        def emit_cache_build(name, src_fp32, cache_ap):
            for kc in range(KC):
                j = state["bpar"] & 1
                state["bpar"] += 1
                em.wait("gpsimd", "s_cv")
                em.emit("gpsimd", lambda g, o=bstage[j][:], s=src_fp32[:, kc, :]:
                        g.dma_start(out=o, in_=s), inc="s_stg", n=16)
                em.wait("vector", "s_stg")
                em.wait("vector", "s_cache")
                em.emit("vector", lambda v, i=bstage[j][:], o=bconv[j][:]:
                        v.tensor_copy(out=o, in_=i), inc="s_cv")
                em.wait("gpsimd", "s_cv")
                em.emit("gpsimd", lambda g, i=bconv[j][:], o=cache_ap[:, kc, :]:
                        g.dma_start(out=o, in_=i), inc="s_cache", n=16)
            state["cache_done"][name] = em.cnt["s_cache"]


        def emit_ln_quarter(ctx_unused, src_ap, uid, xT):
            lctx = ExitStack()
            xld = [lctx.enter_context(nc.sbuf_tensor(f"xld{j}_{uid}", [P, D], F32))
                   for j in range(2)]
            xnt = [lctx.enter_context(nc.sbuf_tensor(f"xnt{j}_{uid}", [P, D], BF16))
                   for j in range(2)]
            ps_tp = [lctx.enter_context(nc.psum_tensor(f"lntp{j}_{uid}",
                                                      [P, KC, P], BF16))
                     for j in range(2)]
            src_t = src_ap.rearrange("(i p) d -> i p d", p=P)
            for i in range(NT):
                j = i % 2
                if i >= 2:
                    em.wait("sync", "s_lnv")
                    em.wait("sync", "s_ev")
                em.emit("sync", lambda sy, o=xld[j][:], s=src_t[i]:
                        sy.dma_start(out=o, in_=s), inc="s_x", n=16)
                em.wait("vector", "s_x")
                em.wait("vector", "s_ev")
                em.emit("vector", lambda v, x=xld[j][:]: v.tensor_reduce(
                    out=sums[:, 0:1], in_=x, axis=AX.X, op=ALU.add), inc="s_lnv")
                em.wait("scalar", "s_lnv")
                em.wait("scalar", "s_dv")
                em.emit("scalar", lambda s, x=xld[j][:]: s.activation(
                    out=scr1[:], in_=x, func=AF.Square, accum_out=sums[:, 1:2]),
                    inc="s_ev")
                em.wait("vector", "s_ev")
                em.emit("vector", lambda v: v.tensor_scalar(
                    out=sums[:, 2:3], in0=sums[:, 0:1], scalar1=1.0 / D,
                    scalar2=None, op0=ALU.mult), inc="s_dv")
                em.wait("vector", "s_dv")
                em.emit("vector", lambda v: v.tensor_tensor(
                    out=sums[:, 3:4], in0=sums[:, 2:3], in1=sums[:, 2:3],
                    op=ALU.mult), inc="s_dv")
                em.wait("vector", "s_dv")
                em.emit("vector", lambda v: v.scalar_tensor_tensor(
                    out=sums[:, 4:5], in0=sums[:, 1:2], scalar=1.0 / D,
                    in1=sums[:, 3:4], op0=ALU.mult, op1=ALU.subtract),
                    inc="s_dv")
                em.wait("scalar", "s_dv")
                em.emit("scalar", lambda s: s.activation(
                    out=sums[:, 5:6], in_=sums[:, 4:5], func=AF.Sqrt,
                    bias=epsln[:], scale=1.0), inc="s_ev")
                em.wait("vector", "s_ev")
                em.emit("vector", lambda v: v.reciprocal(
                    out=sums[:, 6:7], in_=sums[:, 5:6]), inc="s_dv")
                em.wait("vector", "s_dv")
                em.emit("vector", lambda v, x=xld[j][:], o=xnt[j][:]:
                        v.tensor_scalar(out=o, in0=x, scalar1=sums[:, 2:3],
                                        scalar2=sums[:, 6:7], op0=ALU.subtract,
                                        op1=ALU.mult), inc="s_lnv")
                em.wait("tensor", "s_lnv")
                if i >= 2:
                    em.wait("tensor", "s_ln")
                for c in range(KC):
                    em.emit("tensor", lambda t, c=c, ps=ps_tp[j],
                            src=xnt[j][:, c * P:(c + 1) * P]:
                            t.transpose(ps[:, c, :], src, identb[:]),
                            inc=("s_mm" if c == KC - 1 else None))
                em.wait("scalar", "s_mm")
                em.emit("scalar", lambda s, ps=ps_tp[j],
                        o=xT[:, :, i * P:(i + 1) * P]:
                        s.copy(out=o, in_=ps[:]), inc="s_ln")
            em.barrier()
            lctx.close()

        def emit_weight_load(dst_sb, cache_ap, name, wait_free_val):
            em.wait("sync", "s_wfree", wait_free_val)
            em.wait("sync", "s_cache", state["cache_done"][name])
            em.emit("sync", lambda sy, o=dst_sb[:], s=cache_ap:
                    sy.dma_start(out=o, in_=s), inc="s_wld", n=16)
            return em.cnt["s_wld"]

        def emit_proj_stream(sname, xT, dest, uid, do_conv, do_norm, wp,
                             last=False):
            si = PROJ_IDX[sname]
            with ExitStack() as ctx:
                psP = [ctx.enter_context(nc.psum_tensor(
                    f"psP{j}_{uid}_{sname}", [P, min(512, QT)], F32)) for j in range(2)]
                ps_tp = [ctx.enter_context(nc.psum_tensor(
                    f"ptp{j}_{uid}_{sname}", [P, KC, P], BF16)) for j in range(2)]
                pbuf = [ctx.enter_context(nc.sbuf_tensor(
                    f"pb{j}_{uid}_{sname}", [P, QT + 3], F32)) for j in range(2)]
                cA = ctx.enter_context(nc.sbuf_tensor(f"cA_{uid}_{sname}", [P, QT], F32))
                cB = ctx.enter_context(nc.sbuf_tensor(f"cB_{uid}_{sname}", [P, QT], F32))
                csb = ctx.enter_context(nc.sbuf_tensor(f"csb_{uid}_{sname}",
                                                       [P, KC, QT], BF16))
                knr = ctx.enter_context(nc.sbuf_tensor(
                    f"knr_{uid}_{sname}", [P, D], BF16)) if do_norm else None

                w_ready = emit_weight_load(
                    wp, cache_proj[si], f"proj_{sname}",
                    state["wfree_t"].get(id(wp), em.cnt["s_wfree"]))

                em.wait("tensor", "s_wld", w_ready)
                em.wait("tensor", "s_ev")
                evac_t = {}
                for u in range(TH * KC):
                    mc, th = u // TH, u % TH
                    j = u % 2
                    if u >= 2:
                        em.wait("tensor", "s_ev", evac_t[u - 2])
                    for kc in range(KC):
                        em.emit("tensor", lambda t, ps=psP[j],
                                l=wp[:, kc, mc * P:(mc + 1) * P],
                                r=xT[:, kc, th * 512:min(QT, (th + 1) * 512)],
                                st=(kc == 0), sp=(kc == KC - 1):
                                t.matmul(ps[:], l, r, start=st, stop=sp,
                                         skip_group_check=True),
                                inc=("s_mm" if kc == KC - 1 else None))
                    em.wait("scalar", "s_mm")
                    if do_conv:
                        em.wait("scalar", "s_dv")
                        em.emit("scalar", lambda s, ps=psP[j],
                                o=pbuf[mc % 2][:, 3 + th * 512: 3 + min(QT, (th + 1) * 512)]:
                                s.copy(out=o, in_=ps[:]), inc="s_ev")
                    else:
                        em.emit("scalar", lambda s, ps=psP[j],
                                o=csb[:, mc, th * 512:min(QT, (th + 1) * 512)]:
                                s.copy(out=o, in_=ps[:]), inc="s_ev")
                    evac_t[u] = em.cnt["s_ev"]
                    if th == TH - 1 and do_conv:
                        jj = mc % 2
                        cw = cw_sb[sname]
                        em.wait("vector", "s_ev")
                        em.emit("vector", lambda v, o=pbuf[jj][:, 0:3],
                                i=halo[sname][:, mc * 3:(mc + 1) * 3]:
                                v.tensor_copy(out=o, in_=i))
                        em.emit("vector", lambda v, p=pbuf[jj][:, 3:3 + QT],
                                w3=cw[:, mc * 4 + 3: mc * 4 + 4]:
                                v.tensor_scalar(out=cA[:], in0=p, scalar1=w3,
                                                scalar2=None, op0=ALU.mult))
                        chain = [(cA, cB), (cB, cA), (cA, cB)]
                        for d in (1, 2, 3):
                            ai, ao = chain[d - 1]
                            em.emit("vector", lambda v, d=d,
                                    p=pbuf[jj][:, 3 - d:3 + QT - d],
                                    w=cw[:, mc * 4 + (3 - d): mc * 4 + (4 - d)],
                                    ai=ai, ao=ao:
                                    v.scalar_tensor_tensor(
                                        out=ao[:], in0=p, scalar=w, in1=ai[:],
                                        op0=ALU.mult, op1=ALU.add))
                        em.emit("vector", lambda v, i=pbuf[jj][:, QT:QT + 3],
                                o=halo[sname][:, mc * 3:(mc + 1) * 3]:
                                v.tensor_copy(out=o, in_=i), inc="s_dv")
                        em.wait("scalar", "s_dv")
                        em.emit("scalar", lambda s, o=csb[:, mc, :]:
                                s.activation(out=o, in_=cB[:], func=AF.Silu),
                                inc="s_ev")
                        if dbg and sname == "v" and uid == "m0" and mc == KC - 1:
                            em.wait("gpsimd", "s_ev")
                            em.emit("gpsimd", lambda g, pb=pbuf[jj]: g.dma_start(
                                out=dbg_t["d_pb"], in_=pb[:]), inc="s_st", n=16)
                            em.emit("gpsimd", lambda g: g.dma_start(
                                out=dbg_t["d_cb"], in_=cB[:]), inc="s_st", n=16)
                            em.wait("scalar", "s_st")
                            em.wait("vector", "s_st")
                if dest is not None:
                    for i in range(NT):
                        j = i % 2
                        em.wait("tensor", "s_ev")
                        if i >= 2:
                            em.wait("tensor", "s_ln")
                        if do_norm:
                            em.wait("tensor", "s_lnv")
                        for c in range(KC):
                            em.emit("tensor", lambda t, c=c, ps=ps_tp[j],
                                    src=csb[:, c, i * P:(i + 1) * P]:
                                    t.transpose(ps[:, c, :], src, identb[:]),
                                    inc=("s_mm" if c == KC - 1 else None))
                        em.wait("scalar", "s_mm")
                        tgt = knr[:] if do_norm else dest[:, i, :]
                        em.emit("scalar", lambda s, ps=ps_tp[j], o=tgt:
                                s.copy(out=o, in_=ps[:]), inc="s_ln")
                        if do_norm:
                            em.wait("scalar", "s_lnv")
                            em.wait("scalar", "s_dv")
                            em.emit("scalar", lambda s: s.activation(
                                out=scr1[:], in_=knr[:], func=AF.Square),
                                inc="s_ev")
                            em.wait("vector", "s_ev")
                            em.emit("vector", lambda v: v.tensor_reduce(
                                out=s16a[:],
                                in_=scr1[:].rearrange("p (h k) -> p h k", k=DK),
                                axis=AX.X, op=ALU.add), inc="s_dv")
                            em.wait("scalar", "s_dv")
                            em.emit("scalar", lambda s: s.activation(
                                out=s16b[:], in_=s16a[:], func=AF.Sqrt,
                                bias=epsl2[:], scale=1.0), inc="s_ev")
                            em.wait("vector", "s_ev")
                            em.emit("vector", lambda v: v.reciprocal(
                                out=s16c[:], in_=s16b[:]), inc="s_dv")
                            em.wait("vector", "s_dv")
                            for h in range(H):
                                em.emit("vector", lambda v, h=h, i=i:
                                        v.tensor_scalar(
                                            out=dest[:, i, h * DK:(h + 1) * DK],
                                            in0=knr[:, h * DK:(h + 1) * DK],
                                            scalar1=s16c[:, h:h + 1],
                                            scalar2=None, op0=ALU.mult),
                                        inc=("s_lnv" if h == H - 1 else None))
                em.emit("tensor", lambda t: t.nop(), inc="s_wfree")
                state["wfree_t"][id(wp)] = em.cnt["s_wfree"]
                em.barrier()

        def emit_A_update(A_tmp, k_t, v_t, i):
            em.wait("tensor", "s_dv", state["A_upd"])
            for h in range(H):
                em.emit("tensor", lambda t, h=h,
                        st=(h == 0 or h == 8),
                        l=k_t[:, i, h * DK:(h + 1) * DK],
                        r=v_t[:, i, h * DK:(h + 1) * DK]:
                        t.matmul(A_tmp[:, h, :], l, r, start=st, stop=True,
                                 skip_group_check=True),
                        inc=("s_mm" if h == H - 1 else None))
            em.wait("vector", "s_mm")
            par = state["A_par"]
            em.emit("vector", lambda v, a=A_sb[par][:], b=A_sb[par ^ 1][:],
                    ps=A_tmp[:]:
                    v.tensor_tensor(out=b, in0=a, in1=ps, op=ALU.add),
                    inc="s_dv")
            state["A_par"] = par ^ 1
            state["A_upd"] = em.cnt["s_dv"]

        def emit_scan_quarter(ctx, q, uid, q_t, k_t, v_t, gate_t, y_acc,
                              wo_sb, xsrc):
            ps_sm = [ctx.enter_context(nc.psum_tensor(f"pssm{j}_{uid}",
                                                      [P, KC, P], BF16))
                     for j in range(2)]
            ps_qk = [ctx.enter_context(nc.psum_tensor(f"psqk{j}_{uid}",
                                                      [P, 4, P], F32))
                     for j in range(2)]
            ps_o = ctx.enter_context(nc.psum_tensor(f"pso_{uid}", [P, H, DK], F32))
            A_tmp = ctx.enter_context(nc.psum_tensor(f"Atmp_{uid}", [DK, H, DK], F32))
            qkf = [ctx.enter_context(nc.sbuf_tensor(f"qkf{j}_{uid}",
                                                    [DK, KC, P], BF16))
                   for j in range(2)]
            qkm = [ctx.enter_context(nc.sbuf_tensor(f"qkm{j}_{uid}",
                                                    [P, 4, P], BF16))
                   for j in range(2)]
            st_b = ctx.enter_context(nc.sbuf_tensor(f"stb_{uid}", [DK, H, DK], BF16))
            o_sb = ctx.enter_context(nc.sbuf_tensor(f"osb_{uid}", [P, D], F32))
            scr2 = ctx.enter_context(nc.sbuf_tensor(f"oscr_{uid}", [P, D], F32))
            on_sb = scr1
            og_sb = ctx.enter_context(nc.sbuf_tensor(f"ogsb_{uid}", [P, D], F32))
            og2 = scr2
            ogb = ctx.enter_context(nc.sbuf_tensor(f"ogb_{uid}", [P, D], BF16))
            ogT = ctx.enter_context(nc.sbuf_tensor(f"ogT_{uid}", [P, KC, P], BF16))
            xld2 = [ctx.enter_context(nc.sbuf_tensor(f"x2{j}_{uid}", [P, D], F32))
                    for j in range(2)]
            xsrc_t = xsrc.rearrange("(i p) d -> i p d", p=P)

            for i in range(NT):
                em.wait("vector", "s_dv", state["A_upd"])
                em.emit("vector", lambda v, a=A_sb[state["A_par"]][:]:
                        v.tensor_copy(out=st_b[:], in_=a), inc="s_dv")
                st_ready = em.cnt["s_dv"]
                if i >= 2:
                    em.wait("gpsimd", "s_ev")
                em.emit("gpsimd", lambda g, o=xld2[i % 2][:], s=xsrc_t[i]:
                        g.dma_start(out=o, in_=s), inc="s_gx", n=16)
                for grp in range(4):
                    j = grp % 2
                    em.wait("tensor", "s_ev")
                    for hh in range(4):
                        h = grp * 4 + hh
                        em.emit("tensor", lambda t, hh=hh, ps=ps_sm[j],
                                src=q_t[:, i, h * DK:(h + 1) * DK]:
                                t.transpose(ps[0:DK, hh * 2, :], src,
                                            identb[:]))
                        em.emit("tensor", lambda t, hh=hh, ps=ps_sm[j],
                                src=k_t[:, i, h * DK:(h + 1) * DK]:
                                t.transpose(ps[0:DK, hh * 2 + 1, :], src,
                                            identb[:]),
                                inc=("s_mm" if hh == 3 else None))
                    em.wait("scalar", "s_mm")
                    em.emit("scalar", lambda s, ps=ps_sm[j], o=qkf[j][:]:
                            s.copy(out=o, in_=ps[0:DK]), inc="s_ev")
                    em.wait("tensor", "s_ev")
                    em.wait("tensor", "s_dv")
                    for hh in range(4):
                        em.emit("tensor", lambda t, hh=hh, ps=ps_qk[j], f=qkf[j]:
                                t.matmul(ps[:, hh, :], f[:, hh * 2 + 1, :],
                                         f[:, hh * 2, :], start=(hh == 0),
                                         stop=True, skip_group_check=True),
                                inc=("s_mm" if hh == 3 else None))
                    em.wait("vector", "s_mm")
                    em.wait("vector", "s_dv", st_ready)
                    for hh in range(4):
                        em.emit("vector", lambda v, hh=hh, ps=ps_qk[j], m=qkm[j]:
                                v.tensor_tensor(out=m[:, hh, :],
                                                in0=ps[:, hh, :], in1=maskT[:],
                                                op=ALU.mult),
                                inc=("s_dv" if hh == 3 else None))
                    if dbg and uid == "m0" and i == 0 and grp == 0:
                        em.emit("scalar", lambda s: s.copy(
                            out=snap_qkf[:], in_=qkf[0][:]))
                        em.wait("vector", "s_dv")
                        em.emit("vector", lambda v: v.tensor_copy(
                            out=snap_qkm[:], in_=qkm[0][:]))
                        em.emit("vector", lambda v: v.tensor_copy(
                            out=snap_stb[:], in_=st_b[:]), inc="s_dv")
                    em.wait("tensor", "s_dv")
                    for hh in range(4):
                        h = grp * 4 + hh
                        em.emit("tensor", lambda t, hh=hh, h=h, m=qkm[j],
                                vsrc=v_t[:, i, h * DK:(h + 1) * DK]:
                                t.matmul(ps_o[:, h, :], m[:, hh, :], vsrc,
                                         start=(h == 0 or h == 8), stop=False,
                                         skip_group_check=True))
                        em.emit("tensor", lambda t, hh=hh, h=h, f=qkf[j]:
                                t.matmul(ps_o[:, h, :], f[:, hh * 2, :],
                                         st_b[:, h, :], start=False, stop=True,
                                         skip_group_check=True),
                                inc=("s_mm" if hh == 3 else None))
                emit_A_update(A_tmp, k_t, v_t, i)
                em.wait("scalar", "s_mm")
                em.wait("scalar", "s_dv")
                em.emit("scalar", lambda s: s.copy(out=o_sb[:], in_=ps_o[:]),
                        inc="s_ev")
                if dbg and q == 0 and i == 0:
                    em.emit("scalar", lambda s: s.copy(
                        out=snap_osb[:], in_=o_sb[:]), inc="s_ev")
                em.emit("scalar", lambda s: s.activation(
                    out=scr2[:], in_=o_sb[:], func=AF.Square), inc="s_ev")
                em.wait("vector", "s_ev")
                em.emit("vector", lambda v: v.tensor_reduce(
                    out=s16a[:], in_=o_sb[:].rearrange("p (h k) -> p h k", k=DK),
                    axis=AX.X, op=ALU.add))
                em.emit("vector", lambda v: v.tensor_reduce(
                    out=s16b[:], in_=scr2[:].rearrange("p (h k) -> p h k", k=DK),
                    axis=AX.X, op=ALU.add), inc="s_dv")
                em.wait("vector", "s_dv")
                em.emit("vector", lambda v: v.tensor_scalar(
                    out=rtmp[:], in0=s16a[:], scalar1=1.0 / DK, scalar2=None,
                    op0=ALU.mult), inc="s_dv")
                em.wait("vector", "s_dv")
                em.emit("vector", lambda v: v.tensor_tensor(
                    out=s16a[:], in0=rtmp[:], in1=rtmp[:], op=ALU.mult),
                    inc="s_dv")
                em.wait("vector", "s_dv")
                em.emit("vector", lambda v: v.scalar_tensor_tensor(
                    out=eL[:], in0=s16b[:], scalar=1.0 / DK, in1=s16a[:],
                    op0=ALU.mult, op1=ALU.subtract), inc="s_dv")
                em.wait("scalar", "s_dv")
                em.emit("scalar", lambda s: s.activation(
                    out=s16b[:], in_=eL[:], func=AF.Sqrt, bias=epsln[:],
                    scale=1.0), inc="s_ev")
                em.wait("vector", "s_ev")
                em.emit("vector", lambda v: v.reciprocal(
                    out=s16a[:], in_=s16b[:]), inc="s_dv")
                em.wait("vector", "s_dv")
                for h in range(H):
                    em.emit("vector", lambda v, h=h: v.tensor_scalar(
                        out=on_sb[:, h * DK:(h + 1) * DK],
                        in0=o_sb[:, h * DK:(h + 1) * DK],
                        scalar1=rtmp[:, h:h + 1], scalar2=s16a[:, h:h + 1],
                        op0=ALU.subtract, op1=ALU.mult))
                em.emit("vector", lambda v: v.scalar_tensor_tensor(
                    out=og_sb[:], in0=on_sb[:], scalar=1.0, in1=gamB[:],
                    op0=ALU.mult, op1=ALU.mult), inc="s_dv")
                em.wait("gpsimd", "s_dv")
                em.emit("gpsimd", lambda g: g.tensor_tensor(
                    out=og2[:], in0=og_sb[:], in1=betB[:], op=ALU.add),
                    inc="s_gp")
                em.wait("vector", "s_gp")
                em.wait("vector", "s_gx")
                em.emit("vector", lambda v, i=i: v.tensor_tensor(
                    out=ogb[:], in0=og2[:], in1=gate_t[:, i, :], op=ALU.mult),
                    inc="s_dv")
                em.wait("tensor", "s_dv")
                em.wait("tensor", "s_ev")
                for c in range(KC):
                    em.emit("tensor", lambda t, c=c: t.transpose(
                        ps_sm[0][:, c, :], ogb[:, c * P:(c + 1) * P], identb[:]),
                        inc=("s_mm" if c == KC - 1 else None))
                em.wait("scalar", "s_mm")
                em.emit("scalar", lambda s: s.copy(out=ogT[:], in_=ps_sm[0][:]),
                        inc="s_ev")
                em.wait("tensor", "s_ev")
                po = ps_o[:].rearrange("p h k -> p (h k)")
                for th in range(2):
                    for kc in range(KC):
                        em.emit("tensor", lambda t, th=th, kc=kc, po=po:
                                t.matmul(po[:, th * 512:(th + 1) * 512],
                                         ogT[:, kc, :],
                                         wo_sb[:, kc, th * 512:(th + 1) * 512],
                                         start=(kc == 0), stop=(kc == KC - 1),
                                         skip_group_check=True),
                                inc=("s_mm" if th == 1 and kc == KC - 1
                                     else None))
                em.wait("vector", "s_mm")
                em.emit("vector", lambda v, i=i, po=po: v.tensor_tensor(
                    out=y_acc[:, i, :], in0=po, in1=xld2[i % 2][:], op=ALU.add),
                    inc="s_ev")

        def emit_router_quarter(xT, w_q, wTq, uid):
            with ExitStack() as rctx:
                ps_r_raw = rctx.enter_context(nc.psum_tensor(f"psr_{uid}",
                                                             [P, 512], F32))
                ps_r = ps_r_raw[:].rearrange("p (i e) -> p i e", e=NE)
                ps_wt = rctx.enter_context(nc.psum_tensor(f"pswt_{uid}",
                                                          [NE, 4, P], F32))
                em.wait("tensor", "s_ev")
                for i in range(NT):
                    for kc in range(KC):
                        em.emit("tensor", lambda t, i=i, kc=kc: t.matmul(
                            ps_r[:, i, :], xT[:, kc, i * P:(i + 1) * P],
                            wr_sb[:, kc, :], start=(i == 0 and kc == 0),
                            stop=False, skip_group_check=True))
                    em.emit("tensor", lambda t, i=i: t.matmul(
                        ps_r[:, i, :], ones1b[:], brb[:], start=False,
                        stop=True, skip_group_check=True), inc="s_mm")
                    em.wait("vector", "s_mm")
                    em.wait("vector", "s_ev")
                    em.emit("vector", lambda v, i=i: v.reduce_max(
                        out=sums[:, 0:1], in_=ps_r[:, i, :], axis=AX.X,
                        negate=True), inc="s_dv")
                    em.wait("scalar", "s_dv")
                    em.emit("scalar", lambda s, i=i: s.activation(
                        out=eL[:], in_=ps_r[:, i, :], func=AF.Exp,
                        bias=sums[:, 0:1], scale=1.0), inc="s_ev")
                    em.wait("vector", "s_ev")
                    em.emit("vector", lambda v: v.reduce_max(
                        out=sums[:, 1:2], in_=eL[:], axis=AX.X), inc="s_dv")
                    em.wait("vector", "s_dv")
                    em.emit("vector", lambda v: v.tensor_scalar(
                        out=rtmp[:], in0=eL[:], scalar1=sums[:, 1:2],
                        scalar2=0.0, op0=ALU.subtract, op1=ALU.is_ge),
                        inc="s_dv")
                    em.wait("vector", "s_dv")
                    em.emit("vector", lambda v: v.tensor_tensor(
                        out=s16a[:], in0=eL[:], in1=rtmp[:], op=ALU.mult),
                        inc="s_dv")
                    em.wait("vector", "s_dv")
                    em.emit("vector", lambda v: v.tensor_tensor(
                        out=s16b[:], in0=eL[:], in1=s16a[:], op=ALU.subtract),
                        inc="s_dv")
                    em.wait("vector", "s_dv")
                    em.emit("vector", lambda v: v.reduce_max(
                        out=sums[:, 2:3], in_=s16b[:], axis=AX.X), inc="s_dv")
                    em.wait("vector", "s_dv")
                    em.emit("vector", lambda v: v.tensor_scalar(
                        out=rtmp[:], in0=eL[:], scalar1=sums[:, 2:3],
                        scalar2=0.0, op0=ALU.subtract, op1=ALU.is_ge),
                        inc="s_dv")
                    em.wait("vector", "s_dv")
                    em.emit("vector", lambda v: v.tensor_tensor(
                        out=s16a[:], in0=eL[:], in1=rtmp[:], op=ALU.mult))
                    em.emit("vector", lambda v: v.tensor_tensor(
                        out=sums[:, 3:4], in0=sums[:, 1:2], in1=sums[:, 2:3],
                        op=ALU.add), inc="s_dv")
                    em.wait("vector", "s_dv")
                    em.emit("vector", lambda v: v.reciprocal(
                        out=sums[:, 4:5], in_=sums[:, 3:4]), inc="s_dv")
                    em.wait("vector", "s_dv")
                    em.emit("vector", lambda v, i=i: v.tensor_scalar(
                        out=w_q[:, i, :], in0=s16a[:], scalar1=sums[:, 4:5],
                        scalar2=None, op0=ALU.mult), inc="s_dv")
                    if dbg and i == 0 and uid == "m0":
                        em.wait("gpsimd", "s_dv")
                        em.emit("gpsimd", lambda g: g.dma_start(
                            out=dbg_t["d_eL"], in_=eL[:]), inc="s_st", n=16)
                        em.emit("gpsimd", lambda g: g.dma_start(
                            out=dbg_t["d_sums"], in_=sums[:]), inc="s_st", n=16)
                        em.wait("vector", "s_st")
                    em.wait("tensor", "s_dv")
                    if i >= 2:
                        em.wait("tensor", "s_ev")
                    em.emit("tensor", lambda t, i=i: t.transpose(
                        ps_wt[:, i % 2, :], w_q[:, i, :], identf[:]),
                        inc="s_mm")
                    em.wait("scalar", "s_mm")
                    em.emit("scalar", lambda s, i=i: s.copy(
                        out=wTq[:, i, :], in_=ps_wt[:, i % 2, :]), inc="s_ev")
                em.barrier()

        def emit_ffn_quarter(ctx, q, uid, xT, y_acc, w_q, wTq):
            build = False
            nbuf = 2
            w1sb = [ctx.enter_context(nc.sbuf_tensor(f"w1s{j}_{uid}", [P, KC, D], BF16))
                    for j in range(nbuf)]
            w2sb = [ctx.enter_context(nc.sbuf_tensor(f"w2s{j}_{uid}", [P, KC, D], BF16))
                    for j in range(nbuf)]
            b1rb = [ctx.enter_context(nc.sbuf_tensor(f"b1rb{j}_{uid}", [1, D], BF16))
                    for j in range(2)]
            G1B = [ctx.enter_context(nc.sbuf_tensor(f"G1B{j}_{uid}", [P, D], BF16))
                   for j in range(2)]
            BE1B = [ctx.enter_context(nc.sbuf_tensor(f"BE1B{j}_{uid}", [P, D], BF16))
                    for j in range(2)]
            hn = [ctx.enter_context(nc.sbuf_tensor(f"hn{j}_{uid}", [P, D], F32))
                  for j in range(2)]
            hg = [ctx.enter_context(nc.sbuf_tensor(f"hg{j}_{uid}", [P, D], F32))
                  for j in range(2)]
            hgl = [ctx.enter_context(nc.sbuf_tensor(f"hgl{j}_{uid}", [P, D], BF16))
                   for j in range(2)]
            hw = [ctx.enter_context(nc.sbuf_tensor(f"hw{j}_{uid}", [P, D], BF16))
                  for j in range(2)]
            hT = [ctx.enter_context(nc.sbuf_tensor(f"hT{j}_{uid}", [P, KC, P], BF16))
                  for j in range(2)]
            ytmp = [ctx.enter_context(nc.sbuf_tensor(f"ytmp{j}_{uid}", [P, D], F32))
                    for j in range(2)]
            sums2 = ctx.enter_context(nc.sbuf_tensor(f"sums2_{uid}", [P, 16], F32))
            ps_h = [ctx.enter_context(nc.psum_tensor(f"psh{j}_{uid}", [P, D], F32))
                    for j in range(3)]
            ps_hT = ctx.enter_context(nc.psum_tensor(f"pshT_{uid}", [P, KC, P], BF16))
            ps_y = ctx.enter_context(nc.psum_tensor(f"psy_{uid}", [P, 512], F32))
            em.emit("gpsimd", lambda g: g.memset(sums2[:], 0.0), inc="s_gp")
            em.wait("vector", "s_gp")
            em.wait("scalar", "s_gp")

            T = {}
            base_wfree = em.cnt["s_wfree"]

            def hdr(e):
                j = e % nbuf
                p = e % 2
                wf = base_wfree + max(0, e - 1)
                if e >= 2:
                    em.wait("sync", "s_mm", T[(e - 2, "last_mm")])
                    if T.get((e - 2, "last_stt")):
                        em.wait("sync", "s_dv", T[(e - 2, "last_stt")])
                em.emit("sync", lambda sy, e=e, o=b1rb[p][:]: sy.dma_start(
                    out=o, in_=b1bf[e][None, :]), inc="s_x", n=16)
                em.emit("sync", lambda sy, e=e, o=G1B[p][:]: sy.dma_start(
                    out=o, in_=bass.AP(g1bf_t, e * D, [[0, P], [1, D]])),
                    inc="s_x", n=16)
                em.emit("sync", lambda sy, e=e, o=BE1B[p][:]: sy.dma_start(
                    out=o, in_=bass.AP(be1bf_t, e * D, [[0, P], [1, D]])),
                    inc="s_x", n=16)
                rows_ld = em.cnt["s_x"]
                T[(e, "b1")] = 0
                T[(e, "gbx")] = rows_ld
                emit_weight_load(w1sb[j], cache_w1[e], f"w1_{e}", wf)
                w_ready = emit_weight_load(w2sb[j], cache_w2[e], f"w2_{e}", wf)
                T[(e, "gb")] = 0
                T[(e, "wld")] = w_ready
                T[(e, "rows")] = rows_ld

            def w1_mm(e, i):
                j, p = e % nbuf, e % 2
                par = (e * NT + i) % 3
                pgt = e * NT + i - 3
                if pgt >= 0:
                    pk = divmod(pgt, NT)
                    t1 = T.get((pk[0], pk[1], "hn"))
                    t2 = T.get((pk[0], pk[1], "sq"))
                    if t1:
                        em.wait("tensor", "s_dv", t1)
                    if t2:
                        em.wait("tensor", "s_ev", t2)
                em.wait("tensor", "s_wld", T[(e, "wld")])
                em.wait("tensor", "s_x", T[(e, "rows")])
                for th in range(2):
                    for kc in range(KC):
                        em.emit("tensor", lambda t, th=th, kc=kc, j=j, i=i,
                                k=par:
                                t.matmul(ps_h[k][:, th * 512:(th + 1) * 512],
                                         xT[:, kc, i * P:(i + 1) * P],
                                         w1sb[j][:, kc, th * 512:(th + 1) * 512],
                                         start=(kc == 0), stop=False,
                                         skip_group_check=True))
                    em.emit("tensor", lambda t, th=th, p=p, k=par: t.matmul(
                        ps_h[k][:, th * 512:(th + 1) * 512], ones1b[:],
                        b1rb[p][:, th * 512:(th + 1) * 512], start=False,
                        stop=True, skip_group_check=True),
                        inc=("s_mm" if th == 1 else None))
                T[(e, i, "w1")] = em.cnt["s_mm"]

            def epiA(e, i):
                if ffn_stage < 21:
                    return
                j, p, k = e % nbuf, e % 2, i % 2
                par = (e * NT + i) % 3
                so = 8 * (i % 2)
                C = lambda c: sums2[:, so + c: so + c + 1]
                V = lambda kk: sums2[:, 6 + kk: 7 + kk]
                em.wait("scalar", "s_mm", T[(e, i, "w1")])
                pk = (e, i - 2) if i >= 2 else (e - 1, NT - 2 + i)
                t2 = T.get((pk[0], pk[1], "v1r")) if pk[0] >= 0 else None
                if t2:
                    em.wait("scalar", "s_dv", t2)
                em.emit("scalar", lambda s, k=par, C=C: s.activation(
                    out=scr1[:], in_=ps_h[k][:], func=AF.Square,
                    accum_out=C(1)), inc="s_ev")
                T[(e, i, "sq")] = em.cnt["s_ev"]
                em.wait("vector", "s_mm", T[(e, i, "w1")])
                em.wait("vector", "s_ev", T[(e, i, "sq")])
                em.emit("vector", lambda v, k=par, C=C: v.tensor_reduce(
                    out=C(0), in_=ps_h[k][:], axis=AX.X, op=ALU.add),
                    inc="s_dv")
                T[(e, i, "rd")] = em.cnt["s_dv"]
                em.wait("vector", "s_dv")
                em.emit("vector", lambda v, C=C: v.tensor_tensor(
                    out=C(3), in0=C(0), in1=C(0), op=ALU.mult))
                em.emit("vector", lambda v, C=C: v.tensor_scalar(
                    out=C(2), in0=C(0), scalar1=1.0 / D, scalar2=None,
                    op0=ALU.mult), inc="s_dv")
                em.wait("vector", "s_dv")
                ppair = (e, i // 2 - 1) if i >= 2 else (e - 1, 1)
                t4 = T.get((ppair[0], ppair[1], "sigp")) if ppair[0] >= 0 \
                    else None
                if t4:
                    em.wait("vector", "s_ev", t4)
                em.emit("vector", lambda v, k=k, C=C, V=V: v.scalar_tensor_tensor(
                    out=V(k), in0=C(1), scalar=float(D), in1=C(3),
                    op0=ALU.mult, op1=ALU.subtract), inc="s_dv")
                T[(e, i, "v1r")] = em.cnt["s_dv"]

            def sigpair(e, ip):
                if ffn_stage < 21:
                    return
                i0, i1 = 2 * ip, 2 * ip + 1
                em.wait("scalar", "s_dv", T[(e, i1, "v1r")])
                ppair = (e, ip - 1) if ip >= 1 else (e - 1, 1)
                t5 = T.get((ppair[0], ppair[1], "rcp")) if ppair[0] >= 0 \
                    else None
                if t5:
                    em.wait("scalar", "s_dv", t5)
                em.emit("scalar", lambda s: s.activation(
                    out=sums2[:, 12:14], in_=sums2[:, 6:8], func=AF.Sqrt,
                    bias=epsln[:], scale=1.0 / (D * D)), inc="s_ev")
                T[(e, ip, "sigp")] = em.cnt["s_ev"]
                em.wait("vector", "s_ev", T[(e, ip, "sigp")])
                em.emit("vector", lambda v: v.reciprocal(
                    out=sums2[:, 14:16], in_=sums2[:, 12:14]), inc="s_dv")
                T[(e, ip, "rcp")] = em.cnt["s_dv"]
                em.wait("vector", "s_dv")
                for i in (i0, i1):
                    k = i % 2
                    par = (e * NT + i) % 3
                    so = 8 * k
                    pk2 = (e, i - 2) if i >= 2 else (e - 1, NT - 2 + i)
                    t3 = T.get((pk2[0], pk2[1], "gel")) if pk2[0] >= 0 \
                        else None
                    if t3:
                        em.wait("vector", "s_ev", t3)
                    em.emit("vector", lambda v, k=k, par=par, so=so:
                            v.tensor_scalar(
                        out=hn[k][:], in0=ps_h[par][:],
                        scalar1=sums2[:, so + 2: so + 3],
                        scalar2=sums2[:, 14 + k: 15 + k],
                        op0=ALU.subtract, op1=ALU.mult), inc="s_dv")
                    T[(e, i, "hn")] = em.cnt["s_dv"]

            def epiB(e, i):
                if ffn_stage < 26:
                    T[(e, i, "hw")] = 0
                    T[(e, i, "stt")] = 0
                    return
                j, p, k = e % nbuf, e % 2, i % 2
                em.wait("vector", "s_x", T[(e, "rows")])
                em.emit("vector", lambda v, k=k, p=p: v.scalar_tensor_tensor(
                    out=hg[k][:], in0=hn[k][:], scalar=1.0, in1=G1B[p][:],
                    op0=ALU.mult, op1=ALU.mult), inc="s_dv")
                em.wait("vector", "s_dv")
                em.emit("vector", lambda v, k=k, p=p: v.tensor_tensor(
                    out=hn[k][:], in0=hg[k][:], in1=BE1B[p][:], op=ALU.add),
                    inc="s_dv")
                T[(e, i, "stt")] = em.cnt["s_dv"]
                em.wait("scalar", "s_dv", T[(e, i, "stt")])
                em.emit("scalar", lambda s, k=k: s.activation(
                    out=hgl[k][:], in_=hn[k][:], func=AF.Gelu), inc="s_ev")
                T[(e, i, "gel")] = em.cnt["s_ev"]
                em.wait("vector", "s_ev", T[(e, i, "gel")])
                em.emit("vector", lambda v, k=k, e=e, i=i: v.tensor_scalar(
                    out=hw[k][:], in0=hgl[k][:], scalar1=w_q[:, i, e:e + 1],
                    scalar2=None, op0=ALU.mult), inc="s_dv")
                T[(e, i, "hw")] = em.cnt["s_dv"]

            def t_w2(e, i):
                j, p, k = e % nbuf, e % 2, i % 2
                if ffn_stage < 27:
                    if i == NT - 1:
                        T[(e, "last_mm")] = em.cnt["s_mm"]
                        T[(e, "last_stt")] = T.get((e, i, "stt"), 0)
                        if e == n_exp - 1:
                            for i2 in range(NT):
                                em.emit("gpsimd", lambda g, i2=i2,
                                        o=out_t[q * NT + i2],
                                        src=y_acc[:, i2, :]:
                                        g.dma_start(out=o, in_=src),
                                        inc="s_st", n=16)
                        em.emit("tensor", lambda t: t.nop(), inc="s_wfree")
                    return
                em.wait("tensor", "s_dv", T.get((e, i, "hw"), 0))
                prev = T.get((e, i - 1, "hTe")) or T.get((e - 1, NT - 1, "hTe"))
                if prev:
                    em.wait("tensor", "s_ev", prev)
                for c in range(KC):
                    em.emit("tensor", lambda t, c=c, k=k: t.transpose(
                        ps_hT[:, c, :], hw[k][:, c * P:(c + 1) * P], identb[:]),
                        inc=("s_mm" if c == KC - 1 else None))
                T[(e, i, "tp")] = em.cnt["s_mm"]
                em.wait("scalar", "s_mm", T[(e, i, "tp")])
                em.emit("scalar", lambda s, k=k: s.copy(out=hT[k][:],
                                                        in_=ps_hT[:]),
                        inc="s_ev")
                T[(e, i, "hTe")] = em.cnt["s_ev"]
                if ffn_stage < 28:
                    if i == NT - 1:
                        T[(e, "last_mm")] = em.cnt["s_mm"]
                        T[(e, "last_stt")] = T.get((e, i, "stt"), 0)
                        if e == n_exp - 1:
                            for i2 in range(NT):
                                em.emit("gpsimd", lambda g, i2=i2,
                                        o=out_t[q * NT + i2],
                                        src=y_acc[:, i2, :]:
                                        g.dma_start(out=o, in_=src),
                                        inc="s_st", n=16)
                        em.emit("tensor", lambda t: t.nop(), inc="s_wfree")
                    return
                em.wait("tensor", "s_ev", T[(e, i, "hTe")])
                for th in range(2):
                    prev = (T.get((e, i, "yth", 0)) if th == 1 else
                            (T.get((e, i - 1, "yth", 1))
                             or T.get((e - 1, NT - 1, "yth", 1))))
                    if prev:
                        em.wait("tensor", "s_ev", prev)
                    first = (e != 0)
                    if e == 0:
                        em.emit("tensor", lambda t, th=th, i=i: t.matmul(
                            ps_y[:], wTq[:, i, :],
                            b2sb[:, th * 512:(th + 1) * 512], start=True,
                            stop=False, skip_group_check=True))
                    for kc in range(KC):
                        em.emit("tensor", lambda t, th=th, kc=kc, j=j, k=k,
                                st=(first and kc == 0):
                                t.matmul(ps_y[:],
                                         hT[k][:, kc, :],
                                         w2sb[j][:, kc, th * 512:(th + 1) * 512],
                                         start=st, stop=(kc == KC - 1),
                                         skip_group_check=True),
                                inc=("s_mm" if kc == KC - 1 else None))
                    T[(e, i, "w2h", th)] = em.cnt["s_mm"]
                    em.wait("scalar", "s_mm", T[(e, i, "w2h", th)])
                    if th == 0:
                        prev = T.get((e, i - 2, "ya")) or (
                            T.get((e - 1, NT - 2 + i, "ya")) if i < 2 else None)
                        if prev:
                            em.wait("scalar", "s_gp", prev)
                    em.emit("scalar", lambda s, k=k, th=th: s.copy(
                        out=ytmp[k][:, th * 512:(th + 1) * 512], in_=ps_y[:]),
                        inc="s_ev")
                    T[(e, i, "yth", th)] = em.cnt["s_ev"]
                T[(e, i, "w2")] = em.cnt["s_mm"]
                T[(e, "last_mm")] = em.cnt["s_mm"]
                if i == NT - 1:
                    T[(e, "last_stt")] = T[(e, i, "stt")]
                T[(e, i, "yt")] = T[(e, i, "yth", 1)]
                em.wait("gpsimd", "s_ev", T[(e, i, "yt")])
                prev = T.get((e - 1, i, "ya"))
                if prev:
                    em.wait("gpsimd", "s_gp", prev)
                em.emit("gpsimd", lambda g, i=i, k=k: g.tensor_tensor(
                    out=y_acc[:, i, :], in0=y_acc[:, i, :], in1=ytmp[k][:],
                    op=ALU.add), inc="s_gp")
                T[(e, i, "ya")] = em.cnt["s_gp"]
                if e == n_exp - 1:
                    em.emit("gpsimd", lambda g, i=i,
                            o=out_t[q * NT + i], src=y_acc[:, i, :]:
                            g.dma_start(out=o, in_=src), inc="s_st", n=16)
                if i == NT - 1:
                    em.emit("tensor", lambda t: t.nop(), inc="s_wfree")

            if n_exp == 0:
                for i in range(NT):
                    em.emit("gpsimd", lambda g, i=i, o=out_t[q * NT + i],
                            src=y_acc[:, i, :]:
                            g.dma_start(out=o, in_=src), inc="s_st", n=16)
                return
            hdr(0)
            for e in range(n_exp):
                if e + 1 < n_exp:
                    hdr(e + 1)
                w1_mm(e, 0)
                epiA(e, 0)
                w1_mm(e, 1)
                epiA(e, 1)
                sigpair(e, 0)
                w1_mm(e, 2)
                epiB(e, 0)
                w1_mm(e, 3)
                epiB(e, 1)
                t_w2(e, 0)
                epiA(e, 2)
                t_w2(e, 1)
                epiA(e, 3)
                sigpair(e, 1)
                epiB(e, 2)
                t_w2(e, 2)
                epiB(e, 3)
                t_w2(e, 3)

        build_list = [("proj_k", wT["k"], cache_proj[1]),
                      ("proj_v", wT["v"], cache_proj[2]),
                      ("proj_q", wT["q"], cache_proj[0]),
                      ("proj_g", wT["g"], cache_proj[3]),
                      ("proj_o", wT["o"], cache_proj[4])]
        for e_ in range(NE):
            build_list.append((f"w1_{e_}", w1[e_], cache_w1[e_]))
            build_list.append((f"w2_{e_}", w2[e_], cache_w2[e_]))
        for name, s_, c_ in build_list[:3]:
            emit_cache_build(name, s_.rearrange("(c p) d -> p c d", p=P), c_)
        bpos = 3
        nq_total = max(1, n_pref_q)
        per_q = (len(build_list) - 3 + nq_total - 1) // nq_total

        for pq in range(n_pref_q):
            with ExitStack() as ctx:
                xT = ctx.enter_context(nc.sbuf_tensor(f"xTp_{pq}", [P, KC, QT], BF16))
                k_t = ctx.enter_context(nc.sbuf_tensor(f"ktp_{pq}", [P, NT, D], BF16))
                v_t = ctx.enter_context(nc.sbuf_tensor(f"vtp_{pq}", [P, NT, D], BF16))
                wp = [ctx.enter_context(nc.sbuf_tensor(f"wpp{j}_{pq}",
                                                        [P, KC, D], BF16))
                      for j in range(2)]
                emit_ln_quarter(ctx, xp[pq * QT:(pq + 1) * QT], f"p{pq}", xT)
                lastq = (pq == n_pref_q - 1)
                emit_proj_stream("k", xT, k_t, f"p{pq}", True, True, wp[0])
                emit_proj_stream("v", xT, v_t, f"p{pq}", True, False, wp[1],
                                 last=not lastq)
                if lastq:
                    emit_proj_stream("q", xT, None, f"p{pq}", True, False,
                                     wp[0], last=True)
                with nc.psum_tensor(f"Atmp_p{pq}", [DK, H, DK], F32) as A_tmp:
                    for i in range(NT):
                        emit_A_update(A_tmp, k_t, v_t, i)
                    em.barrier()
            for name, s_, c_ in build_list[bpos:bpos + per_q]:
                emit_cache_build(name, s_.rearrange("(c p) d -> p c d", p=P), c_)
            bpos += per_q
        for name, s_, c_ in build_list[bpos:]:
            emit_cache_build(name, s_.rearrange("(c p) d -> p c d", p=P), c_)

        for q in range(n_main_q):
            with ExitStack() as ctx:
                uid = f"m{q}"
                xT = ctx.enter_context(nc.sbuf_tensor(f"xTm_{q}", [P, KC, QT], BF16))
                w_q = ctx.enter_context(nc.sbuf_tensor(f"wq_{q}", [P, NT, NE], F32))
                wTq = ctx.enter_context(nc.sbuf_tensor(f"wTq_{q}", [NE, NT, P], BF16))
                emit_ln_quarter(ctx, xm[q * QT:(q + 1) * QT], uid, xT)
                em.barrier()
                if q == 0:
                    dump("d_xT", xT[:])
                emit_router_quarter(xT, w_q, wTq, uid)
                if q == 0:
                    dump("d_wq", w_q[:])
                y_acc = ctx.enter_context(nc.sbuf_tensor(
                    f"ya_{q}", [P, NT, D], F32))
                with ExitStack() as pctx:
                    q_t = pctx.enter_context(nc.sbuf_tensor(f"qt_{q}", [P, NT, D], BF16))
                    k_t = pctx.enter_context(nc.sbuf_tensor(f"kt_{q}", [P, NT, D], BF16))
                    v_t = pctx.enter_context(nc.sbuf_tensor(f"vt_{q}", [P, NT, D], BF16))
                    gate_t = pctx.enter_context(nc.sbuf_tensor(f"gt_{q}", [P, NT, D], BF16))
                    with ExitStack() as wctx:
                        wp = [wctx.enter_context(nc.sbuf_tensor(
                            f"wpm{j}_{q}", [P, KC, D], BF16)) for j in range(2)]
                        emit_proj_stream("q", xT, q_t, uid, True, True, wp[0])
                        emit_proj_stream("k", xT, k_t, uid, True, True, wp[1])
                        emit_proj_stream("v", xT, v_t, uid, True, False, wp[0])
                        emit_proj_stream("g", xT, gate_t, uid, False, False,
                                         wp[1], last=True)
                    if q == 0:
                        dump("d_qt", q_t[:])
                        dump("d_kt", k_t[:])
                        dump("d_vt", v_t[:])
                        dump("d_gt", gate_t[:])
                    with ExitStack() as sctx:
                        wo_sb = sctx.enter_context(nc.sbuf_tensor(
                            f"wo_{q}", [P, KC, D], BF16))
                        emit_weight_load(wo_sb, cache_proj[PROJ_IDX["o"]],
                                         "proj_o", em.cnt["s_wfree"])
                        emit_scan_quarter(sctx, q, uid, q_t, k_t, v_t, gate_t,
                                          y_acc, wo_sb,
                                          xm[q * QT:(q + 1) * QT])
                        em.emit("tensor", lambda t: t.nop(), inc="s_wfree")
                        em.barrier()
                    if q == 0:
                        dump("d_ya", y_acc[:])
                with ExitStack() as fctx:
                    emit_ffn_quarter(fctx, q, uid, xT, y_acc, w_q, wTq)
                    em.barrier()
                    em.wait("gpsimd", "s_st")

        if dbg:
            em.barrier()
            for nm, sb in (("d_qkf", snap_qkf), ("d_qkm", snap_qkm),
                           ("d_stb", snap_stb), ("d_osb", snap_osb)):
                em.emit("gpsimd", lambda g, o=dbg_t[nm], s=sb[:]:
                        g.dma_start(out=o, in_=s), inc="s_st", n=16)
        em.wait("gpsimd", "s_st")
        em.wait("gpsimd", "s_cache")

        block = E(nc.Block())
        for eng, meth in (("sync", block.sync), ("tensor", block.tensor),
                          ("vector", block.vector), ("scalar", block.scalar),
                          ("gpsimd", block.gpsimd)):
            def body(E_, _q=em.q[eng]):
                for fn in _q:
                    fn(E_)
            meth(body)

    return nc


def make_inmaps(inputs):
    x = np.asarray(inputs["L0_attn_hidden_states"], np.float32)
    zeros = np.zeros((TPC, D), np.float32)

    def T(a):
        return np.ascontiguousarray(np.asarray(a, np.float32).T)

    def cw_layout(a):
        return np.ascontiguousarray(
            np.asarray(a, np.float32).reshape(KC, P, 4).transpose(1, 0, 2)
            .reshape(P, KC * 4))

    shared = {
        "wqT": T(inputs["L0_attn_Wq"]),
        "wkT": T(inputs["L0_attn_Wk"]),
        "wvT": T(inputs["L0_attn_Wv"]),
        "wgT": T(inputs["L0_attn_Wg"]),
        "woT": T(inputs["L0_attn_Wo"]),
        "cwq": cw_layout(inputs["L0_attn_conv_q"]),
        "cwk": cw_layout(inputs["L0_attn_conv_k"]),
        "cwv": cw_layout(inputs["L0_attn_conv_v"]),
        "winit": np.ascontiguousarray(
            np.asarray(inputs["L0_attn_W_init"], np.float32)[0].transpose(1, 0, 2)),
        "gamma_t": np.ascontiguousarray(
            np.tile(np.asarray(inputs["L0_attn_ln_gamma"], np.float32), H)),
        "beta_t": np.ascontiguousarray(
            np.tile(np.asarray(inputs["L0_attn_ln_beta"], np.float32), H)),
        "wr": np.ascontiguousarray(np.asarray(inputs["L0_ffn_Wr"], np.float32)),
        "br": np.ascontiguousarray(np.asarray(inputs["L0_ffn_br"], np.float32)),
        "w1": np.ascontiguousarray(np.asarray(inputs["L0_ffn_W1"], np.float32)),
        "b1": np.ascontiguousarray(np.asarray(inputs["L0_ffn_b1"], np.float32)),
        "g1": np.ascontiguousarray(np.asarray(inputs["L0_ffn_g1"], np.float32)),
        "be1": np.ascontiguousarray(np.asarray(inputs["L0_ffn_be1"], np.float32)),
        "w2": np.ascontiguousarray(np.asarray(inputs["L0_ffn_W2"], np.float32)),
        "b2": np.ascontiguousarray(np.asarray(inputs["L0_ffn_b2"], np.float32)),
    }
    in_maps = []
    for c in range(8):
        b, hf = c // 2, c % 2
        m = dict(shared)
        m["xm"] = np.ascontiguousarray(x[b, hf * TPC:(hf + 1) * TPC])
        m["xp"] = np.ascontiguousarray(x[b, 0:TPC]) if hf == 1 else zeros
        in_maps.append(m)
    return in_maps


def kernel(**inputs):
    in_maps = make_inmaps(inputs)
    nc = build_nc()
    res = run_bass_kernel_spmd(nc, in_maps, core_ids=list(range(8)))
    outp = np.empty((B, L, D), np.float32)
    for c in range(8):
        b, hf = c // 2, c % 2
        outp[b, hf * TPC:(hf + 1) * TPC] = res.results[c]["out"]
    return outp


# version 13
# speedup vs baseline: 1.1665x; 1.0604x over previous
import numpy as np
import concourse.bass as bass
import concourse.mybir as mybir
from concourse.bass_utils import run_bass_kernel_spmd
from contextlib import ExitStack

F32 = mybir.dt.float32
BF16 = mybir.dt.bfloat16
AF = mybir.ActivationFunctionType
ALU = mybir.AluOpType
AX = mybir.AxisListType

B, L, D = 4, 8192, 1024
H, DK, NE = 16, 64, 16
P = 128
KC = D // P
TPC = L // 2
NT = 4
QT = NT * P
TH = max(1, QT // 512)
EPS_LN = 1e-5
EPS_L2 = 1e-12

ENGINES = ("sync", "tensor", "vector", "scalar", "gpsimd")
SEMS = ("s_x", "s_stg", "s_stg2", "s_wld", "s_wfree", "s_ln", "s_lnv",
        "s_mm", "s_ev", "s_dv", "s_gp", "s_st", "s_cache", "s_cache2",
        "s_gx", "s_cv")

N_PREF_Q = TPC // QT
N_MAIN_Q = TPC // QT
N_EXP = NE


class EM:
    def __init__(self):
        self.q = {e: [] for e in ENGINES}
        self.cnt = {}
        self.sems = {}

    def sem(self, name, handle):
        self.sems[name] = handle
        self.cnt[name] = 0

    def inc(self, name, n=1):
        self.cnt[name] += n
        return self.cnt[name]

    def emit(self, eng, fn, inc=None, n=1):
        if inc is None:
            self.q[eng].append(fn)
        else:
            s = self.sems[inc]
            self.inc(inc, n)
            self.q[eng].append(lambda E, fn=fn, s=s, n=n: fn(E).then_inc(s, n))

    def wait(self, eng, name, val=None):
        s = self.sems[name]
        v = self.cnt[name] if val is None else val
        if v > 0:
            self.q[eng].append(lambda E, s=s, v=v: E.wait_ge(s, v))

    def barrier(self, engines=ENGINES, sems=("s_mm", "s_ev", "s_dv", "s_lnv",
                                             "s_ln", "s_gp")):
        for e in engines:
            for s in sems:
                self.wait(e, s)


def build_nc(n_pref_q=None, n_main_q=None, n_exp=None, dbg=False, ffn_stage=99):
    n_pref_q = N_PREF_Q if n_pref_q is None else n_pref_q
    n_main_q = N_MAIN_Q if n_main_q is None else n_main_q
    n_exp = N_EXP if n_exp is None else n_exp

    nc = bass.Bass("TRN2", target_bir_lowering=False, debug=False)
    em = EM()

    xm = nc.dram_tensor("xm", [TPC, D], F32, kind="ExternalInput").ap()
    xp = nc.dram_tensor("xp", [TPC, D], F32, kind="ExternalInput").ap()
    wT = {s: nc.dram_tensor(f"w{s}T", [D, D], F32, kind="ExternalInput").ap()
          for s in ("q", "k", "v", "g", "o")}
    cw_d = {s: nc.dram_tensor(f"cw{s}", [P, KC * 4], F32, kind="ExternalInput").ap()
            for s in ("q", "k", "v")}
    winit = nc.dram_tensor("winit", [DK, H, DK], F32, kind="ExternalInput").ap()
    gamma_t = nc.dram_tensor("gamma_t", [D], F32, kind="ExternalInput").ap()
    beta_t = nc.dram_tensor("beta_t", [D], F32, kind="ExternalInput").ap()
    wr = nc.dram_tensor("wr", [D, NE], F32, kind="ExternalInput").ap()
    br = nc.dram_tensor("br", [NE], F32, kind="ExternalInput").ap()
    w1 = nc.dram_tensor("w1", [NE, D, D], F32, kind="ExternalInput").ap()
    b1 = nc.dram_tensor("b1", [NE, D], F32, kind="ExternalInput").ap()
    g1 = nc.dram_tensor("g1", [NE, D], F32, kind="ExternalInput").ap()
    be1 = nc.dram_tensor("be1", [NE, D], F32, kind="ExternalInput").ap()
    w2 = nc.dram_tensor("w2", [NE, D, D], F32, kind="ExternalInput").ap()
    b2 = nc.dram_tensor("b2", [NE, D], F32, kind="ExternalInput").ap()
    out = nc.dram_tensor("out", [TPC, D], F32, kind="ExternalOutput").ap()
    out_t = out.rearrange("(i p) d -> i p d", p=P)

    dbg_t = {}
    if dbg:
        for nm, shape, dt in (("d_xT", [P, KC, QT], BF16),
                              ("d_qt", [P, NT, D], BF16),
                              ("d_kt", [P, NT, D], BF16),
                              ("d_vt", [P, NT, D], BF16),
                              ("d_gt", [P, NT, D], BF16),
                              ("d_wq", [P, NT, NE], F32),
                              ("d_osb", [P, D], F32),
                              ("d_ya", [P, NT, D], F32),
                              ("d_eL", [P, NE], F32),
                              ("d_sums", [P, 8], F32),
                              ("d_pb", [P, QT + 3], F32),
                              ("d_cb", [P, QT], F32),
                              ("d_qkf", [DK, KC, P], BF16),
                              ("d_qkm", [P, 4, P], BF16),
                              ("d_stb", [DK, H, DK], BF16)):
            dbg_t[nm] = nc.dram_tensor(nm, shape, dt, kind="ExternalOutput").ap()

    def dump(nm, src_ap):
        if dbg:
            em.barrier()
            em.emit("gpsimd", lambda g, o=dbg_t[nm], s=src_ap:
                    g.dma_start(out=o, in_=s), inc="s_st", n=16)

    snap_qkf = snap_qkm = snap_stb = snap_osb = None

    g1bf_t = nc.dram_tensor("g1bf", [NE, D], BF16, kind="Internal")
    be1bf_t = nc.dram_tensor("be1bf", [NE, D], BF16, kind="Internal")
    b1bf_t = nc.dram_tensor("b1bf", [NE, D], BF16, kind="Internal")
    g1bf, be1bf, b1bf = g1bf_t.ap(), be1bf_t.ap(), b1bf_t.ap()
    cache_proj = nc.dram_tensor("cache_proj", [5, P, KC, D], BF16, kind="Internal").ap()
    cache_w1 = nc.dram_tensor("cache_w1", [NE, P, KC, D], BF16, kind="Internal").ap()
    cache_w2 = nc.dram_tensor("cache_w2", [NE, P, KC, D], BF16, kind="Internal").ap()

    PROJ_IDX = {"q": 0, "k": 1, "v": 2, "g": 3, "o": 4}

    with ExitStack() as top:
        E = top.enter_context
        for s in SEMS:
            em.sem(s, E(nc.semaphore(s)))
        if dbg:
            snap_qkf = E(nc.sbuf_tensor("snap_qkf", [DK, KC, P], BF16))
            snap_qkm = E(nc.sbuf_tensor("snap_qkm", [P, 4, P], BF16))
            snap_stb = E(nc.sbuf_tensor("snap_stb", [DK, H, DK], BF16))
            snap_osb = E(nc.sbuf_tensor("snap_osb", [P, D], F32))

        identb = E(nc.sbuf_tensor("identb", [P, P], BF16, side="right"))
        identf = E(nc.sbuf_tensor("identf", [P, P], F32, side="right"))
        maskT = E(nc.sbuf_tensor("maskT", [P, P], F32, side="right"))
        ones1f = E(nc.sbuf_tensor("ones1f", [1, P], F32, side="right"))
        ones1b = E(nc.sbuf_tensor("ones1b", [1, P], BF16, side="right"))
        onecol = E(nc.sbuf_tensor("onecol", [P, 1], F32, side="right"))
        epsln = E(nc.sbuf_tensor("epsln", [P, 1], F32, side="right"))
        epsl2 = E(nc.sbuf_tensor("epsl2", [P, 1], F32, side="right"))
        gamB = E(nc.sbuf_tensor("gamB", [P, D], F32, side="right"))
        betB = E(nc.sbuf_tensor("betB", [P, D], F32, side="right"))
        winitB = E(nc.sbuf_tensor("winitB", [DK, H, DK], F32, side="right"))
        cw_sb = {s: E(nc.sbuf_tensor(f"cw_{s}", [P, KC * 4], F32, side="right"))
                 for s in ("q", "k", "v")}
        wr_sb = E(nc.sbuf_tensor("wr_sb", [P, KC, NE], BF16, side="right"))
        brb = E(nc.sbuf_tensor("brb", [1, NE], BF16, side="right"))
        b2sb = E(nc.sbuf_tensor("b2sb", [NE, D], BF16, side="right"))
        setup_ctx = ExitStack()
        b2f = setup_ctx.enter_context(nc.sbuf_tensor("b2f", [NE, D], F32))
        rowf = setup_ctx.enter_context(nc.sbuf_tensor("rowf", [NE, D], F32))
        rowb = setup_ctx.enter_context(nc.sbuf_tensor("rowb", [NE, D], BF16))
        grow = setup_ctx.enter_context(nc.sbuf_tensor("grow", [1, D], F32))
        grow2 = setup_ctx.enter_context(nc.sbuf_tensor("grow2", [1, D], F32))
        sums = E(nc.sbuf_tensor("sums", [P, 8], F32, side="right"))
        s16a = E(nc.sbuf_tensor("s16a", [P, NE], F32, side="right"))
        s16b = E(nc.sbuf_tensor("s16b", [P, NE], F32, side="right"))
        s16c = E(nc.sbuf_tensor("s16c", [P, NE], F32, side="right"))
        eL = E(nc.sbuf_tensor("eL", [P, NE], F32, side="right"))
        rtmp = E(nc.sbuf_tensor("rtmp", [P, NE], F32, side="right"))
        scr1 = E(nc.sbuf_tensor("scr1", [P, D], F32, side="right"))
        halo = {s: E(nc.sbuf_tensor(f"halo_{s}", [P, KC * 3], F32, side="right"))
                for s in ("q", "k", "v")}
        bstage = [E(nc.sbuf_tensor(f"bstg{j}", [P, D], F32, side="right"))
                  for j in range(4)]
        bconv = [E(nc.sbuf_tensor(f"bcv{j}", [P, D], BF16, side="right"))
                 for j in range(4)]
        A_sb = [E(nc.sbuf_tensor(f"A_sb{j}", [DK, H, DK], F32, side="right"))
                for j in range(2)]

        def gp_setup(g):
            g.memset(identb[:], 0.0)
            g.memset(identf[:], 0.0)
            g.memset(maskT[:], 1.0)
            g.memset(ones1f[:], 1.0)
            g.memset(ones1b[:], 1.0)
            g.memset(onecol[:], 1.0)
            g.memset(epsln[:], EPS_LN)
            g.memset(epsl2[:], EPS_L2)
            for s in ("q", "k", "v"):
                g.memset(halo[s][:], 0.0)
            g.memset(sums[:], 0.0)
            g.memset(s16a[:], 0.0)
            g.memset(s16b[:], 0.0)
            g.memset(s16c[:], 0.0)
            g.memset(eL[:], 0.0)
            g.memset(rtmp[:], 0.0)
            return g.memset(scr1[:], 0.0)
        em.emit("gpsimd", gp_setup, inc="s_gp")
        em.wait("gpsimd", "s_gp")

        def gp_setup2(g):
            g.affine_select(out=identb[:], in_=identb[:],
                            compare_op=ALU.not_equal, fill=1.0, base=0,
                            pattern=[[-1, P]], channel_multiplier=1)
            g.affine_select(out=identf[:], in_=identf[:],
                            compare_op=ALU.not_equal, fill=1.0, base=0,
                            pattern=[[-1, P]], channel_multiplier=1)
            return g.affine_select(out=maskT[:], in_=maskT[:],
                                   compare_op=ALU.is_ge, fill=0.0, base=0,
                                   pattern=[[1, P]], channel_multiplier=-1)
        em.emit("gpsimd", gp_setup2, inc="s_gp")

        setup_loads = [
            (winitB[:], winit),
            (cw_sb["q"][:], cw_d["q"]),
            (cw_sb["k"][:], cw_d["k"]),
            (cw_sb["v"][:], cw_d["v"]),
            (grow[:], gamma_t[None, :]),
            (grow2[:], beta_t[None, :]),
            (b2f[:], b2),
            (scr1[0:1, 0:NE], br[None, :]),
            (scr1[:, 128:128 + KC * NE].rearrange("p (c e) -> p c e", c=KC),
             wr.rearrange("(c p) e -> p c e", p=P)),
        ]
        for o_, i_ in setup_loads:
            em.emit("sync", lambda sy, o=o_, i=i_: sy.dma_start(out=o, in_=i),
                    inc="s_x", n=16)

        em.wait("vector", "s_x")
        def v_setup0(v):
            return v.tensor_copy(out=A_sb[0][:], in_=winitB[:])
        em.emit("vector", v_setup0, inc="s_dv")
        for srcd, dstd in ((g1, g1bf), (be1, be1bf), (b1, b1bf)):
            em.emit("sync", lambda sy, s=srcd: sy.dma_start(out=rowf[:], in_=s),
                    inc="s_x", n=16)
            em.wait("vector", "s_x")
            em.emit("vector", lambda v: v.tensor_copy(out=rowb[:], in_=rowf[:]),
                    inc="s_dv")
            em.wait("gpsimd", "s_dv")
            em.emit("gpsimd", lambda g, d=dstd: g.dma_start(out=d, in_=rowb[:]),
                    inc="s_cache", n=16)
            em.wait("sync", "s_cache")
        def v_setup1(v):
            for si_, st in enumerate(("q", "k", "v")):
                for mc in range(KC):
                    sl = slice(mc * 4 + 3, mc * 4 + 4)
                    v.tensor_scalar(out=scr1[:, 32 + si_ * KC + mc: 33 + si_ * KC + mc],
                                    in0=cw_sb[st][:, sl],
                                    scalar1=1.0, scalar2=None, op0=ALU.add)
            v.tensor_copy(out=b2sb[:], in_=b2f[:])
            v.tensor_copy(out=brb[:], in_=scr1[0:1, 0:NE])
            return v.tensor_copy(
                out=wr_sb[:], in_=scr1[:, 128:128 + KC * NE].rearrange(
                    "p (c e) -> p c e", c=KC))
        em.emit("vector", v_setup1, inc="s_dv")
        em.wait("vector", "s_dv")
        def v_setup2(v):
            for si_, st in enumerate(("q", "k", "v")):
                for mc in range(KC):
                    sl = slice(mc * 4 + 3, mc * 4 + 4)
                    v.tensor_copy(out=cw_sb[st][:, sl],
                                  in_=scr1[:, 32 + si_ * KC + mc: 33 + si_ * KC + mc])
            return v.tensor_copy(out=scr1[:, 32:33], in_=scr1[:, 33:34])
        em.emit("vector", v_setup2, inc="s_dv")

        with nc.psum_tensor("ps_gb", [P, 512], F32) as ps_gb, \
             nc.psum_tensor("ps_gb2", [P, 512], F32) as ps_gb2:
            em.wait("tensor", "s_x")
            em.wait("tensor", "s_gp")
            for hf in range(2):
                sl = slice(hf * 512, (hf + 1) * 512)
                em.emit("tensor", lambda t, sl=sl: t.matmul(
                    ps_gb[:], ones1f[:], grow[:, sl], start=True, stop=True))
                em.emit("tensor", lambda t, sl=sl: t.matmul(
                    ps_gb2[:], ones1f[:], grow2[:, sl], start=True, stop=True),
                    inc="s_mm")
                em.wait("scalar", "s_mm")
                em.emit("scalar", lambda s, sl=sl: s.copy(
                    out=gamB[:, sl], in_=ps_gb[:]))
                em.emit("scalar", lambda s, sl=sl: s.copy(
                    out=betB[:, sl], in_=ps_gb2[:]), inc="s_ev")
                em.wait("tensor", "s_ev")

        em.barrier()
        setup_ctx.close()

        state = {"proj_cached": set(), "exp_cached": False,
                 "wfree_t": {}, "A_par": 0, "A_upd": 0,
                 "cache_done": {}, "bpar": 0}

        def emit_cache_build(name, src_fp32, cache_ap):
            stg_t = state.setdefault("b_stg_t", {})
            outt = state.setdefault("b_out_t", {})
            srcs = state.setdefault("b_srcs", {})
            SIN = ("s_stg", "s_stg2")
            SOUT = ("s_cache", "s_cache2")

            def piece(pn):
                em.wait("gpsimd", SIN[pn % 2], stg_t[pn])
                if pn - 2 in outt:
                    em.wait("gpsimd", SOUT[pn % 2], outt[pn - 2])
                em.emit("gpsimd", lambda g, i=bstage[pn % 2][:],
                        o=bconv[pn % 2][:]: g.tensor_copy(out=o, in_=i))
                em.emit("gpsimd", lambda g, i=bconv[pn % 2][:],
                        o=srcs[pn][1]: g.dma_start(out=o, in_=i),
                        inc=SOUT[pn % 2], n=16)
                outt[pn] = em.cnt[SOUT[pn % 2]]

            for kc in range(KC):
                n = state["bpar"]
                state["bpar"] += 1
                srcs[n] = (src_fp32[:, kc, :], cache_ap[:, kc, :])
                em.emit("gpsimd", lambda g, o=bstage[n % 2][:],
                        s=srcs[n][0]: g.dma_start(out=o, in_=s),
                        inc=SIN[n % 2], n=16)
                stg_t[n] = em.cnt[SIN[n % 2]]
                if n - 1 >= 0 and n - 1 not in outt:
                    piece(n - 1)
            piece(state["bpar"] - 1)
            state["cache_done"][name] = (em.cnt["s_cache"],
                                         em.cnt["s_cache2"])


        def emit_ln_quarter(ctx_unused, src_ap, uid, xT):
            lctx = ExitStack()
            xld = [lctx.enter_context(nc.sbuf_tensor(f"xld{j}_{uid}", [P, D], F32))
                   for j in range(2)]
            xnt = [lctx.enter_context(nc.sbuf_tensor(f"xnt{j}_{uid}", [P, D], BF16))
                   for j in range(2)]
            ps_tp = [lctx.enter_context(nc.psum_tensor(f"lntp{j}_{uid}",
                                                      [P, KC, P], BF16))
                     for j in range(2)]
            src_t = src_ap.rearrange("(i p) d -> i p d", p=P)
            for i in range(NT):
                j = i % 2
                if i >= 2:
                    em.wait("sync", "s_lnv")
                    em.wait("sync", "s_ev")
                em.emit("sync", lambda sy, o=xld[j][:], s=src_t[i]:
                        sy.dma_start(out=o, in_=s), inc="s_x", n=16)
                em.wait("vector", "s_x")
                em.wait("vector", "s_ev")
                em.emit("vector", lambda v, x=xld[j][:]: v.tensor_reduce(
                    out=sums[:, 0:1], in_=x, axis=AX.X, op=ALU.add), inc="s_lnv")
                em.wait("scalar", "s_lnv")
                em.wait("scalar", "s_dv")
                em.emit("scalar", lambda s, x=xld[j][:]: s.activation(
                    out=scr1[:], in_=x, func=AF.Square, accum_out=sums[:, 1:2]),
                    inc="s_ev")
                em.wait("vector", "s_ev")
                em.emit("vector", lambda v: v.tensor_scalar(
                    out=sums[:, 2:3], in0=sums[:, 0:1], scalar1=1.0 / D,
                    scalar2=None, op0=ALU.mult), inc="s_dv")
                em.wait("vector", "s_dv")
                em.emit("vector", lambda v: v.tensor_tensor(
                    out=sums[:, 3:4], in0=sums[:, 2:3], in1=sums[:, 2:3],
                    op=ALU.mult), inc="s_dv")
                em.wait("vector", "s_dv")
                em.emit("vector", lambda v: v.scalar_tensor_tensor(
                    out=sums[:, 4:5], in0=sums[:, 1:2], scalar=1.0 / D,
                    in1=sums[:, 3:4], op0=ALU.mult, op1=ALU.subtract),
                    inc="s_dv")
                em.wait("scalar", "s_dv")
                em.emit("scalar", lambda s: s.activation(
                    out=sums[:, 5:6], in_=sums[:, 4:5], func=AF.Sqrt,
                    bias=epsln[:], scale=1.0), inc="s_ev")
                em.wait("vector", "s_ev")
                em.emit("vector", lambda v: v.reciprocal(
                    out=sums[:, 6:7], in_=sums[:, 5:6]), inc="s_dv")
                em.wait("vector", "s_dv")
                em.emit("vector", lambda v, x=xld[j][:], o=xnt[j][:]:
                        v.tensor_scalar(out=o, in0=x, scalar1=sums[:, 2:3],
                                        scalar2=sums[:, 6:7], op0=ALU.subtract,
                                        op1=ALU.mult), inc="s_lnv")
                em.wait("tensor", "s_lnv")
                if i >= 2:
                    em.wait("tensor", "s_ln")
                for c in range(KC):
                    em.emit("tensor", lambda t, c=c, ps=ps_tp[j],
                            src=xnt[j][:, c * P:(c + 1) * P]:
                            t.transpose(ps[:, c, :], src, identb[:]),
                            inc=("s_mm" if c == KC - 1 else None))
                em.wait("scalar", "s_mm")
                em.emit("scalar", lambda s, ps=ps_tp[j],
                        o=xT[:, :, i * P:(i + 1) * P]:
                        s.copy(out=o, in_=ps[:]), inc="s_ln")
            em.barrier()
            lctx.close()

        def emit_weight_load(dst_sb, cache_ap, name, wait_free_val):
            em.wait("sync", "s_wfree", wait_free_val)
            cd = state["cache_done"][name]
            em.wait("sync", "s_cache", cd[0])
            em.wait("sync", "s_cache2", cd[1])
            em.emit("sync", lambda sy, o=dst_sb[:], s=cache_ap:
                    sy.dma_start(out=o, in_=s), inc="s_wld", n=16)
            return em.cnt["s_wld"]

        def emit_proj_stream(sname, xT, dest, uid, do_conv, do_norm, wp,
                             last=False):
            si = PROJ_IDX[sname]
            with ExitStack() as ctx:
                psP = [ctx.enter_context(nc.psum_tensor(
                    f"psP{j}_{uid}_{sname}", [P, min(512, QT)], F32)) for j in range(2)]
                ps_tp = [ctx.enter_context(nc.psum_tensor(
                    f"ptp{j}_{uid}_{sname}", [P, KC, P], BF16)) for j in range(2)]
                pbuf = [ctx.enter_context(nc.sbuf_tensor(
                    f"pb{j}_{uid}_{sname}", [P, QT + 3], F32)) for j in range(2)]
                cA = ctx.enter_context(nc.sbuf_tensor(f"cA_{uid}_{sname}", [P, QT], F32))
                cB = ctx.enter_context(nc.sbuf_tensor(f"cB_{uid}_{sname}", [P, QT], F32))
                csb = ctx.enter_context(nc.sbuf_tensor(f"csb_{uid}_{sname}",
                                                       [P, KC, QT], BF16))
                knr = ctx.enter_context(nc.sbuf_tensor(
                    f"knr_{uid}_{sname}", [P, D], BF16)) if do_norm else None

                w_ready = emit_weight_load(
                    wp, cache_proj[si], f"proj_{sname}",
                    state["wfree_t"].get(id(wp), em.cnt["s_wfree"]))

                em.wait("tensor", "s_wld", w_ready)
                em.wait("tensor", "s_ev")
                evac_t = {}
                for u in range(TH * KC):
                    mc, th = u // TH, u % TH
                    j = u % 2
                    if u >= 2:
                        em.wait("tensor", "s_ev", evac_t[u - 2])
                    for kc in range(KC):
                        em.emit("tensor", lambda t, ps=psP[j],
                                l=wp[:, kc, mc * P:(mc + 1) * P],
                                r=xT[:, kc, th * 512:min(QT, (th + 1) * 512)],
                                st=(kc == 0), sp=(kc == KC - 1):
                                t.matmul(ps[:], l, r, start=st, stop=sp,
                                         skip_group_check=True),
                                inc=("s_mm" if kc == KC - 1 else None))
                    em.wait("scalar", "s_mm")
                    if do_conv:
                        em.wait("scalar", "s_dv")
                        em.emit("scalar", lambda s, ps=psP[j],
                                o=pbuf[mc % 2][:, 3 + th * 512: 3 + min(QT, (th + 1) * 512)]:
                                s.copy(out=o, in_=ps[:]), inc="s_ev")
                    else:
                        em.emit("scalar", lambda s, ps=psP[j],
                                o=csb[:, mc, th * 512:min(QT, (th + 1) * 512)]:
                                s.copy(out=o, in_=ps[:]), inc="s_ev")
                    evac_t[u] = em.cnt["s_ev"]
                    if th == TH - 1 and do_conv:
                        jj = mc % 2
                        cw = cw_sb[sname]
                        em.wait("vector", "s_ev")
                        em.emit("vector", lambda v, o=pbuf[jj][:, 0:3],
                                i=halo[sname][:, mc * 3:(mc + 1) * 3]:
                                v.tensor_copy(out=o, in_=i))
                        em.emit("vector", lambda v, p=pbuf[jj][:, 3:3 + QT],
                                w3=cw[:, mc * 4 + 3: mc * 4 + 4]:
                                v.tensor_scalar(out=cA[:], in0=p, scalar1=w3,
                                                scalar2=None, op0=ALU.mult))
                        chain = [(cA, cB), (cB, cA), (cA, cB)]
                        for d in (1, 2, 3):
                            ai, ao = chain[d - 1]
                            em.emit("vector", lambda v, d=d,
                                    p=pbuf[jj][:, 3 - d:3 + QT - d],
                                    w=cw[:, mc * 4 + (3 - d): mc * 4 + (4 - d)],
                                    ai=ai, ao=ao:
                                    v.scalar_tensor_tensor(
                                        out=ao[:], in0=p, scalar=w, in1=ai[:],
                                        op0=ALU.mult, op1=ALU.add))
                        em.emit("vector", lambda v, i=pbuf[jj][:, QT:QT + 3],
                                o=halo[sname][:, mc * 3:(mc + 1) * 3]:
                                v.tensor_copy(out=o, in_=i), inc="s_dv")
                        em.wait("scalar", "s_dv")
                        em.emit("scalar", lambda s, o=csb[:, mc, :]:
                                s.activation(out=o, in_=cB[:], func=AF.Silu),
                                inc="s_ev")
                        if dbg and sname == "v" and uid == "m0" and mc == KC - 1:
                            em.wait("gpsimd", "s_ev")
                            em.emit("gpsimd", lambda g, pb=pbuf[jj]: g.dma_start(
                                out=dbg_t["d_pb"], in_=pb[:]), inc="s_st", n=16)
                            em.emit("gpsimd", lambda g: g.dma_start(
                                out=dbg_t["d_cb"], in_=cB[:]), inc="s_st", n=16)
                            em.wait("scalar", "s_st")
                            em.wait("vector", "s_st")
                if dest is not None:
                    for i in range(NT):
                        j = i % 2
                        em.wait("tensor", "s_ev")
                        if i >= 2:
                            em.wait("tensor", "s_ln")
                        if do_norm:
                            em.wait("tensor", "s_lnv")
                        for c in range(KC):
                            em.emit("tensor", lambda t, c=c, ps=ps_tp[j],
                                    src=csb[:, c, i * P:(i + 1) * P]:
                                    t.transpose(ps[:, c, :], src, identb[:]),
                                    inc=("s_mm" if c == KC - 1 else None))
                        em.wait("scalar", "s_mm")
                        tgt = knr[:] if do_norm else dest[:, i, :]
                        em.emit("scalar", lambda s, ps=ps_tp[j], o=tgt:
                                s.copy(out=o, in_=ps[:]), inc="s_ln")
                        if do_norm:
                            em.wait("scalar", "s_lnv")
                            em.wait("scalar", "s_dv")
                            em.emit("scalar", lambda s: s.activation(
                                out=scr1[:], in_=knr[:], func=AF.Square),
                                inc="s_ev")
                            em.wait("vector", "s_ev")
                            em.emit("vector", lambda v: v.tensor_reduce(
                                out=s16a[:],
                                in_=scr1[:].rearrange("p (h k) -> p h k", k=DK),
                                axis=AX.X, op=ALU.add), inc="s_dv")
                            em.wait("scalar", "s_dv")
                            em.emit("scalar", lambda s: s.activation(
                                out=s16b[:], in_=s16a[:], func=AF.Sqrt,
                                bias=epsl2[:], scale=1.0), inc="s_ev")
                            em.wait("vector", "s_ev")
                            em.emit("vector", lambda v: v.reciprocal(
                                out=s16c[:], in_=s16b[:]), inc="s_dv")
                            em.wait("vector", "s_dv")
                            for h in range(H):
                                em.emit("vector", lambda v, h=h, i=i:
                                        v.tensor_scalar(
                                            out=dest[:, i, h * DK:(h + 1) * DK],
                                            in0=knr[:, h * DK:(h + 1) * DK],
                                            scalar1=s16c[:, h:h + 1],
                                            scalar2=None, op0=ALU.mult),
                                        inc=("s_lnv" if h == H - 1 else None))
                em.emit("tensor", lambda t: t.nop(), inc="s_wfree")
                state["wfree_t"][id(wp)] = em.cnt["s_wfree"]
                em.barrier()

        def emit_A_update(A_tmp, k_t, v_t, i):
            em.wait("tensor", "s_dv", state["A_upd"])
            for h in range(H):
                em.emit("tensor", lambda t, h=h,
                        st=(h == 0 or h == 8),
                        l=k_t[:, i, h * DK:(h + 1) * DK],
                        r=v_t[:, i, h * DK:(h + 1) * DK]:
                        t.matmul(A_tmp[:, h, :], l, r, start=st, stop=True,
                                 skip_group_check=True),
                        inc=("s_mm" if h == H - 1 else None))
            em.wait("vector", "s_mm")
            par = state["A_par"]
            em.emit("vector", lambda v, a=A_sb[par][:], b=A_sb[par ^ 1][:],
                    ps=A_tmp[:]:
                    v.tensor_tensor(out=b, in0=a, in1=ps, op=ALU.add),
                    inc="s_dv")
            state["A_par"] = par ^ 1
            state["A_upd"] = em.cnt["s_dv"]

        def emit_scan_quarter(ctx, q, uid, q_t, k_t, v_t, gate_t, y_acc,
                              wo_sb, xsrc):
            ps_sm = [ctx.enter_context(nc.psum_tensor(f"pssm{j}_{uid}",
                                                      [P, KC, P], BF16))
                     for j in range(2)]
            ps_qk = [ctx.enter_context(nc.psum_tensor(f"psqk{j}_{uid}",
                                                      [P, 4, P], F32))
                     for j in range(2)]
            ps_o = ctx.enter_context(nc.psum_tensor(f"pso_{uid}", [P, H, DK], F32))
            A_tmp = ctx.enter_context(nc.psum_tensor(f"Atmp_{uid}", [DK, H, DK], F32))
            qkf = [ctx.enter_context(nc.sbuf_tensor(f"qkf{j}_{uid}",
                                                    [DK, KC, P], BF16))
                   for j in range(2)]
            qkm = [ctx.enter_context(nc.sbuf_tensor(f"qkm{j}_{uid}",
                                                    [P, 4, P], BF16))
                   for j in range(2)]
            st_b = ctx.enter_context(nc.sbuf_tensor(f"stb_{uid}", [DK, H, DK], BF16))
            o_sb = ctx.enter_context(nc.sbuf_tensor(f"osb_{uid}", [P, D], F32))
            scr2 = ctx.enter_context(nc.sbuf_tensor(f"oscr_{uid}", [P, D], F32))
            on_sb = scr1
            og_sb = ctx.enter_context(nc.sbuf_tensor(f"ogsb_{uid}", [P, D], F32))
            og2 = scr2
            ogb = ctx.enter_context(nc.sbuf_tensor(f"ogb_{uid}", [P, D], BF16))
            ogT = ctx.enter_context(nc.sbuf_tensor(f"ogT_{uid}", [P, KC, P], BF16))
            xld2 = [ctx.enter_context(nc.sbuf_tensor(f"x2{j}_{uid}", [P, D], F32))
                    for j in range(2)]
            xsrc_t = xsrc.rearrange("(i p) d -> i p d", p=P)

            for i in range(NT):
                em.wait("vector", "s_dv", state["A_upd"])
                em.emit("vector", lambda v, a=A_sb[state["A_par"]][:]:
                        v.tensor_copy(out=st_b[:], in_=a), inc="s_dv")
                st_ready = em.cnt["s_dv"]
                if i >= 2:
                    em.wait("gpsimd", "s_ev")
                em.emit("gpsimd", lambda g, o=xld2[i % 2][:], s=xsrc_t[i]:
                        g.dma_start(out=o, in_=s), inc="s_gx", n=16)
                for grp in range(4):
                    j = grp % 2
                    em.wait("tensor", "s_ev")
                    for hh in range(4):
                        h = grp * 4 + hh
                        em.emit("tensor", lambda t, hh=hh, ps=ps_sm[j],
                                src=q_t[:, i, h * DK:(h + 1) * DK]:
                                t.transpose(ps[0:DK, hh * 2, :], src,
                                            identb[:]))
                        em.emit("tensor", lambda t, hh=hh, ps=ps_sm[j],
                                src=k_t[:, i, h * DK:(h + 1) * DK]:
                                t.transpose(ps[0:DK, hh * 2 + 1, :], src,
                                            identb[:]),
                                inc=("s_mm" if hh == 3 else None))
                    em.wait("scalar", "s_mm")
                    em.emit("scalar", lambda s, ps=ps_sm[j], o=qkf[j][:]:
                            s.copy(out=o, in_=ps[0:DK]), inc="s_ev")
                    em.wait("tensor", "s_ev")
                    em.wait("tensor", "s_dv")
                    for hh in range(4):
                        em.emit("tensor", lambda t, hh=hh, ps=ps_qk[j], f=qkf[j]:
                                t.matmul(ps[:, hh, :], f[:, hh * 2 + 1, :],
                                         f[:, hh * 2, :], start=(hh == 0),
                                         stop=True, skip_group_check=True),
                                inc=("s_mm" if hh == 3 else None))
                    em.wait("vector", "s_mm")
                    em.wait("vector", "s_dv", st_ready)
                    for hh in range(4):
                        em.emit("vector", lambda v, hh=hh, ps=ps_qk[j], m=qkm[j]:
                                v.tensor_tensor(out=m[:, hh, :],
                                                in0=ps[:, hh, :], in1=maskT[:],
                                                op=ALU.mult),
                                inc=("s_dv" if hh == 3 else None))
                    if dbg and uid == "m0" and i == 0 and grp == 0:
                        em.emit("scalar", lambda s: s.copy(
                            out=snap_qkf[:], in_=qkf[0][:]))
                        em.wait("vector", "s_dv")
                        em.emit("vector", lambda v: v.tensor_copy(
                            out=snap_qkm[:], in_=qkm[0][:]))
                        em.emit("vector", lambda v: v.tensor_copy(
                            out=snap_stb[:], in_=st_b[:]), inc="s_dv")
                    em.wait("tensor", "s_dv")
                    for hh in range(4):
                        h = grp * 4 + hh
                        em.emit("tensor", lambda t, hh=hh, h=h, m=qkm[j],
                                vsrc=v_t[:, i, h * DK:(h + 1) * DK]:
                                t.matmul(ps_o[:, h, :], m[:, hh, :], vsrc,
                                         start=(h == 0 or h == 8), stop=False,
                                         skip_group_check=True))
                        em.emit("tensor", lambda t, hh=hh, h=h, f=qkf[j]:
                                t.matmul(ps_o[:, h, :], f[:, hh * 2, :],
                                         st_b[:, h, :], start=False, stop=True,
                                         skip_group_check=True),
                                inc=("s_mm" if hh == 3 else None))
                emit_A_update(A_tmp, k_t, v_t, i)
                em.wait("scalar", "s_mm")
                em.wait("scalar", "s_dv")
                em.emit("scalar", lambda s: s.copy(out=o_sb[:], in_=ps_o[:]),
                        inc="s_ev")
                if dbg and q == 0 and i == 0:
                    em.emit("scalar", lambda s: s.copy(
                        out=snap_osb[:], in_=o_sb[:]), inc="s_ev")
                em.emit("scalar", lambda s: s.activation(
                    out=scr2[:], in_=o_sb[:], func=AF.Square), inc="s_ev")
                em.wait("vector", "s_ev")
                em.emit("vector", lambda v: v.tensor_reduce(
                    out=s16a[:], in_=o_sb[:].rearrange("p (h k) -> p h k", k=DK),
                    axis=AX.X, op=ALU.add))
                em.emit("vector", lambda v: v.tensor_reduce(
                    out=s16b[:], in_=scr2[:].rearrange("p (h k) -> p h k", k=DK),
                    axis=AX.X, op=ALU.add), inc="s_dv")
                em.wait("vector", "s_dv")
                em.emit("vector", lambda v: v.tensor_scalar(
                    out=rtmp[:], in0=s16a[:], scalar1=1.0 / DK, scalar2=None,
                    op0=ALU.mult), inc="s_dv")
                em.wait("vector", "s_dv")
                em.emit("vector", lambda v: v.tensor_tensor(
                    out=s16a[:], in0=rtmp[:], in1=rtmp[:], op=ALU.mult),
                    inc="s_dv")
                em.wait("vector", "s_dv")
                em.emit("vector", lambda v: v.scalar_tensor_tensor(
                    out=eL[:], in0=s16b[:], scalar=1.0 / DK, in1=s16a[:],
                    op0=ALU.mult, op1=ALU.subtract), inc="s_dv")
                em.wait("scalar", "s_dv")
                em.emit("scalar", lambda s: s.activation(
                    out=s16b[:], in_=eL[:], func=AF.Sqrt, bias=epsln[:],
                    scale=1.0), inc="s_ev")
                em.wait("vector", "s_ev")
                em.emit("vector", lambda v: v.reciprocal(
                    out=s16a[:], in_=s16b[:]), inc="s_dv")
                em.wait("vector", "s_dv")
                for h in range(H):
                    em.emit("vector", lambda v, h=h: v.tensor_scalar(
                        out=on_sb[:, h * DK:(h + 1) * DK],
                        in0=o_sb[:, h * DK:(h + 1) * DK],
                        scalar1=rtmp[:, h:h + 1], scalar2=s16a[:, h:h + 1],
                        op0=ALU.subtract, op1=ALU.mult))
                em.emit("vector", lambda v: v.scalar_tensor_tensor(
                    out=og_sb[:], in0=on_sb[:], scalar=1.0, in1=gamB[:],
                    op0=ALU.mult, op1=ALU.mult), inc="s_dv")
                em.wait("gpsimd", "s_dv")
                em.emit("gpsimd", lambda g: g.tensor_tensor(
                    out=og2[:], in0=og_sb[:], in1=betB[:], op=ALU.add),
                    inc="s_gp")
                em.wait("vector", "s_gp")
                em.wait("vector", "s_gx")
                em.emit("vector", lambda v, i=i: v.tensor_tensor(
                    out=ogb[:], in0=og2[:], in1=gate_t[:, i, :], op=ALU.mult),
                    inc="s_dv")
                em.wait("tensor", "s_dv")
                em.wait("tensor", "s_ev")
                for c in range(KC):
                    em.emit("tensor", lambda t, c=c: t.transpose(
                        ps_sm[0][:, c, :], ogb[:, c * P:(c + 1) * P], identb[:]),
                        inc=("s_mm" if c == KC - 1 else None))
                em.wait("scalar", "s_mm")
                em.emit("scalar", lambda s: s.copy(out=ogT[:], in_=ps_sm[0][:]),
                        inc="s_ev")
                em.wait("tensor", "s_ev")
                po = ps_o[:].rearrange("p h k -> p (h k)")
                for th in range(2):
                    for kc in range(KC):
                        em.emit("tensor", lambda t, th=th, kc=kc, po=po:
                                t.matmul(po[:, th * 512:(th + 1) * 512],
                                         ogT[:, kc, :],
                                         wo_sb[:, kc, th * 512:(th + 1) * 512],
                                         start=(kc == 0), stop=(kc == KC - 1),
                                         skip_group_check=True),
                                inc=("s_mm" if th == 1 and kc == KC - 1
                                     else None))
                em.wait("vector", "s_mm")
                em.emit("vector", lambda v, i=i, po=po: v.tensor_tensor(
                    out=y_acc[:, i, :], in0=po, in1=xld2[i % 2][:], op=ALU.add),
                    inc="s_ev")

        def emit_router_quarter(xT, w_q, wTq, uid):
            with ExitStack() as rctx:
                ps_r_raw = rctx.enter_context(nc.psum_tensor(f"psr_{uid}",
                                                             [P, 512], F32))
                ps_r = ps_r_raw[:].rearrange("p (i e) -> p i e", e=NE)
                ps_wt = rctx.enter_context(nc.psum_tensor(f"pswt_{uid}",
                                                          [NE, 4, P], F32))
                em.wait("tensor", "s_ev")
                for i in range(NT):
                    for kc in range(KC):
                        em.emit("tensor", lambda t, i=i, kc=kc: t.matmul(
                            ps_r[:, i, :], xT[:, kc, i * P:(i + 1) * P],
                            wr_sb[:, kc, :], start=(i == 0 and kc == 0),
                            stop=False, skip_group_check=True))
                    em.emit("tensor", lambda t, i=i: t.matmul(
                        ps_r[:, i, :], ones1b[:], brb[:], start=False,
                        stop=True, skip_group_check=True), inc="s_mm")
                    em.wait("vector", "s_mm")
                    em.wait("vector", "s_ev")
                    em.emit("vector", lambda v, i=i: v.reduce_max(
                        out=sums[:, 0:1], in_=ps_r[:, i, :], axis=AX.X,
                        negate=True), inc="s_dv")
                    em.wait("scalar", "s_dv")
                    em.emit("scalar", lambda s, i=i: s.activation(
                        out=eL[:], in_=ps_r[:, i, :], func=AF.Exp,
                        bias=sums[:, 0:1], scale=1.0), inc="s_ev")
                    em.wait("vector", "s_ev")
                    em.emit("vector", lambda v: v.reduce_max(
                        out=sums[:, 1:2], in_=eL[:], axis=AX.X), inc="s_dv")
                    em.wait("vector", "s_dv")
                    em.emit("vector", lambda v: v.tensor_scalar(
                        out=rtmp[:], in0=eL[:], scalar1=sums[:, 1:2],
                        scalar2=0.0, op0=ALU.subtract, op1=ALU.is_ge),
                        inc="s_dv")
                    em.wait("vector", "s_dv")
                    em.emit("vector", lambda v: v.tensor_tensor(
                        out=s16a[:], in0=eL[:], in1=rtmp[:], op=ALU.mult),
                        inc="s_dv")
                    em.wait("vector", "s_dv")
                    em.emit("vector", lambda v: v.tensor_tensor(
                        out=s16b[:], in0=eL[:], in1=s16a[:], op=ALU.subtract),
                        inc="s_dv")
                    em.wait("vector", "s_dv")
                    em.emit("vector", lambda v: v.reduce_max(
                        out=sums[:, 2:3], in_=s16b[:], axis=AX.X), inc="s_dv")
                    em.wait("vector", "s_dv")
                    em.emit("vector", lambda v: v.tensor_scalar(
                        out=rtmp[:], in0=eL[:], scalar1=sums[:, 2:3],
                        scalar2=0.0, op0=ALU.subtract, op1=ALU.is_ge),
                        inc="s_dv")
                    em.wait("vector", "s_dv")
                    em.emit("vector", lambda v: v.tensor_tensor(
                        out=s16a[:], in0=eL[:], in1=rtmp[:], op=ALU.mult))
                    em.emit("vector", lambda v: v.tensor_tensor(
                        out=sums[:, 3:4], in0=sums[:, 1:2], in1=sums[:, 2:3],
                        op=ALU.add), inc="s_dv")
                    em.wait("vector", "s_dv")
                    em.emit("vector", lambda v: v.reciprocal(
                        out=sums[:, 4:5], in_=sums[:, 3:4]), inc="s_dv")
                    em.wait("vector", "s_dv")
                    em.emit("vector", lambda v, i=i: v.tensor_scalar(
                        out=w_q[:, i, :], in0=s16a[:], scalar1=sums[:, 4:5],
                        scalar2=None, op0=ALU.mult), inc="s_dv")
                    if dbg and i == 0 and uid == "m0":
                        em.wait("gpsimd", "s_dv")
                        em.emit("gpsimd", lambda g: g.dma_start(
                            out=dbg_t["d_eL"], in_=eL[:]), inc="s_st", n=16)
                        em.emit("gpsimd", lambda g: g.dma_start(
                            out=dbg_t["d_sums"], in_=sums[:]), inc="s_st", n=16)
                        em.wait("vector", "s_st")
                    em.wait("tensor", "s_dv")
                    if i >= 2:
                        em.wait("tensor", "s_ev")
                    em.emit("tensor", lambda t, i=i: t.transpose(
                        ps_wt[:, i % 2, :], w_q[:, i, :], identf[:]),
                        inc="s_mm")
                    em.wait("scalar", "s_mm")
                    em.emit("scalar", lambda s, i=i: s.copy(
                        out=wTq[:, i, :], in_=ps_wt[:, i % 2, :]), inc="s_ev")
                em.barrier()

        def emit_ffn_quarter(ctx, q, uid, xT, y_acc, w_q, wTq):
            build = False
            nbuf = 2
            w1sb = [ctx.enter_context(nc.sbuf_tensor(f"w1s{j}_{uid}", [P, KC, D], BF16))
                    for j in range(nbuf)]
            w2sb = [ctx.enter_context(nc.sbuf_tensor(f"w2s{j}_{uid}", [P, KC, D], BF16))
                    for j in range(nbuf)]
            b1rb = [ctx.enter_context(nc.sbuf_tensor(f"b1rb{j}_{uid}", [1, D], BF16))
                    for j in range(2)]
            G1B = [ctx.enter_context(nc.sbuf_tensor(f"G1B{j}_{uid}", [P, D], BF16))
                   for j in range(2)]
            BE1B = [ctx.enter_context(nc.sbuf_tensor(f"BE1B{j}_{uid}", [P, D], BF16))
                    for j in range(2)]
            hn = [ctx.enter_context(nc.sbuf_tensor(f"hn{j}_{uid}", [P, D], F32))
                  for j in range(2)]
            hg = [ctx.enter_context(nc.sbuf_tensor(f"hg{j}_{uid}", [P, D], F32))
                  for j in range(2)]
            hgl = [ctx.enter_context(nc.sbuf_tensor(f"hgl{j}_{uid}", [P, D], BF16))
                   for j in range(2)]
            hw = [ctx.enter_context(nc.sbuf_tensor(f"hw{j}_{uid}", [P, D], BF16))
                  for j in range(2)]
            hT = [ctx.enter_context(nc.sbuf_tensor(f"hT{j}_{uid}", [P, KC, P], BF16))
                  for j in range(2)]
            ytmp = [ctx.enter_context(nc.sbuf_tensor(f"ytmp{j}_{uid}", [P, D], F32))
                    for j in range(2)]
            sums2 = ctx.enter_context(nc.sbuf_tensor(f"sums2_{uid}", [P, 16], F32))
            ps_h = [ctx.enter_context(nc.psum_tensor(f"psh{j}_{uid}", [P, D], F32))
                    for j in range(3)]
            ps_hT = ctx.enter_context(nc.psum_tensor(f"pshT_{uid}", [P, KC, P], BF16))
            ps_y = ctx.enter_context(nc.psum_tensor(f"psy_{uid}", [P, 512], F32))
            em.emit("gpsimd", lambda g: g.memset(sums2[:], 0.0), inc="s_gp")
            em.wait("vector", "s_gp")
            em.wait("scalar", "s_gp")

            T = {}
            base_wfree = em.cnt["s_wfree"]

            def hdr(e):
                j = e % nbuf
                p = e % 2
                wf = base_wfree + max(0, e - 1)
                if e >= 2:
                    em.wait("sync", "s_mm", T[(e - 2, "last_mm")])
                    if T.get((e - 2, "last_stt")):
                        em.wait("sync", "s_dv", T[(e - 2, "last_stt")])
                em.emit("sync", lambda sy, e=e, o=b1rb[p][:]: sy.dma_start(
                    out=o, in_=b1bf[e][None, :]), inc="s_x", n=16)
                em.emit("sync", lambda sy, e=e, o=G1B[p][:]: sy.dma_start(
                    out=o, in_=bass.AP(g1bf_t, e * D, [[0, P], [1, D]])),
                    inc="s_x", n=16)
                em.emit("sync", lambda sy, e=e, o=BE1B[p][:]: sy.dma_start(
                    out=o, in_=bass.AP(be1bf_t, e * D, [[0, P], [1, D]])),
                    inc="s_x", n=16)
                rows_ld = em.cnt["s_x"]
                T[(e, "b1")] = 0
                T[(e, "gbx")] = rows_ld
                emit_weight_load(w1sb[j], cache_w1[e], f"w1_{e}", wf)
                w_ready = emit_weight_load(w2sb[j], cache_w2[e], f"w2_{e}", wf)
                T[(e, "gb")] = 0
                T[(e, "wld")] = w_ready
                T[(e, "rows")] = rows_ld

            def w1_mm(e, i):
                j, p = e % nbuf, e % 2
                par = (e * NT + i) % 3
                pgt = e * NT + i - 3
                if pgt >= 0:
                    pk = divmod(pgt, NT)
                    t1 = T.get((pk[0], pk[1], "hn"))
                    t2 = T.get((pk[0], pk[1], "sq"))
                    if t1:
                        em.wait("tensor", "s_dv", t1)
                    if t2:
                        em.wait("tensor", "s_ev", t2)
                em.wait("tensor", "s_wld", T[(e, "wld")])
                em.wait("tensor", "s_x", T[(e, "rows")])
                for th in range(2):
                    for kc in range(KC):
                        em.emit("tensor", lambda t, th=th, kc=kc, j=j, i=i,
                                k=par:
                                t.matmul(ps_h[k][:, th * 512:(th + 1) * 512],
                                         xT[:, kc, i * P:(i + 1) * P],
                                         w1sb[j][:, kc, th * 512:(th + 1) * 512],
                                         start=(kc == 0), stop=False,
                                         skip_group_check=True))
                    em.emit("tensor", lambda t, th=th, p=p, k=par: t.matmul(
                        ps_h[k][:, th * 512:(th + 1) * 512], ones1b[:],
                        b1rb[p][:, th * 512:(th + 1) * 512], start=False,
                        stop=True, skip_group_check=True),
                        inc=("s_mm" if th == 1 else None))
                T[(e, i, "w1")] = em.cnt["s_mm"]

            def epiA(e, i):
                if ffn_stage < 21:
                    return
                j, p, k = e % nbuf, e % 2, i % 2
                par = (e * NT + i) % 3
                so = 8 * (i % 2)
                C = lambda c: sums2[:, so + c: so + c + 1]
                V = lambda kk: sums2[:, 6 + kk: 7 + kk]
                em.wait("scalar", "s_mm", T[(e, i, "w1")])
                pk = (e, i - 2) if i >= 2 else (e - 1, NT - 2 + i)
                t2 = T.get((pk[0], pk[1], "v1r")) if pk[0] >= 0 else None
                if t2:
                    em.wait("scalar", "s_dv", t2)
                em.emit("scalar", lambda s, k=par, C=C: s.activation(
                    out=scr1[:], in_=ps_h[k][:], func=AF.Square,
                    accum_out=C(1)), inc="s_ev")
                T[(e, i, "sq")] = em.cnt["s_ev"]
                em.wait("vector", "s_mm", T[(e, i, "w1")])
                em.wait("vector", "s_ev", T[(e, i, "sq")])
                em.emit("vector", lambda v, k=par, C=C: v.tensor_reduce(
                    out=C(0), in_=ps_h[k][:], axis=AX.X, op=ALU.add),
                    inc="s_dv")
                T[(e, i, "rd")] = em.cnt["s_dv"]
                em.wait("vector", "s_dv")
                em.emit("vector", lambda v, C=C: v.tensor_tensor(
                    out=C(3), in0=C(0), in1=C(0), op=ALU.mult))
                em.emit("vector", lambda v, C=C: v.tensor_scalar(
                    out=C(2), in0=C(0), scalar1=1.0 / D, scalar2=None,
                    op0=ALU.mult), inc="s_dv")
                em.wait("vector", "s_dv")
                ppair = (e, i // 2 - 1) if i >= 2 else (e - 1, 1)
                t4 = T.get((ppair[0], ppair[1], "sigp")) if ppair[0] >= 0 \
                    else None
                if t4:
                    em.wait("vector", "s_ev", t4)
                em.emit("vector", lambda v, k=k, C=C, V=V: v.scalar_tensor_tensor(
                    out=V(k), in0=C(1), scalar=float(D), in1=C(3),
                    op0=ALU.mult, op1=ALU.subtract), inc="s_dv")
                T[(e, i, "v1r")] = em.cnt["s_dv"]

            def sigpair(e, ip):
                if ffn_stage < 21:
                    return
                i0, i1 = 2 * ip, 2 * ip + 1
                em.wait("scalar", "s_dv", T[(e, i1, "v1r")])
                ppair = (e, ip - 1) if ip >= 1 else (e - 1, 1)
                t5 = T.get((ppair[0], ppair[1], "rcp")) if ppair[0] >= 0 \
                    else None
                if t5:
                    em.wait("scalar", "s_dv", t5)
                em.emit("scalar", lambda s: s.activation(
                    out=sums2[:, 12:14], in_=sums2[:, 6:8], func=AF.Sqrt,
                    bias=epsln[:], scale=1.0 / (D * D)), inc="s_ev")
                T[(e, ip, "sigp")] = em.cnt["s_ev"]
                em.wait("vector", "s_ev", T[(e, ip, "sigp")])
                em.emit("vector", lambda v: v.reciprocal(
                    out=sums2[:, 14:16], in_=sums2[:, 12:14]), inc="s_dv")
                T[(e, ip, "rcp")] = em.cnt["s_dv"]
                em.wait("vector", "s_dv")
                for i in (i0, i1):
                    k = i % 2
                    par = (e * NT + i) % 3
                    so = 8 * k
                    pk2 = (e, i - 2) if i >= 2 else (e - 1, NT - 2 + i)
                    t3 = T.get((pk2[0], pk2[1], "gel")) if pk2[0] >= 0 \
                        else None
                    if t3:
                        em.wait("vector", "s_ev", t3)
                    em.emit("vector", lambda v, k=k, par=par, so=so:
                            v.tensor_scalar(
                        out=hn[k][:], in0=ps_h[par][:],
                        scalar1=sums2[:, so + 2: so + 3],
                        scalar2=sums2[:, 14 + k: 15 + k],
                        op0=ALU.subtract, op1=ALU.mult), inc="s_dv")
                    T[(e, i, "hn")] = em.cnt["s_dv"]

            def epiB(e, i):
                if ffn_stage < 26:
                    T[(e, i, "hw")] = 0
                    T[(e, i, "stt")] = 0
                    return
                j, p, k = e % nbuf, e % 2, i % 2
                em.wait("vector", "s_x", T[(e, "rows")])
                em.emit("vector", lambda v, k=k, p=p: v.scalar_tensor_tensor(
                    out=hg[k][:], in0=hn[k][:], scalar=1.0, in1=G1B[p][:],
                    op0=ALU.mult, op1=ALU.mult), inc="s_dv")
                em.wait("vector", "s_dv")
                em.emit("vector", lambda v, k=k, p=p: v.tensor_tensor(
                    out=hn[k][:], in0=hg[k][:], in1=BE1B[p][:], op=ALU.add),
                    inc="s_dv")
                T[(e, i, "stt")] = em.cnt["s_dv"]
                em.wait("scalar", "s_dv", T[(e, i, "stt")])
                em.emit("scalar", lambda s, k=k: s.activation(
                    out=hgl[k][:], in_=hn[k][:], func=AF.Gelu), inc="s_ev")
                T[(e, i, "gel")] = em.cnt["s_ev"]
                em.wait("vector", "s_ev", T[(e, i, "gel")])
                em.emit("vector", lambda v, k=k, e=e, i=i: v.tensor_scalar(
                    out=hw[k][:], in0=hgl[k][:], scalar1=w_q[:, i, e:e + 1],
                    scalar2=None, op0=ALU.mult), inc="s_dv")
                T[(e, i, "hw")] = em.cnt["s_dv"]

            def t_w2(e, i):
                j, p, k = e % nbuf, e % 2, i % 2
                if ffn_stage < 27:
                    if i == NT - 1:
                        T[(e, "last_mm")] = em.cnt["s_mm"]
                        T[(e, "last_stt")] = T.get((e, i, "stt"), 0)
                        if e == n_exp - 1:
                            for i2 in range(NT):
                                em.emit("gpsimd", lambda g, i2=i2,
                                        o=out_t[q * NT + i2],
                                        src=y_acc[:, i2, :]:
                                        g.dma_start(out=o, in_=src),
                                        inc="s_st", n=16)
                        em.emit("tensor", lambda t: t.nop(), inc="s_wfree")
                    return
                em.wait("tensor", "s_dv", T.get((e, i, "hw"), 0))
                prev = T.get((e, i - 1, "hTe")) or T.get((e - 1, NT - 1, "hTe"))
                if prev:
                    em.wait("tensor", "s_ev", prev)
                for c in range(KC):
                    em.emit("tensor", lambda t, c=c, k=k: t.transpose(
                        ps_hT[:, c, :], hw[k][:, c * P:(c + 1) * P], identb[:]),
                        inc=("s_mm" if c == KC - 1 else None))
                T[(e, i, "tp")] = em.cnt["s_mm"]
                em.wait("scalar", "s_mm", T[(e, i, "tp")])
                em.emit("scalar", lambda s, k=k: s.copy(out=hT[k][:],
                                                        in_=ps_hT[:]),
                        inc="s_ev")
                T[(e, i, "hTe")] = em.cnt["s_ev"]
                if ffn_stage < 28:
                    if i == NT - 1:
                        T[(e, "last_mm")] = em.cnt["s_mm"]
                        T[(e, "last_stt")] = T.get((e, i, "stt"), 0)
                        if e == n_exp - 1:
                            for i2 in range(NT):
                                em.emit("gpsimd", lambda g, i2=i2,
                                        o=out_t[q * NT + i2],
                                        src=y_acc[:, i2, :]:
                                        g.dma_start(out=o, in_=src),
                                        inc="s_st", n=16)
                        em.emit("tensor", lambda t: t.nop(), inc="s_wfree")
                    return
                em.wait("tensor", "s_ev", T[(e, i, "hTe")])
                for th in range(2):
                    prev = (T.get((e, i, "yth", 0)) if th == 1 else
                            (T.get((e, i - 1, "yth", 1))
                             or T.get((e - 1, NT - 1, "yth", 1))))
                    if prev:
                        em.wait("tensor", "s_ev", prev)
                    first = (e != 0)
                    if e == 0:
                        em.emit("tensor", lambda t, th=th, i=i: t.matmul(
                            ps_y[:], wTq[:, i, :],
                            b2sb[:, th * 512:(th + 1) * 512], start=True,
                            stop=False, skip_group_check=True))
                    for kc in range(KC):
                        em.emit("tensor", lambda t, th=th, kc=kc, j=j, k=k,
                                st=(first and kc == 0):
                                t.matmul(ps_y[:],
                                         hT[k][:, kc, :],
                                         w2sb[j][:, kc, th * 512:(th + 1) * 512],
                                         start=st, stop=(kc == KC - 1),
                                         skip_group_check=True),
                                inc=("s_mm" if kc == KC - 1 else None))
                    T[(e, i, "w2h", th)] = em.cnt["s_mm"]
                    em.wait("scalar", "s_mm", T[(e, i, "w2h", th)])
                    if th == 0:
                        prev = T.get((e, i - 2, "ya")) or (
                            T.get((e - 1, NT - 2 + i, "ya")) if i < 2 else None)
                        if prev:
                            em.wait("scalar", "s_gp", prev)
                    em.emit("scalar", lambda s, k=k, th=th: s.copy(
                        out=ytmp[k][:, th * 512:(th + 1) * 512], in_=ps_y[:]),
                        inc="s_ev")
                    T[(e, i, "yth", th)] = em.cnt["s_ev"]
                T[(e, i, "w2")] = em.cnt["s_mm"]
                T[(e, "last_mm")] = em.cnt["s_mm"]
                if i == NT - 1:
                    T[(e, "last_stt")] = T[(e, i, "stt")]
                T[(e, i, "yt")] = T[(e, i, "yth", 1)]
                em.wait("gpsimd", "s_ev", T[(e, i, "yt")])
                prev = T.get((e - 1, i, "ya"))
                if prev:
                    em.wait("gpsimd", "s_gp", prev)
                em.emit("gpsimd", lambda g, i=i, k=k: g.tensor_tensor(
                    out=y_acc[:, i, :], in0=y_acc[:, i, :], in1=ytmp[k][:],
                    op=ALU.add), inc="s_gp")
                T[(e, i, "ya")] = em.cnt["s_gp"]
                if e == n_exp - 1:
                    em.emit("gpsimd", lambda g, i=i,
                            o=out_t[q * NT + i], src=y_acc[:, i, :]:
                            g.dma_start(out=o, in_=src), inc="s_st", n=16)
                if i == NT - 1:
                    em.emit("tensor", lambda t: t.nop(), inc="s_wfree")

            if n_exp == 0:
                for i in range(NT):
                    em.emit("gpsimd", lambda g, i=i, o=out_t[q * NT + i],
                            src=y_acc[:, i, :]:
                            g.dma_start(out=o, in_=src), inc="s_st", n=16)
                return
            hdr(0)
            for e in range(n_exp):
                if e + 1 < n_exp:
                    hdr(e + 1)
                w1_mm(e, 0)
                epiA(e, 0)
                w1_mm(e, 1)
                epiA(e, 1)
                sigpair(e, 0)
                w1_mm(e, 2)
                epiB(e, 0)
                w1_mm(e, 3)
                epiB(e, 1)
                t_w2(e, 0)
                epiA(e, 2)
                t_w2(e, 1)
                epiA(e, 3)
                sigpair(e, 1)
                epiB(e, 2)
                t_w2(e, 2)
                epiB(e, 3)
                t_w2(e, 3)

        build_list = [("proj_k", wT["k"], cache_proj[1]),
                      ("proj_v", wT["v"], cache_proj[2]),
                      ("proj_q", wT["q"], cache_proj[0]),
                      ("proj_g", wT["g"], cache_proj[3]),
                      ("proj_o", wT["o"], cache_proj[4])]
        for e_ in range(NE):
            build_list.append((f"w1_{e_}", w1[e_], cache_w1[e_]))
            build_list.append((f"w2_{e_}", w2[e_], cache_w2[e_]))
        for name, s_, c_ in build_list[:3]:
            emit_cache_build(name, s_.rearrange("(c p) d -> p c d", p=P), c_)
        bpos = 3
        nq_total = max(1, n_pref_q)
        per_q = (len(build_list) - 3 + nq_total - 1) // nq_total

        for pq in range(n_pref_q):
            with ExitStack() as ctx:
                xT = ctx.enter_context(nc.sbuf_tensor(f"xTp_{pq}", [P, KC, QT], BF16))
                k_t = ctx.enter_context(nc.sbuf_tensor(f"ktp_{pq}", [P, NT, D], BF16))
                v_t = ctx.enter_context(nc.sbuf_tensor(f"vtp_{pq}", [P, NT, D], BF16))
                wp = [ctx.enter_context(nc.sbuf_tensor(f"wpp{j}_{pq}",
                                                        [P, KC, D], BF16))
                      for j in range(2)]
                emit_ln_quarter(ctx, xp[pq * QT:(pq + 1) * QT], f"p{pq}", xT)
                lastq = (pq == n_pref_q - 1)
                emit_proj_stream("k", xT, k_t, f"p{pq}", True, True, wp[0])
                emit_proj_stream("v", xT, v_t, f"p{pq}", True, False, wp[1],
                                 last=not lastq)
                if lastq:
                    emit_proj_stream("q", xT, None, f"p{pq}", True, False,
                                     wp[0], last=True)
                with nc.psum_tensor(f"Atmp_p{pq}", [DK, H, DK], F32) as A_tmp:
                    for i in range(NT):
                        emit_A_update(A_tmp, k_t, v_t, i)
                    em.barrier()
            for name, s_, c_ in build_list[bpos:bpos + per_q]:
                emit_cache_build(name, s_.rearrange("(c p) d -> p c d", p=P), c_)
            bpos += per_q
        for name, s_, c_ in build_list[bpos:]:
            emit_cache_build(name, s_.rearrange("(c p) d -> p c d", p=P), c_)

        for q in range(n_main_q):
            with ExitStack() as ctx:
                uid = f"m{q}"
                xT = ctx.enter_context(nc.sbuf_tensor(f"xTm_{q}", [P, KC, QT], BF16))
                w_q = ctx.enter_context(nc.sbuf_tensor(f"wq_{q}", [P, NT, NE], F32))
                wTq = ctx.enter_context(nc.sbuf_tensor(f"wTq_{q}", [NE, NT, P], BF16))
                emit_ln_quarter(ctx, xm[q * QT:(q + 1) * QT], uid, xT)
                em.barrier()
                if q == 0:
                    dump("d_xT", xT[:])
                emit_router_quarter(xT, w_q, wTq, uid)
                if q == 0:
                    dump("d_wq", w_q[:])
                y_acc = ctx.enter_context(nc.sbuf_tensor(
                    f"ya_{q}", [P, NT, D], F32))
                with ExitStack() as pctx:
                    q_t = pctx.enter_context(nc.sbuf_tensor(f"qt_{q}", [P, NT, D], BF16))
                    k_t = pctx.enter_context(nc.sbuf_tensor(f"kt_{q}", [P, NT, D], BF16))
                    v_t = pctx.enter_context(nc.sbuf_tensor(f"vt_{q}", [P, NT, D], BF16))
                    gate_t = pctx.enter_context(nc.sbuf_tensor(f"gt_{q}", [P, NT, D], BF16))
                    with ExitStack() as wctx:
                        wp = [wctx.enter_context(nc.sbuf_tensor(
                            f"wpm{j}_{q}", [P, KC, D], BF16)) for j in range(2)]
                        emit_proj_stream("q", xT, q_t, uid, True, True, wp[0])
                        emit_proj_stream("k", xT, k_t, uid, True, True, wp[1])
                        emit_proj_stream("v", xT, v_t, uid, True, False, wp[0])
                        emit_proj_stream("g", xT, gate_t, uid, False, False,
                                         wp[1], last=True)
                    if q == 0:
                        dump("d_qt", q_t[:])
                        dump("d_kt", k_t[:])
                        dump("d_vt", v_t[:])
                        dump("d_gt", gate_t[:])
                    with ExitStack() as sctx:
                        wo_sb = sctx.enter_context(nc.sbuf_tensor(
                            f"wo_{q}", [P, KC, D], BF16))
                        emit_weight_load(wo_sb, cache_proj[PROJ_IDX["o"]],
                                         "proj_o", em.cnt["s_wfree"])
                        emit_scan_quarter(sctx, q, uid, q_t, k_t, v_t, gate_t,
                                          y_acc, wo_sb,
                                          xm[q * QT:(q + 1) * QT])
                        em.emit("tensor", lambda t: t.nop(), inc="s_wfree")
                        em.barrier()
                    if q == 0:
                        dump("d_ya", y_acc[:])
                with ExitStack() as fctx:
                    emit_ffn_quarter(fctx, q, uid, xT, y_acc, w_q, wTq)
                    em.barrier()
                    em.wait("gpsimd", "s_st")

        if dbg:
            em.barrier()
            for nm, sb in (("d_qkf", snap_qkf), ("d_qkm", snap_qkm),
                           ("d_stb", snap_stb), ("d_osb", snap_osb)):
                em.emit("gpsimd", lambda g, o=dbg_t[nm], s=sb[:]:
                        g.dma_start(out=o, in_=s), inc="s_st", n=16)
        em.wait("gpsimd", "s_st")
        em.wait("gpsimd", "s_cache")
        em.wait("gpsimd", "s_cache2")

        block = E(nc.Block())
        for eng, meth in (("sync", block.sync), ("tensor", block.tensor),
                          ("vector", block.vector), ("scalar", block.scalar),
                          ("gpsimd", block.gpsimd)):
            def body(E_, _q=em.q[eng]):
                for fn in _q:
                    fn(E_)
            meth(body)

    return nc


def make_inmaps(inputs):
    x = np.asarray(inputs["L0_attn_hidden_states"], np.float32)
    zeros = np.zeros((TPC, D), np.float32)

    def T(a):
        return np.ascontiguousarray(np.asarray(a, np.float32).T)

    def cw_layout(a):
        return np.ascontiguousarray(
            np.asarray(a, np.float32).reshape(KC, P, 4).transpose(1, 0, 2)
            .reshape(P, KC * 4))

    shared = {
        "wqT": T(inputs["L0_attn_Wq"]),
        "wkT": T(inputs["L0_attn_Wk"]),
        "wvT": T(inputs["L0_attn_Wv"]),
        "wgT": T(inputs["L0_attn_Wg"]),
        "woT": T(inputs["L0_attn_Wo"]),
        "cwq": cw_layout(inputs["L0_attn_conv_q"]),
        "cwk": cw_layout(inputs["L0_attn_conv_k"]),
        "cwv": cw_layout(inputs["L0_attn_conv_v"]),
        "winit": np.ascontiguousarray(
            np.asarray(inputs["L0_attn_W_init"], np.float32)[0].transpose(1, 0, 2)),
        "gamma_t": np.ascontiguousarray(
            np.tile(np.asarray(inputs["L0_attn_ln_gamma"], np.float32), H)),
        "beta_t": np.ascontiguousarray(
            np.tile(np.asarray(inputs["L0_attn_ln_beta"], np.float32), H)),
        "wr": np.ascontiguousarray(np.asarray(inputs["L0_ffn_Wr"], np.float32)),
        "br": np.ascontiguousarray(np.asarray(inputs["L0_ffn_br"], np.float32)),
        "w1": np.ascontiguousarray(np.asarray(inputs["L0_ffn_W1"], np.float32)),
        "b1": np.ascontiguousarray(np.asarray(inputs["L0_ffn_b1"], np.float32)),
        "g1": np.ascontiguousarray(np.asarray(inputs["L0_ffn_g1"], np.float32)),
        "be1": np.ascontiguousarray(np.asarray(inputs["L0_ffn_be1"], np.float32)),
        "w2": np.ascontiguousarray(np.asarray(inputs["L0_ffn_W2"], np.float32)),
        "b2": np.ascontiguousarray(np.asarray(inputs["L0_ffn_b2"], np.float32)),
    }
    in_maps = []
    for c in range(8):
        b, hf = c // 2, c % 2
        m = dict(shared)
        m["xm"] = np.ascontiguousarray(x[b, hf * TPC:(hf + 1) * TPC])
        m["xp"] = np.ascontiguousarray(x[b, 0:TPC]) if hf == 1 else zeros
        in_maps.append(m)
    return in_maps


def kernel(**inputs):
    in_maps = make_inmaps(inputs)
    nc = build_nc()
    res = run_bass_kernel_spmd(nc, in_maps, core_ids=list(range(8)))
    outp = np.empty((B, L, D), np.float32)
    for c in range(8):
        b, hf = c // 2, c % 2
        outp[b, hf * TPC:(hf + 1) * TPC] = res.results[c]["out"]
    return outp


# version 14 (final)
# speedup vs baseline: 1.1697x; 1.0028x over previous
import numpy as np
import concourse.bass as bass
import concourse.mybir as mybir
from concourse.bass_utils import run_bass_kernel_spmd
from contextlib import ExitStack

F32 = mybir.dt.float32
BF16 = mybir.dt.bfloat16
AF = mybir.ActivationFunctionType
ALU = mybir.AluOpType
AX = mybir.AxisListType

B, L, D = 4, 8192, 1024
H, DK, NE = 16, 64, 16
P = 128
KC = D // P
TPC = L // 2
NT = 4
QT = NT * P
TH = max(1, QT // 512)
EPS_LN = 1e-5
EPS_L2 = 1e-12

ENGINES = ("sync", "tensor", "vector", "scalar", "gpsimd")
SEMS = ("s_x", "s_stg", "s_stg2", "s_wld", "s_wfree", "s_ln", "s_lnv",
        "s_mm", "s_ev", "s_dv", "s_gp", "s_st", "s_cache", "s_cache2",
        "s_gx", "s_cv")

N_PREF_Q = TPC // QT
N_MAIN_Q = TPC // QT
N_EXP = NE


class EM:
    def __init__(self):
        self.q = {e: [] for e in ENGINES}
        self.cnt = {}
        self.sems = {}

    def sem(self, name, handle):
        self.sems[name] = handle
        self.cnt[name] = 0

    def inc(self, name, n=1):
        self.cnt[name] += n
        return self.cnt[name]

    def emit(self, eng, fn, inc=None, n=1):
        if inc is None:
            self.q[eng].append(fn)
        else:
            s = self.sems[inc]
            self.inc(inc, n)
            self.q[eng].append(lambda E, fn=fn, s=s, n=n: fn(E).then_inc(s, n))

    def wait(self, eng, name, val=None):
        s = self.sems[name]
        v = self.cnt[name] if val is None else val
        if v > 0:
            self.q[eng].append(lambda E, s=s, v=v: E.wait_ge(s, v))

    def barrier(self, engines=ENGINES, sems=("s_mm", "s_ev", "s_dv", "s_lnv",
                                             "s_ln", "s_gp")):
        for e in engines:
            for s in sems:
                self.wait(e, s)


def build_nc(n_pref_q=None, n_main_q=None, n_exp=None, dbg=False, ffn_stage=99):
    n_pref_q = N_PREF_Q if n_pref_q is None else n_pref_q
    n_main_q = N_MAIN_Q if n_main_q is None else n_main_q
    n_exp = N_EXP if n_exp is None else n_exp

    nc = bass.Bass("TRN2", target_bir_lowering=False, debug=False)
    em = EM()

    xm = nc.dram_tensor("xm", [TPC, D], F32, kind="ExternalInput").ap()
    xp = nc.dram_tensor("xp", [TPC, D], F32, kind="ExternalInput").ap()
    wT = {s: nc.dram_tensor(f"w{s}T", [D, D], F32, kind="ExternalInput").ap()
          for s in ("q", "k", "v", "g", "o")}
    cw_d = {s: nc.dram_tensor(f"cw{s}", [P, KC * 4], F32, kind="ExternalInput").ap()
            for s in ("q", "k", "v")}
    winit = nc.dram_tensor("winit", [DK, H, DK], F32, kind="ExternalInput").ap()
    gamma_t = nc.dram_tensor("gamma_t", [D], F32, kind="ExternalInput").ap()
    beta_t = nc.dram_tensor("beta_t", [D], F32, kind="ExternalInput").ap()
    wr = nc.dram_tensor("wr", [D, NE], F32, kind="ExternalInput").ap()
    br = nc.dram_tensor("br", [NE], F32, kind="ExternalInput").ap()
    w1 = nc.dram_tensor("w1", [NE, D, D], F32, kind="ExternalInput").ap()
    b1 = nc.dram_tensor("b1", [NE, D], F32, kind="ExternalInput").ap()
    g1 = nc.dram_tensor("g1", [NE, D], F32, kind="ExternalInput").ap()
    be1 = nc.dram_tensor("be1", [NE, D], F32, kind="ExternalInput").ap()
    w2 = nc.dram_tensor("w2", [NE, D, D], F32, kind="ExternalInput").ap()
    b2 = nc.dram_tensor("b2", [NE, D], F32, kind="ExternalInput").ap()
    out = nc.dram_tensor("out", [TPC, D], F32, kind="ExternalOutput").ap()
    out_t = out.rearrange("(i p) d -> i p d", p=P)

    dbg_t = {}
    if dbg:
        for nm, shape, dt in (("d_xT", [P, KC, QT], BF16),
                              ("d_qt", [P, NT, D], BF16),
                              ("d_kt", [P, NT, D], BF16),
                              ("d_vt", [P, NT, D], BF16),
                              ("d_gt", [P, NT, D], BF16),
                              ("d_wq", [P, NT, NE], F32),
                              ("d_osb", [P, D], F32),
                              ("d_ya", [P, NT, D], F32),
                              ("d_eL", [P, NE], F32),
                              ("d_sums", [P, 8], F32),
                              ("d_pb", [P, QT + 3], F32),
                              ("d_cb", [P, QT], F32),
                              ("d_qkf", [DK, KC, P], BF16),
                              ("d_qkm", [P, 4, P], BF16),
                              ("d_stb", [DK, H, DK], BF16)):
            dbg_t[nm] = nc.dram_tensor(nm, shape, dt, kind="ExternalOutput").ap()

    def dump(nm, src_ap):
        if dbg:
            em.barrier()
            em.emit("gpsimd", lambda g, o=dbg_t[nm], s=src_ap:
                    g.dma_start(out=o, in_=s), inc="s_st", n=16)

    snap_qkf = snap_qkm = snap_stb = snap_osb = None

    g1bf_t = nc.dram_tensor("g1bf", [NE, D], BF16, kind="Internal")
    be1bf_t = nc.dram_tensor("be1bf", [NE, D], BF16, kind="Internal")
    b1bf_t = nc.dram_tensor("b1bf", [NE, D], BF16, kind="Internal")
    g1bf, be1bf, b1bf = g1bf_t.ap(), be1bf_t.ap(), b1bf_t.ap()
    cache_proj = nc.dram_tensor("cache_proj", [5, P, KC, D], BF16, kind="Internal").ap()
    cache_w1 = nc.dram_tensor("cache_w1", [NE, P, KC, D], BF16, kind="Internal").ap()
    cache_w2 = nc.dram_tensor("cache_w2", [NE, P, KC, D], BF16, kind="Internal").ap()

    PROJ_IDX = {"q": 0, "k": 1, "v": 2, "g": 3, "o": 4}

    with ExitStack() as top:
        E = top.enter_context
        for s in SEMS:
            em.sem(s, E(nc.semaphore(s)))
        if dbg:
            snap_qkf = E(nc.sbuf_tensor("snap_qkf", [DK, KC, P], BF16))
            snap_qkm = E(nc.sbuf_tensor("snap_qkm", [P, 4, P], BF16))
            snap_stb = E(nc.sbuf_tensor("snap_stb", [DK, H, DK], BF16))
            snap_osb = E(nc.sbuf_tensor("snap_osb", [P, D], F32))

        identb = E(nc.sbuf_tensor("identb", [P, P], BF16, side="right"))
        identf = E(nc.sbuf_tensor("identf", [P, P], F32, side="right"))
        maskT = E(nc.sbuf_tensor("maskT", [P, P], F32, side="right"))
        ones1f = E(nc.sbuf_tensor("ones1f", [1, P], F32, side="right"))
        ones1b = E(nc.sbuf_tensor("ones1b", [1, P], BF16, side="right"))
        onecol = E(nc.sbuf_tensor("onecol", [P, 1], F32, side="right"))
        epsln = E(nc.sbuf_tensor("epsln", [P, 1], F32, side="right"))
        epsl2 = E(nc.sbuf_tensor("epsl2", [P, 1], F32, side="right"))
        gamB = E(nc.sbuf_tensor("gamB", [P, D], F32, side="right"))
        betB = E(nc.sbuf_tensor("betB", [P, D], F32, side="right"))
        winitB = E(nc.sbuf_tensor("winitB", [DK, H, DK], F32, side="right"))
        cw_sb = {s: E(nc.sbuf_tensor(f"cw_{s}", [P, KC * 4], F32, side="right"))
                 for s in ("q", "k", "v")}
        wr_sb = E(nc.sbuf_tensor("wr_sb", [P, KC, NE], BF16, side="right"))
        brb = E(nc.sbuf_tensor("brb", [1, NE], BF16, side="right"))
        b2sb = E(nc.sbuf_tensor("b2sb", [NE, D], BF16, side="right"))
        setup_ctx = ExitStack()
        b2f = setup_ctx.enter_context(nc.sbuf_tensor("b2f", [NE, D], F32))
        rowf = setup_ctx.enter_context(nc.sbuf_tensor("rowf", [NE, D], F32))
        rowb = setup_ctx.enter_context(nc.sbuf_tensor("rowb", [NE, D], BF16))
        grow = setup_ctx.enter_context(nc.sbuf_tensor("grow", [1, D], F32))
        grow2 = setup_ctx.enter_context(nc.sbuf_tensor("grow2", [1, D], F32))
        sums = E(nc.sbuf_tensor("sums", [P, 8], F32, side="right"))
        s16a = E(nc.sbuf_tensor("s16a", [P, NE], F32, side="right"))
        s16b = E(nc.sbuf_tensor("s16b", [P, NE], F32, side="right"))
        s16c = E(nc.sbuf_tensor("s16c", [P, NE], F32, side="right"))
        eL = E(nc.sbuf_tensor("eL", [P, NE], F32, side="right"))
        rtmp = E(nc.sbuf_tensor("rtmp", [P, NE], F32, side="right"))
        scr1 = E(nc.sbuf_tensor("scr1", [P, D], F32, side="right"))
        halo = {s: E(nc.sbuf_tensor(f"halo_{s}", [P, KC * 3], F32, side="right"))
                for s in ("q", "k", "v")}
        bstage = [E(nc.sbuf_tensor(f"bstg{j}", [P, D], F32, side="right"))
                  for j in range(4)]
        bconv = [E(nc.sbuf_tensor(f"bcv{j}", [P, D], BF16, side="right"))
                 for j in range(4)]
        A_sb = [E(nc.sbuf_tensor(f"A_sb{j}", [DK, H, DK], F32, side="right"))
                for j in range(2)]

        def gp_setup(g):
            g.memset(identb[:], 0.0)
            g.memset(identf[:], 0.0)
            g.memset(maskT[:], 1.0)
            g.memset(ones1f[:], 1.0)
            g.memset(ones1b[:], 1.0)
            g.memset(onecol[:], 1.0)
            g.memset(epsln[:], EPS_LN)
            g.memset(epsl2[:], EPS_L2)
            for s in ("q", "k", "v"):
                g.memset(halo[s][:], 0.0)
            g.memset(sums[:], 0.0)
            g.memset(s16a[:], 0.0)
            g.memset(s16b[:], 0.0)
            g.memset(s16c[:], 0.0)
            g.memset(eL[:], 0.0)
            g.memset(rtmp[:], 0.0)
            return g.memset(scr1[:], 0.0)
        em.emit("gpsimd", gp_setup, inc="s_gp")
        em.wait("gpsimd", "s_gp")

        def gp_setup2(g):
            g.affine_select(out=identb[:], in_=identb[:],
                            compare_op=ALU.not_equal, fill=1.0, base=0,
                            pattern=[[-1, P]], channel_multiplier=1)
            g.affine_select(out=identf[:], in_=identf[:],
                            compare_op=ALU.not_equal, fill=1.0, base=0,
                            pattern=[[-1, P]], channel_multiplier=1)
            return g.affine_select(out=maskT[:], in_=maskT[:],
                                   compare_op=ALU.is_ge, fill=0.0, base=0,
                                   pattern=[[1, P]], channel_multiplier=-1)
        em.emit("gpsimd", gp_setup2, inc="s_gp")

        setup_loads = [
            (winitB[:], winit),
            (cw_sb["q"][:], cw_d["q"]),
            (cw_sb["k"][:], cw_d["k"]),
            (cw_sb["v"][:], cw_d["v"]),
            (grow[:], gamma_t[None, :]),
            (grow2[:], beta_t[None, :]),
            (b2f[:], b2),
            (scr1[0:1, 0:NE], br[None, :]),
            (scr1[:, 128:128 + KC * NE].rearrange("p (c e) -> p c e", c=KC),
             wr.rearrange("(c p) e -> p c e", p=P)),
        ]
        em.wait("sync", "s_gp")
        for o_, i_ in setup_loads:
            em.emit("sync", lambda sy, o=o_, i=i_: sy.dma_start(out=o, in_=i),
                    inc="s_x", n=16)

        em.wait("vector", "s_x")
        def v_setup0(v):
            return v.tensor_copy(out=A_sb[0][:], in_=winitB[:])
        em.emit("vector", v_setup0, inc="s_dv")
        for srcd, dstd in ((g1, g1bf), (be1, be1bf), (b1, b1bf)):
            em.emit("sync", lambda sy, s=srcd: sy.dma_start(out=rowf[:], in_=s),
                    inc="s_x", n=16)
            em.wait("vector", "s_x")
            em.emit("vector", lambda v: v.tensor_copy(out=rowb[:], in_=rowf[:]),
                    inc="s_dv")
            em.wait("gpsimd", "s_dv")
            em.emit("gpsimd", lambda g, d=dstd: g.dma_start(out=d, in_=rowb[:]),
                    inc="s_cache", n=16)
            em.wait("sync", "s_cache")
        def v_setup1(v):
            for si_, st in enumerate(("q", "k", "v")):
                for mc in range(KC):
                    sl = slice(mc * 4 + 3, mc * 4 + 4)
                    v.tensor_scalar(out=scr1[:, 32 + si_ * KC + mc: 33 + si_ * KC + mc],
                                    in0=cw_sb[st][:, sl],
                                    scalar1=1.0, scalar2=None, op0=ALU.add)
            v.tensor_copy(out=b2sb[:], in_=b2f[:])
            v.tensor_copy(out=brb[:], in_=scr1[0:1, 0:NE])
            return v.tensor_copy(
                out=wr_sb[:], in_=scr1[:, 128:128 + KC * NE].rearrange(
                    "p (c e) -> p c e", c=KC))
        em.emit("vector", v_setup1, inc="s_dv")
        em.wait("vector", "s_dv")
        def v_setup2(v):
            for si_, st in enumerate(("q", "k", "v")):
                for mc in range(KC):
                    sl = slice(mc * 4 + 3, mc * 4 + 4)
                    v.tensor_copy(out=cw_sb[st][:, sl],
                                  in_=scr1[:, 32 + si_ * KC + mc: 33 + si_ * KC + mc])
            return v.tensor_copy(out=scr1[:, 32:33], in_=scr1[:, 33:34])
        em.emit("vector", v_setup2, inc="s_dv")

        with nc.psum_tensor("ps_gb", [P, 512], F32) as ps_gb, \
             nc.psum_tensor("ps_gb2", [P, 512], F32) as ps_gb2:
            em.wait("tensor", "s_x")
            em.wait("tensor", "s_gp")
            for hf in range(2):
                sl = slice(hf * 512, (hf + 1) * 512)
                em.emit("tensor", lambda t, sl=sl: t.matmul(
                    ps_gb[:], ones1f[:], grow[:, sl], start=True, stop=True))
                em.emit("tensor", lambda t, sl=sl: t.matmul(
                    ps_gb2[:], ones1f[:], grow2[:, sl], start=True, stop=True),
                    inc="s_mm")
                em.wait("scalar", "s_mm")
                em.emit("scalar", lambda s, sl=sl: s.copy(
                    out=gamB[:, sl], in_=ps_gb[:]))
                em.emit("scalar", lambda s, sl=sl: s.copy(
                    out=betB[:, sl], in_=ps_gb2[:]), inc="s_ev")
                em.wait("tensor", "s_ev")

        em.barrier()
        setup_ctx.close()

        state = {"proj_cached": set(), "exp_cached": False,
                 "wfree_t": {}, "A_par": 0, "A_upd": 0,
                 "cache_done": {}, "bpar": 0}

        def emit_cache_build(name, src_fp32, cache_ap):
            stg_t = state.setdefault("b_stg_t", {})
            outt = state.setdefault("b_out_t", {})
            srcs = state.setdefault("b_srcs", {})
            SIN = ("s_stg", "s_stg2")
            SOUT = ("s_cache", "s_cache2")

            def piece(pn):
                em.wait("gpsimd", SIN[pn % 2], stg_t[pn])
                if pn - 2 in outt:
                    em.wait("gpsimd", SOUT[pn % 2], outt[pn - 2])
                em.emit("gpsimd", lambda g, i=bstage[pn % 2][:],
                        o=bconv[pn % 2][:]: g.tensor_copy(out=o, in_=i))
                em.emit("gpsimd", lambda g, i=bconv[pn % 2][:],
                        o=srcs[pn][1]: g.dma_start(out=o, in_=i),
                        inc=SOUT[pn % 2], n=16)
                outt[pn] = em.cnt[SOUT[pn % 2]]

            for kc in range(KC):
                n = state["bpar"]
                state["bpar"] += 1
                srcs[n] = (src_fp32[:, kc, :], cache_ap[:, kc, :])
                em.emit("gpsimd", lambda g, o=bstage[n % 2][:],
                        s=srcs[n][0]: g.dma_start(out=o, in_=s),
                        inc=SIN[n % 2], n=16)
                stg_t[n] = em.cnt[SIN[n % 2]]
                if n - 1 >= 0 and n - 1 not in outt:
                    piece(n - 1)
            piece(state["bpar"] - 1)
            state["cache_done"][name] = (em.cnt["s_cache"],
                                         em.cnt["s_cache2"])


        def emit_ln_quarter(ctx_unused, src_ap, uid, xT):
            lctx = ExitStack()
            xld = [lctx.enter_context(nc.sbuf_tensor(f"xld{j}_{uid}", [P, D], F32))
                   for j in range(2)]
            xnt = [lctx.enter_context(nc.sbuf_tensor(f"xnt{j}_{uid}", [P, D], BF16))
                   for j in range(2)]
            ps_tp = [lctx.enter_context(nc.psum_tensor(f"lntp{j}_{uid}",
                                                      [P, KC, P], BF16))
                     for j in range(2)]
            src_t = src_ap.rearrange("(i p) d -> i p d", p=P)
            for i in range(NT):
                j = i % 2
                if i >= 2:
                    em.wait("sync", "s_lnv")
                    em.wait("sync", "s_ev")
                em.emit("sync", lambda sy, o=xld[j][:], s=src_t[i]:
                        sy.dma_start(out=o, in_=s), inc="s_x", n=16)
                em.wait("vector", "s_x")
                em.wait("vector", "s_ev")
                em.emit("vector", lambda v, x=xld[j][:]: v.tensor_reduce(
                    out=sums[:, 0:1], in_=x, axis=AX.X, op=ALU.add), inc="s_lnv")
                em.wait("scalar", "s_lnv")
                em.wait("scalar", "s_dv")
                em.emit("scalar", lambda s, x=xld[j][:]: s.activation(
                    out=scr1[:], in_=x, func=AF.Square, accum_out=sums[:, 1:2]),
                    inc="s_ev")
                em.wait("vector", "s_ev")
                em.emit("vector", lambda v: v.tensor_scalar(
                    out=sums[:, 2:3], in0=sums[:, 0:1], scalar1=1.0 / D,
                    scalar2=None, op0=ALU.mult), inc="s_dv")
                em.wait("vector", "s_dv")
                em.emit("vector", lambda v: v.tensor_tensor(
                    out=sums[:, 3:4], in0=sums[:, 2:3], in1=sums[:, 2:3],
                    op=ALU.mult), inc="s_dv")
                em.wait("vector", "s_dv")
                em.emit("vector", lambda v: v.scalar_tensor_tensor(
                    out=sums[:, 4:5], in0=sums[:, 1:2], scalar=1.0 / D,
                    in1=sums[:, 3:4], op0=ALU.mult, op1=ALU.subtract),
                    inc="s_dv")
                em.wait("scalar", "s_dv")
                em.emit("scalar", lambda s: s.activation(
                    out=sums[:, 5:6], in_=sums[:, 4:5], func=AF.Sqrt,
                    bias=epsln[:], scale=1.0), inc="s_ev")
                em.wait("vector", "s_ev")
                em.emit("vector", lambda v: v.reciprocal(
                    out=sums[:, 6:7], in_=sums[:, 5:6]), inc="s_dv")
                em.wait("vector", "s_dv")
                em.emit("vector", lambda v, x=xld[j][:], o=xnt[j][:]:
                        v.tensor_scalar(out=o, in0=x, scalar1=sums[:, 2:3],
                                        scalar2=sums[:, 6:7], op0=ALU.subtract,
                                        op1=ALU.mult), inc="s_lnv")
                em.wait("tensor", "s_lnv")
                if i >= 2:
                    em.wait("tensor", "s_ln")
                for c in range(KC):
                    em.emit("tensor", lambda t, c=c, ps=ps_tp[j],
                            src=xnt[j][:, c * P:(c + 1) * P]:
                            t.transpose(ps[:, c, :], src, identb[:]),
                            inc=("s_mm" if c == KC - 1 else None))
                em.wait("scalar", "s_mm")
                em.emit("scalar", lambda s, ps=ps_tp[j],
                        o=xT[:, :, i * P:(i + 1) * P]:
                        s.copy(out=o, in_=ps[:]), inc="s_ln")
            em.barrier()
            lctx.close()

        def emit_weight_load(dst_sb, cache_ap, name, wait_free_val):
            em.wait("sync", "s_wfree", wait_free_val)
            cd = state["cache_done"][name]
            em.wait("sync", "s_cache", cd[0])
            em.wait("sync", "s_cache2", cd[1])
            em.emit("sync", lambda sy, o=dst_sb[:], s=cache_ap:
                    sy.dma_start(out=o, in_=s), inc="s_wld", n=16)
            return em.cnt["s_wld"]

        def emit_proj_stream(sname, xT, dest, uid, do_conv, do_norm, wp,
                             last=False):
            si = PROJ_IDX[sname]
            with ExitStack() as ctx:
                psP = [ctx.enter_context(nc.psum_tensor(
                    f"psP{j}_{uid}_{sname}", [P, min(512, QT)], F32)) for j in range(2)]
                ps_tp = [ctx.enter_context(nc.psum_tensor(
                    f"ptp{j}_{uid}_{sname}", [P, KC, P], BF16)) for j in range(2)]
                pbuf = [ctx.enter_context(nc.sbuf_tensor(
                    f"pb{j}_{uid}_{sname}", [P, QT + 3], F32)) for j in range(2)]
                cA = ctx.enter_context(nc.sbuf_tensor(f"cA_{uid}_{sname}", [P, QT], F32))
                cB = ctx.enter_context(nc.sbuf_tensor(f"cB_{uid}_{sname}", [P, QT], F32))
                csb = ctx.enter_context(nc.sbuf_tensor(f"csb_{uid}_{sname}",
                                                       [P, KC, QT], BF16))
                knr = ctx.enter_context(nc.sbuf_tensor(
                    f"knr_{uid}_{sname}", [P, D], BF16)) if do_norm else None

                w_ready = emit_weight_load(
                    wp, cache_proj[si], f"proj_{sname}",
                    state["wfree_t"].get(id(wp), em.cnt["s_wfree"]))

                em.wait("tensor", "s_wld", w_ready)
                em.wait("tensor", "s_ev")
                evac_t = {}
                for u in range(TH * KC):
                    mc, th = u // TH, u % TH
                    j = u % 2
                    if u >= 2:
                        em.wait("tensor", "s_ev", evac_t[u - 2])
                    for kc in range(KC):
                        em.emit("tensor", lambda t, ps=psP[j],
                                l=wp[:, kc, mc * P:(mc + 1) * P],
                                r=xT[:, kc, th * 512:min(QT, (th + 1) * 512)],
                                st=(kc == 0), sp=(kc == KC - 1):
                                t.matmul(ps[:], l, r, start=st, stop=sp,
                                         skip_group_check=True),
                                inc=("s_mm" if kc == KC - 1 else None))
                    em.wait("scalar", "s_mm")
                    if do_conv:
                        em.wait("scalar", "s_dv")
                        em.emit("scalar", lambda s, ps=psP[j],
                                o=pbuf[mc % 2][:, 3 + th * 512: 3 + min(QT, (th + 1) * 512)]:
                                s.copy(out=o, in_=ps[:]), inc="s_ev")
                    else:
                        em.emit("scalar", lambda s, ps=psP[j],
                                o=csb[:, mc, th * 512:min(QT, (th + 1) * 512)]:
                                s.copy(out=o, in_=ps[:]), inc="s_ev")
                    evac_t[u] = em.cnt["s_ev"]
                    if th == TH - 1 and do_conv:
                        jj = mc % 2
                        cw = cw_sb[sname]
                        em.wait("vector", "s_ev")
                        em.emit("vector", lambda v, o=pbuf[jj][:, 0:3],
                                i=halo[sname][:, mc * 3:(mc + 1) * 3]:
                                v.tensor_copy(out=o, in_=i))
                        em.emit("vector", lambda v, p=pbuf[jj][:, 3:3 + QT],
                                w3=cw[:, mc * 4 + 3: mc * 4 + 4]:
                                v.tensor_scalar(out=cA[:], in0=p, scalar1=w3,
                                                scalar2=None, op0=ALU.mult))
                        chain = [(cA, cB), (cB, cA), (cA, cB)]
                        for d in (1, 2, 3):
                            ai, ao = chain[d - 1]
                            em.emit("vector", lambda v, d=d,
                                    p=pbuf[jj][:, 3 - d:3 + QT - d],
                                    w=cw[:, mc * 4 + (3 - d): mc * 4 + (4 - d)],
                                    ai=ai, ao=ao:
                                    v.scalar_tensor_tensor(
                                        out=ao[:], in0=p, scalar=w, in1=ai[:],
                                        op0=ALU.mult, op1=ALU.add))
                        em.emit("vector", lambda v, i=pbuf[jj][:, QT:QT + 3],
                                o=halo[sname][:, mc * 3:(mc + 1) * 3]:
                                v.tensor_copy(out=o, in_=i), inc="s_dv")
                        em.wait("scalar", "s_dv")
                        em.emit("scalar", lambda s, o=csb[:, mc, :]:
                                s.activation(out=o, in_=cB[:], func=AF.Silu),
                                inc="s_ev")
                        if dbg and sname == "v" and uid == "m0" and mc == KC - 1:
                            em.wait("gpsimd", "s_ev")
                            em.emit("gpsimd", lambda g, pb=pbuf[jj]: g.dma_start(
                                out=dbg_t["d_pb"], in_=pb[:]), inc="s_st", n=16)
                            em.emit("gpsimd", lambda g: g.dma_start(
                                out=dbg_t["d_cb"], in_=cB[:]), inc="s_st", n=16)
                            em.wait("scalar", "s_st")
                            em.wait("vector", "s_st")
                if dest is not None:
                    for i in range(NT):
                        j = i % 2
                        em.wait("tensor", "s_ev")
                        if i >= 2:
                            em.wait("tensor", "s_ln")
                        if do_norm:
                            em.wait("tensor", "s_lnv")
                        for c in range(KC):
                            em.emit("tensor", lambda t, c=c, ps=ps_tp[j],
                                    src=csb[:, c, i * P:(i + 1) * P]:
                                    t.transpose(ps[:, c, :], src, identb[:]),
                                    inc=("s_mm" if c == KC - 1 else None))
                        em.wait("scalar", "s_mm")
                        tgt = knr[:] if do_norm else dest[:, i, :]
                        em.emit("scalar", lambda s, ps=ps_tp[j], o=tgt:
                                s.copy(out=o, in_=ps[:]), inc="s_ln")
                        if do_norm:
                            em.wait("scalar", "s_lnv")
                            em.wait("scalar", "s_dv")
                            em.emit("scalar", lambda s: s.activation(
                                out=scr1[:], in_=knr[:], func=AF.Square),
                                inc="s_ev")
                            em.wait("vector", "s_ev")
                            em.emit("vector", lambda v: v.tensor_reduce(
                                out=s16a[:],
                                in_=scr1[:].rearrange("p (h k) -> p h k", k=DK),
                                axis=AX.X, op=ALU.add), inc="s_dv")
                            em.wait("scalar", "s_dv")
                            em.emit("scalar", lambda s: s.activation(
                                out=s16b[:], in_=s16a[:], func=AF.Sqrt,
                                bias=epsl2[:], scale=1.0), inc="s_ev")
                            em.wait("vector", "s_ev")
                            em.emit("vector", lambda v: v.reciprocal(
                                out=s16c[:], in_=s16b[:]), inc="s_dv")
                            em.wait("vector", "s_dv")
                            for h in range(H):
                                em.emit("vector", lambda v, h=h, i=i:
                                        v.tensor_scalar(
                                            out=dest[:, i, h * DK:(h + 1) * DK],
                                            in0=knr[:, h * DK:(h + 1) * DK],
                                            scalar1=s16c[:, h:h + 1],
                                            scalar2=None, op0=ALU.mult),
                                        inc=("s_lnv" if h == H - 1 else None))
                em.emit("tensor", lambda t: t.nop(), inc="s_wfree")
                state["wfree_t"][id(wp)] = em.cnt["s_wfree"]
                em.barrier()

        def emit_A_update(A_tmp, k_t, v_t, i):
            em.wait("tensor", "s_dv", state["A_upd"])
            for h in range(H):
                em.emit("tensor", lambda t, h=h,
                        st=(h == 0 or h == 8),
                        l=k_t[:, i, h * DK:(h + 1) * DK],
                        r=v_t[:, i, h * DK:(h + 1) * DK]:
                        t.matmul(A_tmp[:, h, :], l, r, start=st, stop=True,
                                 skip_group_check=True),
                        inc=("s_mm" if h == H - 1 else None))
            em.wait("vector", "s_mm")
            par = state["A_par"]
            em.emit("vector", lambda v, a=A_sb[par][:], b=A_sb[par ^ 1][:],
                    ps=A_tmp[:]:
                    v.tensor_tensor(out=b, in0=a, in1=ps, op=ALU.add),
                    inc="s_dv")
            state["A_par"] = par ^ 1
            state["A_upd"] = em.cnt["s_dv"]

        def emit_scan_quarter(ctx, q, uid, q_t, k_t, v_t, gate_t, y_acc,
                              wo_sb, xsrc):
            ps_sm = [ctx.enter_context(nc.psum_tensor(f"pssm{j}_{uid}",
                                                      [P, KC, P], BF16))
                     for j in range(2)]
            ps_qk = [ctx.enter_context(nc.psum_tensor(f"psqk{j}_{uid}",
                                                      [P, 4, P], F32))
                     for j in range(2)]
            ps_o = ctx.enter_context(nc.psum_tensor(f"pso_{uid}", [P, H, DK], F32))
            A_tmp = ctx.enter_context(nc.psum_tensor(f"Atmp_{uid}", [DK, H, DK], F32))
            qkf = [ctx.enter_context(nc.sbuf_tensor(f"qkf{j}_{uid}",
                                                    [DK, KC, P], BF16))
                   for j in range(2)]
            qkm = [ctx.enter_context(nc.sbuf_tensor(f"qkm{j}_{uid}",
                                                    [P, 4, P], BF16))
                   for j in range(2)]
            st_b = ctx.enter_context(nc.sbuf_tensor(f"stb_{uid}", [DK, H, DK], BF16))
            o_sb = ctx.enter_context(nc.sbuf_tensor(f"osb_{uid}", [P, D], F32))
            scr2 = ctx.enter_context(nc.sbuf_tensor(f"oscr_{uid}", [P, D], F32))
            on_sb = scr1
            og_sb = ctx.enter_context(nc.sbuf_tensor(f"ogsb_{uid}", [P, D], F32))
            og2 = scr2
            ogb = ctx.enter_context(nc.sbuf_tensor(f"ogb_{uid}", [P, D], BF16))
            ogT = ctx.enter_context(nc.sbuf_tensor(f"ogT_{uid}", [P, KC, P], BF16))
            xld2 = [ctx.enter_context(nc.sbuf_tensor(f"x2{j}_{uid}", [P, D], F32))
                    for j in range(2)]
            xsrc_t = xsrc.rearrange("(i p) d -> i p d", p=P)

            for i in range(NT):
                em.wait("vector", "s_dv", state["A_upd"])
                em.emit("vector", lambda v, a=A_sb[state["A_par"]][:]:
                        v.tensor_copy(out=st_b[:], in_=a), inc="s_dv")
                st_ready = em.cnt["s_dv"]
                if i >= 2:
                    em.wait("gpsimd", "s_ev")
                em.emit("gpsimd", lambda g, o=xld2[i % 2][:], s=xsrc_t[i]:
                        g.dma_start(out=o, in_=s), inc="s_gx", n=16)
                for grp in range(4):
                    j = grp % 2
                    em.wait("tensor", "s_ev")
                    for hh in range(4):
                        h = grp * 4 + hh
                        em.emit("tensor", lambda t, hh=hh, ps=ps_sm[j],
                                src=q_t[:, i, h * DK:(h + 1) * DK]:
                                t.transpose(ps[0:DK, hh * 2, :], src,
                                            identb[:]))
                        em.emit("tensor", lambda t, hh=hh, ps=ps_sm[j],
                                src=k_t[:, i, h * DK:(h + 1) * DK]:
                                t.transpose(ps[0:DK, hh * 2 + 1, :], src,
                                            identb[:]),
                                inc=("s_mm" if hh == 3 else None))
                    em.wait("scalar", "s_mm")
                    em.emit("scalar", lambda s, ps=ps_sm[j], o=qkf[j][:]:
                            s.copy(out=o, in_=ps[0:DK]), inc="s_ev")
                    em.wait("tensor", "s_ev")
                    em.wait("tensor", "s_dv")
                    for hh in range(4):
                        em.emit("tensor", lambda t, hh=hh, ps=ps_qk[j], f=qkf[j]:
                                t.matmul(ps[:, hh, :], f[:, hh * 2 + 1, :],
                                         f[:, hh * 2, :], start=(hh == 0),
                                         stop=True, skip_group_check=True),
                                inc=("s_mm" if hh == 3 else None))
                    em.wait("vector", "s_mm")
                    em.wait("vector", "s_dv", st_ready)
                    for hh in range(4):
                        em.emit("vector", lambda v, hh=hh, ps=ps_qk[j], m=qkm[j]:
                                v.tensor_tensor(out=m[:, hh, :],
                                                in0=ps[:, hh, :], in1=maskT[:],
                                                op=ALU.mult),
                                inc=("s_dv" if hh == 3 else None))
                    if dbg and uid == "m0" and i == 0 and grp == 0:
                        em.emit("scalar", lambda s: s.copy(
                            out=snap_qkf[:], in_=qkf[0][:]))
                        em.wait("vector", "s_dv")
                        em.emit("vector", lambda v: v.tensor_copy(
                            out=snap_qkm[:], in_=qkm[0][:]))
                        em.emit("vector", lambda v: v.tensor_copy(
                            out=snap_stb[:], in_=st_b[:]), inc="s_dv")
                    em.wait("tensor", "s_dv")
                    for hh in range(4):
                        h = grp * 4 + hh
                        em.emit("tensor", lambda t, hh=hh, h=h, m=qkm[j],
                                vsrc=v_t[:, i, h * DK:(h + 1) * DK]:
                                t.matmul(ps_o[:, h, :], m[:, hh, :], vsrc,
                                         start=(h == 0 or h == 8), stop=False,
                                         skip_group_check=True))
                        em.emit("tensor", lambda t, hh=hh, h=h, f=qkf[j]:
                                t.matmul(ps_o[:, h, :], f[:, hh * 2, :],
                                         st_b[:, h, :], start=False, stop=True,
                                         skip_group_check=True),
                                inc=("s_mm" if hh == 3 else None))
                emit_A_update(A_tmp, k_t, v_t, i)
                em.wait("scalar", "s_mm")
                em.wait("scalar", "s_dv")
                em.emit("scalar", lambda s: s.copy(out=o_sb[:], in_=ps_o[:]),
                        inc="s_ev")
                if dbg and q == 0 and i == 0:
                    em.emit("scalar", lambda s: s.copy(
                        out=snap_osb[:], in_=o_sb[:]), inc="s_ev")
                em.emit("scalar", lambda s: s.activation(
                    out=scr2[:], in_=o_sb[:], func=AF.Square), inc="s_ev")
                em.wait("vector", "s_ev")
                em.emit("vector", lambda v: v.tensor_reduce(
                    out=s16a[:], in_=o_sb[:].rearrange("p (h k) -> p h k", k=DK),
                    axis=AX.X, op=ALU.add))
                em.emit("vector", lambda v: v.tensor_reduce(
                    out=s16b[:], in_=scr2[:].rearrange("p (h k) -> p h k", k=DK),
                    axis=AX.X, op=ALU.add), inc="s_dv")
                em.wait("vector", "s_dv")
                em.emit("vector", lambda v: v.tensor_scalar(
                    out=rtmp[:], in0=s16a[:], scalar1=1.0 / DK, scalar2=None,
                    op0=ALU.mult), inc="s_dv")
                em.wait("vector", "s_dv")
                em.emit("vector", lambda v: v.tensor_tensor(
                    out=s16a[:], in0=rtmp[:], in1=rtmp[:], op=ALU.mult),
                    inc="s_dv")
                em.wait("vector", "s_dv")
                em.emit("vector", lambda v: v.scalar_tensor_tensor(
                    out=eL[:], in0=s16b[:], scalar=1.0 / DK, in1=s16a[:],
                    op0=ALU.mult, op1=ALU.subtract), inc="s_dv")
                em.wait("scalar", "s_dv")
                em.emit("scalar", lambda s: s.activation(
                    out=s16b[:], in_=eL[:], func=AF.Sqrt, bias=epsln[:],
                    scale=1.0), inc="s_ev")
                em.wait("vector", "s_ev")
                em.emit("vector", lambda v: v.reciprocal(
                    out=s16a[:], in_=s16b[:]), inc="s_dv")
                em.wait("vector", "s_dv")
                for h in range(H):
                    em.emit("vector", lambda v, h=h: v.tensor_scalar(
                        out=on_sb[:, h * DK:(h + 1) * DK],
                        in0=o_sb[:, h * DK:(h + 1) * DK],
                        scalar1=rtmp[:, h:h + 1], scalar2=s16a[:, h:h + 1],
                        op0=ALU.subtract, op1=ALU.mult))
                em.emit("vector", lambda v: v.scalar_tensor_tensor(
                    out=og_sb[:], in0=on_sb[:], scalar=1.0, in1=gamB[:],
                    op0=ALU.mult, op1=ALU.mult), inc="s_dv")
                em.wait("gpsimd", "s_dv")
                em.emit("gpsimd", lambda g: g.tensor_tensor(
                    out=og2[:], in0=og_sb[:], in1=betB[:], op=ALU.add),
                    inc="s_gp")
                em.wait("vector", "s_gp")
                em.wait("vector", "s_gx")
                em.emit("vector", lambda v, i=i: v.tensor_tensor(
                    out=ogb[:], in0=og2[:], in1=gate_t[:, i, :], op=ALU.mult),
                    inc="s_dv")
                em.wait("tensor", "s_dv")
                em.wait("tensor", "s_ev")
                for c in range(KC):
                    em.emit("tensor", lambda t, c=c: t.transpose(
                        ps_sm[0][:, c, :], ogb[:, c * P:(c + 1) * P], identb[:]),
                        inc=("s_mm" if c == KC - 1 else None))
                em.wait("scalar", "s_mm")
                em.emit("scalar", lambda s: s.copy(out=ogT[:], in_=ps_sm[0][:]),
                        inc="s_ev")
                em.wait("tensor", "s_ev")
                po = ps_o[:].rearrange("p h k -> p (h k)")
                for th in range(2):
                    for kc in range(KC):
                        em.emit("tensor", lambda t, th=th, kc=kc, po=po:
                                t.matmul(po[:, th * 512:(th + 1) * 512],
                                         ogT[:, kc, :],
                                         wo_sb[:, kc, th * 512:(th + 1) * 512],
                                         start=(kc == 0), stop=(kc == KC - 1),
                                         skip_group_check=True),
                                inc=("s_mm" if th == 1 and kc == KC - 1
                                     else None))
                em.wait("vector", "s_mm")
                em.emit("vector", lambda v, i=i, po=po: v.tensor_tensor(
                    out=y_acc[:, i, :], in0=po, in1=xld2[i % 2][:], op=ALU.add),
                    inc="s_ev")

        def emit_router_quarter(xT, w_q, wTq, uid):
            with ExitStack() as rctx:
                ps_r_raw = rctx.enter_context(nc.psum_tensor(f"psr_{uid}",
                                                             [P, 512], F32))
                ps_r = ps_r_raw[:].rearrange("p (i e) -> p i e", e=NE)
                ps_wt = rctx.enter_context(nc.psum_tensor(f"pswt_{uid}",
                                                          [NE, 4, P], F32))
                em.wait("tensor", "s_ev")
                for i in range(NT):
                    for kc in range(KC):
                        em.emit("tensor", lambda t, i=i, kc=kc: t.matmul(
                            ps_r[:, i, :], xT[:, kc, i * P:(i + 1) * P],
                            wr_sb[:, kc, :], start=(i == 0 and kc == 0),
                            stop=False, skip_group_check=True))
                    em.emit("tensor", lambda t, i=i: t.matmul(
                        ps_r[:, i, :], ones1b[:], brb[:], start=False,
                        stop=True, skip_group_check=True), inc="s_mm")
                    em.wait("vector", "s_mm")
                    em.wait("vector", "s_ev")
                    em.emit("vector", lambda v, i=i: v.reduce_max(
                        out=sums[:, 0:1], in_=ps_r[:, i, :], axis=AX.X,
                        negate=True), inc="s_dv")
                    em.wait("scalar", "s_dv")
                    em.emit("scalar", lambda s, i=i: s.activation(
                        out=eL[:], in_=ps_r[:, i, :], func=AF.Exp,
                        bias=sums[:, 0:1], scale=1.0), inc="s_ev")
                    em.wait("vector", "s_ev")
                    em.emit("vector", lambda v: v.reduce_max(
                        out=sums[:, 1:2], in_=eL[:], axis=AX.X), inc="s_dv")
                    em.wait("vector", "s_dv")
                    em.emit("vector", lambda v: v.tensor_scalar(
                        out=rtmp[:], in0=eL[:], scalar1=sums[:, 1:2],
                        scalar2=0.0, op0=ALU.subtract, op1=ALU.is_ge),
                        inc="s_dv")
                    em.wait("vector", "s_dv")
                    em.emit("vector", lambda v: v.tensor_tensor(
                        out=s16a[:], in0=eL[:], in1=rtmp[:], op=ALU.mult),
                        inc="s_dv")
                    em.wait("vector", "s_dv")
                    em.emit("vector", lambda v: v.tensor_tensor(
                        out=s16b[:], in0=eL[:], in1=s16a[:], op=ALU.subtract),
                        inc="s_dv")
                    em.wait("vector", "s_dv")
                    em.emit("vector", lambda v: v.reduce_max(
                        out=sums[:, 2:3], in_=s16b[:], axis=AX.X), inc="s_dv")
                    em.wait("vector", "s_dv")
                    em.emit("vector", lambda v: v.tensor_scalar(
                        out=rtmp[:], in0=eL[:], scalar1=sums[:, 2:3],
                        scalar2=0.0, op0=ALU.subtract, op1=ALU.is_ge),
                        inc="s_dv")
                    em.wait("vector", "s_dv")
                    em.emit("vector", lambda v: v.tensor_tensor(
                        out=s16a[:], in0=eL[:], in1=rtmp[:], op=ALU.mult))
                    em.emit("vector", lambda v: v.tensor_tensor(
                        out=sums[:, 3:4], in0=sums[:, 1:2], in1=sums[:, 2:3],
                        op=ALU.add), inc="s_dv")
                    em.wait("vector", "s_dv")
                    em.emit("vector", lambda v: v.reciprocal(
                        out=sums[:, 4:5], in_=sums[:, 3:4]), inc="s_dv")
                    em.wait("vector", "s_dv")
                    em.emit("vector", lambda v, i=i: v.tensor_scalar(
                        out=w_q[:, i, :], in0=s16a[:], scalar1=sums[:, 4:5],
                        scalar2=None, op0=ALU.mult), inc="s_dv")
                    if dbg and i == 0 and uid == "m0":
                        em.wait("gpsimd", "s_dv")
                        em.emit("gpsimd", lambda g: g.dma_start(
                            out=dbg_t["d_eL"], in_=eL[:]), inc="s_st", n=16)
                        em.emit("gpsimd", lambda g: g.dma_start(
                            out=dbg_t["d_sums"], in_=sums[:]), inc="s_st", n=16)
                        em.wait("vector", "s_st")
                    em.wait("tensor", "s_dv")
                    if i >= 2:
                        em.wait("tensor", "s_ev")
                    em.emit("tensor", lambda t, i=i: t.transpose(
                        ps_wt[:, i % 2, :], w_q[:, i, :], identf[:]),
                        inc="s_mm")
                    em.wait("scalar", "s_mm")
                    em.emit("scalar", lambda s, i=i: s.copy(
                        out=wTq[:, i, :], in_=ps_wt[:, i % 2, :]), inc="s_ev")
                em.barrier()

        def emit_ffn_quarter(ctx, q, uid, xT, y_acc, w_q, wTq):
            build = False
            nbuf = 2
            w1sb = [ctx.enter_context(nc.sbuf_tensor(f"w1s{j}_{uid}", [P, KC, D], BF16))
                    for j in range(nbuf)]
            w2sb = [ctx.enter_context(nc.sbuf_tensor(f"w2s{j}_{uid}", [P, KC, D], BF16))
                    for j in range(nbuf)]
            b1rb = [ctx.enter_context(nc.sbuf_tensor(f"b1rb{j}_{uid}", [1, D], BF16))
                    for j in range(2)]
            G1B = [ctx.enter_context(nc.sbuf_tensor(f"G1B{j}_{uid}", [P, D], BF16))
                   for j in range(2)]
            BE1B = [ctx.enter_context(nc.sbuf_tensor(f"BE1B{j}_{uid}", [P, D], BF16))
                    for j in range(2)]
            hn = [ctx.enter_context(nc.sbuf_tensor(f"hn{j}_{uid}", [P, D], F32))
                  for j in range(2)]
            hg = [ctx.enter_context(nc.sbuf_tensor(f"hg{j}_{uid}", [P, D], F32))
                  for j in range(2)]
            hgl = [ctx.enter_context(nc.sbuf_tensor(f"hgl{j}_{uid}", [P, D], BF16))
                   for j in range(2)]
            hw = [ctx.enter_context(nc.sbuf_tensor(f"hw{j}_{uid}", [P, D], BF16))
                  for j in range(2)]
            hT = [ctx.enter_context(nc.sbuf_tensor(f"hT{j}_{uid}", [P, KC, P], BF16))
                  for j in range(2)]
            ytmp = [ctx.enter_context(nc.sbuf_tensor(f"ytmp{j}_{uid}", [P, D], F32))
                    for j in range(2)]
            sums2 = ctx.enter_context(nc.sbuf_tensor(f"sums2_{uid}", [P, 16], F32))
            ps_h = [ctx.enter_context(nc.psum_tensor(f"psh{j}_{uid}", [P, D], F32))
                    for j in range(3)]
            ps_hT = ctx.enter_context(nc.psum_tensor(f"pshT_{uid}", [P, KC, P], BF16))
            ps_y = ctx.enter_context(nc.psum_tensor(f"psy_{uid}", [P, 512], F32))
            em.emit("gpsimd", lambda g: g.memset(sums2[:], 0.0), inc="s_gp")
            em.wait("vector", "s_gp")
            em.wait("scalar", "s_gp")

            T = {}
            base_wfree = em.cnt["s_wfree"]

            def hdr(e):
                j = e % nbuf
                p = e % 2
                wf = base_wfree + max(0, e - 1)
                if e >= 2:
                    em.wait("sync", "s_mm", T[(e - 2, "last_mm")])
                    if T.get((e - 2, "last_stt")):
                        em.wait("sync", "s_dv", T[(e - 2, "last_stt")])
                em.emit("sync", lambda sy, e=e, o=b1rb[p][:]: sy.dma_start(
                    out=o, in_=b1bf[e][None, :]), inc="s_x", n=16)
                em.emit("sync", lambda sy, e=e, o=G1B[p][:]: sy.dma_start(
                    out=o, in_=bass.AP(g1bf_t, e * D, [[0, P], [1, D]])),
                    inc="s_x", n=16)
                em.emit("sync", lambda sy, e=e, o=BE1B[p][:]: sy.dma_start(
                    out=o, in_=bass.AP(be1bf_t, e * D, [[0, P], [1, D]])),
                    inc="s_x", n=16)
                rows_ld = em.cnt["s_x"]
                T[(e, "b1")] = 0
                T[(e, "gbx")] = rows_ld
                emit_weight_load(w1sb[j], cache_w1[e], f"w1_{e}", wf)
                w_ready = emit_weight_load(w2sb[j], cache_w2[e], f"w2_{e}", wf)
                T[(e, "gb")] = 0
                T[(e, "wld")] = w_ready
                T[(e, "rows")] = rows_ld

            def w1_mm(e, i):
                j, p = e % nbuf, e % 2
                par = (e * NT + i) % 3
                pgt = e * NT + i - 3
                if pgt >= 0:
                    pk = divmod(pgt, NT)
                    t1 = T.get((pk[0], pk[1], "hn"))
                    t2 = T.get((pk[0], pk[1], "sq"))
                    if t1:
                        em.wait("tensor", "s_dv", t1)
                    if t2:
                        em.wait("tensor", "s_ev", t2)
                em.wait("tensor", "s_wld", T[(e, "wld")])
                em.wait("tensor", "s_x", T[(e, "rows")])
                for th in range(2):
                    for kc in range(KC):
                        em.emit("tensor", lambda t, th=th, kc=kc, j=j, i=i,
                                k=par:
                                t.matmul(ps_h[k][:, th * 512:(th + 1) * 512],
                                         xT[:, kc, i * P:(i + 1) * P],
                                         w1sb[j][:, kc, th * 512:(th + 1) * 512],
                                         start=(kc == 0), stop=False,
                                         skip_group_check=True))
                    em.emit("tensor", lambda t, th=th, p=p, k=par: t.matmul(
                        ps_h[k][:, th * 512:(th + 1) * 512], ones1b[:],
                        b1rb[p][:, th * 512:(th + 1) * 512], start=False,
                        stop=True, skip_group_check=True),
                        inc=("s_mm" if th == 1 else None))
                T[(e, i, "w1")] = em.cnt["s_mm"]

            def epiA(e, i):
                if ffn_stage < 21:
                    return
                j, p, k = e % nbuf, e % 2, i % 2
                par = (e * NT + i) % 3
                so = 8 * (i % 2)
                C = lambda c: sums2[:, so + c: so + c + 1]
                V = lambda kk: sums2[:, 6 + kk: 7 + kk]
                em.wait("scalar", "s_mm", T[(e, i, "w1")])
                pk = (e, i - 2) if i >= 2 else (e - 1, NT - 2 + i)
                t2 = T.get((pk[0], pk[1], "v1r")) if pk[0] >= 0 else None
                if t2:
                    em.wait("scalar", "s_dv", t2)
                em.emit("scalar", lambda s, k=par, C=C: s.activation(
                    out=scr1[:], in_=ps_h[k][:], func=AF.Square,
                    accum_out=C(1)), inc="s_ev")
                T[(e, i, "sq")] = em.cnt["s_ev"]
                em.wait("vector", "s_mm", T[(e, i, "w1")])
                em.wait("vector", "s_ev", T[(e, i, "sq")])
                em.emit("vector", lambda v, k=par, C=C: v.tensor_reduce(
                    out=C(0), in_=ps_h[k][:], axis=AX.X, op=ALU.add),
                    inc="s_dv")
                T[(e, i, "rd")] = em.cnt["s_dv"]
                em.wait("vector", "s_dv")
                em.emit("vector", lambda v, C=C: v.tensor_tensor(
                    out=C(3), in0=C(0), in1=C(0), op=ALU.mult))
                em.emit("vector", lambda v, C=C: v.tensor_scalar(
                    out=C(2), in0=C(0), scalar1=1.0 / D, scalar2=None,
                    op0=ALU.mult), inc="s_dv")
                em.wait("vector", "s_dv")
                ppair = (e, i // 2 - 1) if i >= 2 else (e - 1, 1)
                t4 = T.get((ppair[0], ppair[1], "sigp")) if ppair[0] >= 0 \
                    else None
                if t4:
                    em.wait("vector", "s_ev", t4)
                em.emit("vector", lambda v, k=k, C=C, V=V: v.scalar_tensor_tensor(
                    out=V(k), in0=C(1), scalar=float(D), in1=C(3),
                    op0=ALU.mult, op1=ALU.subtract), inc="s_dv")
                T[(e, i, "v1r")] = em.cnt["s_dv"]

            def sigpair(e, ip):
                if ffn_stage < 21:
                    return
                i0, i1 = 2 * ip, 2 * ip + 1
                em.wait("scalar", "s_dv", T[(e, i1, "v1r")])
                ppair = (e, ip - 1) if ip >= 1 else (e - 1, 1)
                t5 = T.get((ppair[0], ppair[1], "rcp")) if ppair[0] >= 0 \
                    else None
                if t5:
                    em.wait("scalar", "s_dv", t5)
                em.emit("scalar", lambda s: s.activation(
                    out=sums2[:, 12:14], in_=sums2[:, 6:8], func=AF.Sqrt,
                    bias=epsln[:], scale=1.0 / (D * D)), inc="s_ev")
                T[(e, ip, "sigp")] = em.cnt["s_ev"]
                em.wait("vector", "s_ev", T[(e, ip, "sigp")])
                em.emit("vector", lambda v: v.reciprocal(
                    out=sums2[:, 14:16], in_=sums2[:, 12:14]), inc="s_dv")
                T[(e, ip, "rcp")] = em.cnt["s_dv"]
                em.wait("vector", "s_dv")
                for i in (i0, i1):
                    k = i % 2
                    par = (e * NT + i) % 3
                    so = 8 * k
                    pk2 = (e, i - 2) if i >= 2 else (e - 1, NT - 2 + i)
                    t3 = T.get((pk2[0], pk2[1], "gel")) if pk2[0] >= 0 \
                        else None
                    if t3:
                        em.wait("vector", "s_ev", t3)
                    em.emit("vector", lambda v, k=k, par=par, so=so:
                            v.tensor_scalar(
                        out=hn[k][:], in0=ps_h[par][:],
                        scalar1=sums2[:, so + 2: so + 3],
                        scalar2=sums2[:, 14 + k: 15 + k],
                        op0=ALU.subtract, op1=ALU.mult), inc="s_dv")
                    T[(e, i, "hn")] = em.cnt["s_dv"]

            def epiB(e, i):
                if ffn_stage < 26:
                    T[(e, i, "hw")] = 0
                    T[(e, i, "stt")] = 0
                    return
                j, p, k = e % nbuf, e % 2, i % 2
                em.wait("vector", "s_x", T[(e, "rows")])
                em.emit("vector", lambda v, k=k, p=p: v.scalar_tensor_tensor(
                    out=hg[k][:], in0=hn[k][:], scalar=1.0, in1=G1B[p][:],
                    op0=ALU.mult, op1=ALU.mult), inc="s_dv")
                em.wait("vector", "s_dv")
                em.emit("vector", lambda v, k=k, p=p: v.tensor_tensor(
                    out=hn[k][:], in0=hg[k][:], in1=BE1B[p][:], op=ALU.add),
                    inc="s_dv")
                T[(e, i, "stt")] = em.cnt["s_dv"]
                em.wait("scalar", "s_dv", T[(e, i, "stt")])
                em.emit("scalar", lambda s, k=k: s.activation(
                    out=hgl[k][:], in_=hn[k][:], func=AF.Gelu), inc="s_ev")
                T[(e, i, "gel")] = em.cnt["s_ev"]
                em.wait("vector", "s_ev", T[(e, i, "gel")])
                em.emit("vector", lambda v, k=k, e=e, i=i: v.tensor_scalar(
                    out=hw[k][:], in0=hgl[k][:], scalar1=w_q[:, i, e:e + 1],
                    scalar2=None, op0=ALU.mult), inc="s_dv")
                T[(e, i, "hw")] = em.cnt["s_dv"]

            def t_w2(e, i):
                j, p, k = e % nbuf, e % 2, i % 2
                if ffn_stage < 27:
                    if i == NT - 1:
                        T[(e, "last_mm")] = em.cnt["s_mm"]
                        T[(e, "last_stt")] = T.get((e, i, "stt"), 0)
                        if e == n_exp - 1:
                            for i2 in range(NT):
                                em.emit("gpsimd", lambda g, i2=i2,
                                        o=out_t[q * NT + i2],
                                        src=y_acc[:, i2, :]:
                                        g.dma_start(out=o, in_=src),
                                        inc="s_st", n=16)
                        em.emit("tensor", lambda t: t.nop(), inc="s_wfree")
                    return
                em.wait("tensor", "s_dv", T.get((e, i, "hw"), 0))
                prev = T.get((e, i - 1, "hTe")) or T.get((e - 1, NT - 1, "hTe"))
                if prev:
                    em.wait("tensor", "s_ev", prev)
                for c in range(KC):
                    em.emit("tensor", lambda t, c=c, k=k: t.transpose(
                        ps_hT[:, c, :], hw[k][:, c * P:(c + 1) * P], identb[:]),
                        inc=("s_mm" if c == KC - 1 else None))
                T[(e, i, "tp")] = em.cnt["s_mm"]
                em.wait("scalar", "s_mm", T[(e, i, "tp")])
                em.emit("scalar", lambda s, k=k: s.copy(out=hT[k][:],
                                                        in_=ps_hT[:]),
                        inc="s_ev")
                T[(e, i, "hTe")] = em.cnt["s_ev"]
                if ffn_stage < 28:
                    if i == NT - 1:
                        T[(e, "last_mm")] = em.cnt["s_mm"]
                        T[(e, "last_stt")] = T.get((e, i, "stt"), 0)
                        if e == n_exp - 1:
                            for i2 in range(NT):
                                em.emit("gpsimd", lambda g, i2=i2,
                                        o=out_t[q * NT + i2],
                                        src=y_acc[:, i2, :]:
                                        g.dma_start(out=o, in_=src),
                                        inc="s_st", n=16)
                        em.emit("tensor", lambda t: t.nop(), inc="s_wfree")
                    return
                em.wait("tensor", "s_ev", T[(e, i, "hTe")])
                for th in range(2):
                    prev = (T.get((e, i, "yth", 0)) if th == 1 else
                            (T.get((e, i - 1, "yth", 1))
                             or T.get((e - 1, NT - 1, "yth", 1))))
                    if prev:
                        em.wait("tensor", "s_ev", prev)
                    first = (e != 0)
                    if e == 0:
                        em.emit("tensor", lambda t, th=th, i=i: t.matmul(
                            ps_y[:], wTq[:, i, :],
                            b2sb[:, th * 512:(th + 1) * 512], start=True,
                            stop=False, skip_group_check=True))
                    for kc in range(KC):
                        em.emit("tensor", lambda t, th=th, kc=kc, j=j, k=k,
                                st=(first and kc == 0):
                                t.matmul(ps_y[:],
                                         hT[k][:, kc, :],
                                         w2sb[j][:, kc, th * 512:(th + 1) * 512],
                                         start=st, stop=(kc == KC - 1),
                                         skip_group_check=True),
                                inc=("s_mm" if kc == KC - 1 else None))
                    T[(e, i, "w2h", th)] = em.cnt["s_mm"]
                    em.wait("scalar", "s_mm", T[(e, i, "w2h", th)])
                    if th == 0:
                        prev = T.get((e, i - 2, "ya")) or (
                            T.get((e - 1, NT - 2 + i, "ya")) if i < 2 else None)
                        if prev:
                            em.wait("scalar", "s_gp", prev)
                    em.emit("scalar", lambda s, k=k, th=th: s.copy(
                        out=ytmp[k][:, th * 512:(th + 1) * 512], in_=ps_y[:]),
                        inc="s_ev")
                    T[(e, i, "yth", th)] = em.cnt["s_ev"]
                T[(e, i, "w2")] = em.cnt["s_mm"]
                T[(e, "last_mm")] = em.cnt["s_mm"]
                if i == NT - 1:
                    T[(e, "last_stt")] = T[(e, i, "stt")]
                T[(e, i, "yt")] = T[(e, i, "yth", 1)]
                em.wait("gpsimd", "s_ev", T[(e, i, "yt")])
                prev = T.get((e - 1, i, "ya"))
                if prev:
                    em.wait("gpsimd", "s_gp", prev)
                em.emit("gpsimd", lambda g, i=i, k=k: g.tensor_tensor(
                    out=y_acc[:, i, :], in0=y_acc[:, i, :], in1=ytmp[k][:],
                    op=ALU.add), inc="s_gp")
                T[(e, i, "ya")] = em.cnt["s_gp"]
                if e == n_exp - 1:
                    em.emit("gpsimd", lambda g, i=i,
                            o=out_t[q * NT + i], src=y_acc[:, i, :]:
                            g.dma_start(out=o, in_=src), inc="s_st", n=16)
                if i == NT - 1:
                    em.emit("tensor", lambda t: t.nop(), inc="s_wfree")

            if n_exp == 0:
                for i in range(NT):
                    em.emit("gpsimd", lambda g, i=i, o=out_t[q * NT + i],
                            src=y_acc[:, i, :]:
                            g.dma_start(out=o, in_=src), inc="s_st", n=16)
                return
            hdr(0)
            for e in range(n_exp):
                if e + 1 < n_exp:
                    hdr(e + 1)
                w1_mm(e, 0)
                epiA(e, 0)
                w1_mm(e, 1)
                epiA(e, 1)
                sigpair(e, 0)
                w1_mm(e, 2)
                epiB(e, 0)
                w1_mm(e, 3)
                epiB(e, 1)
                t_w2(e, 0)
                epiA(e, 2)
                t_w2(e, 1)
                epiA(e, 3)
                sigpair(e, 1)
                epiB(e, 2)
                t_w2(e, 2)
                epiB(e, 3)
                t_w2(e, 3)

        build_list = [("proj_k", wT["k"], cache_proj[1]),
                      ("proj_v", wT["v"], cache_proj[2]),
                      ("proj_q", wT["q"], cache_proj[0]),
                      ("proj_g", wT["g"], cache_proj[3]),
                      ("proj_o", wT["o"], cache_proj[4])]
        for e_ in range(NE):
            build_list.append((f"w1_{e_}", w1[e_], cache_w1[e_]))
            build_list.append((f"w2_{e_}", w2[e_], cache_w2[e_]))
        for name, s_, c_ in build_list[:3]:
            emit_cache_build(name, s_.rearrange("(c p) d -> p c d", p=P), c_)
        bpos = 3
        nq_total = max(1, n_pref_q)
        per_q = (len(build_list) - 3 + nq_total - 1) // nq_total

        for pq in range(n_pref_q):
            with ExitStack() as ctx:
                xT = ctx.enter_context(nc.sbuf_tensor(f"xTp_{pq}", [P, KC, QT], BF16))
                k_t = ctx.enter_context(nc.sbuf_tensor(f"ktp_{pq}", [P, NT, D], BF16))
                v_t = ctx.enter_context(nc.sbuf_tensor(f"vtp_{pq}", [P, NT, D], BF16))
                wp = [ctx.enter_context(nc.sbuf_tensor(f"wpp{j}_{pq}",
                                                        [P, KC, D], BF16))
                      for j in range(2)]
                emit_ln_quarter(ctx, xp[pq * QT:(pq + 1) * QT], f"p{pq}", xT)
                lastq = (pq == n_pref_q - 1)
                emit_proj_stream("k", xT, k_t, f"p{pq}", True, True, wp[0])
                emit_proj_stream("v", xT, v_t, f"p{pq}", True, False, wp[1],
                                 last=not lastq)
                if lastq:
                    emit_proj_stream("q", xT, None, f"p{pq}", True, False,
                                     wp[0], last=True)
                with nc.psum_tensor(f"Atmp_p{pq}", [DK, H, DK], F32) as A_tmp:
                    for i in range(NT):
                        emit_A_update(A_tmp, k_t, v_t, i)
                    em.barrier()
            for name, s_, c_ in build_list[bpos:bpos + per_q]:
                emit_cache_build(name, s_.rearrange("(c p) d -> p c d", p=P), c_)
            bpos += per_q
        for name, s_, c_ in build_list[bpos:]:
            emit_cache_build(name, s_.rearrange("(c p) d -> p c d", p=P), c_)

        for q in range(n_main_q):
            with ExitStack() as ctx:
                uid = f"m{q}"
                xT = ctx.enter_context(nc.sbuf_tensor(f"xTm_{q}", [P, KC, QT], BF16))
                w_q = ctx.enter_context(nc.sbuf_tensor(f"wq_{q}", [P, NT, NE], F32))
                wTq = ctx.enter_context(nc.sbuf_tensor(f"wTq_{q}", [NE, NT, P], BF16))
                emit_ln_quarter(ctx, xm[q * QT:(q + 1) * QT], uid, xT)
                em.barrier()
                if q == 0:
                    dump("d_xT", xT[:])
                emit_router_quarter(xT, w_q, wTq, uid)
                if q == 0:
                    dump("d_wq", w_q[:])
                y_acc = ctx.enter_context(nc.sbuf_tensor(
                    f"ya_{q}", [P, NT, D], F32))
                with ExitStack() as pctx:
                    q_t = pctx.enter_context(nc.sbuf_tensor(f"qt_{q}", [P, NT, D], BF16))
                    k_t = pctx.enter_context(nc.sbuf_tensor(f"kt_{q}", [P, NT, D], BF16))
                    v_t = pctx.enter_context(nc.sbuf_tensor(f"vt_{q}", [P, NT, D], BF16))
                    gate_t = pctx.enter_context(nc.sbuf_tensor(f"gt_{q}", [P, NT, D], BF16))
                    with ExitStack() as wctx:
                        wp = [wctx.enter_context(nc.sbuf_tensor(
                            f"wpm{j}_{q}", [P, KC, D], BF16)) for j in range(2)]
                        emit_proj_stream("q", xT, q_t, uid, True, True, wp[0])
                        emit_proj_stream("k", xT, k_t, uid, True, True, wp[1])
                        emit_proj_stream("v", xT, v_t, uid, True, False, wp[0])
                        emit_proj_stream("g", xT, gate_t, uid, False, False,
                                         wp[1], last=True)
                    if q == 0:
                        dump("d_qt", q_t[:])
                        dump("d_kt", k_t[:])
                        dump("d_vt", v_t[:])
                        dump("d_gt", gate_t[:])
                    with ExitStack() as sctx:
                        wo_sb = sctx.enter_context(nc.sbuf_tensor(
                            f"wo_{q}", [P, KC, D], BF16))
                        emit_weight_load(wo_sb, cache_proj[PROJ_IDX["o"]],
                                         "proj_o", em.cnt["s_wfree"])
                        emit_scan_quarter(sctx, q, uid, q_t, k_t, v_t, gate_t,
                                          y_acc, wo_sb,
                                          xm[q * QT:(q + 1) * QT])
                        em.emit("tensor", lambda t: t.nop(), inc="s_wfree")
                        em.barrier()
                    if q == 0:
                        dump("d_ya", y_acc[:])
                with ExitStack() as fctx:
                    emit_ffn_quarter(fctx, q, uid, xT, y_acc, w_q, wTq)
                    em.barrier()
                    em.wait("gpsimd", "s_st")

        if dbg:
            em.barrier()
            for nm, sb in (("d_qkf", snap_qkf), ("d_qkm", snap_qkm),
                           ("d_stb", snap_stb), ("d_osb", snap_osb)):
                em.emit("gpsimd", lambda g, o=dbg_t[nm], s=sb[:]:
                        g.dma_start(out=o, in_=s), inc="s_st", n=16)
        em.wait("gpsimd", "s_st")
        em.wait("gpsimd", "s_cache")
        em.wait("gpsimd", "s_cache2")

        block = E(nc.Block())
        for eng, meth in (("sync", block.sync), ("tensor", block.tensor),
                          ("vector", block.vector), ("scalar", block.scalar),
                          ("gpsimd", block.gpsimd)):
            def body(E_, _q=em.q[eng]):
                for fn in _q:
                    fn(E_)
            meth(body)

    return nc


def make_inmaps(inputs):
    x = np.asarray(inputs["L0_attn_hidden_states"], np.float32)
    zeros = np.zeros((TPC, D), np.float32)

    def T(a):
        return np.ascontiguousarray(np.asarray(a, np.float32).T)

    def cw_layout(a):
        return np.ascontiguousarray(
            np.asarray(a, np.float32).reshape(KC, P, 4).transpose(1, 0, 2)
            .reshape(P, KC * 4))

    shared = {
        "wqT": T(inputs["L0_attn_Wq"]),
        "wkT": T(inputs["L0_attn_Wk"]),
        "wvT": T(inputs["L0_attn_Wv"]),
        "wgT": T(inputs["L0_attn_Wg"]),
        "woT": T(inputs["L0_attn_Wo"]),
        "cwq": cw_layout(inputs["L0_attn_conv_q"]),
        "cwk": cw_layout(inputs["L0_attn_conv_k"]),
        "cwv": cw_layout(inputs["L0_attn_conv_v"]),
        "winit": np.ascontiguousarray(
            np.asarray(inputs["L0_attn_W_init"], np.float32)[0].transpose(1, 0, 2)),
        "gamma_t": np.ascontiguousarray(
            np.tile(np.asarray(inputs["L0_attn_ln_gamma"], np.float32), H)),
        "beta_t": np.ascontiguousarray(
            np.tile(np.asarray(inputs["L0_attn_ln_beta"], np.float32), H)),
        "wr": np.ascontiguousarray(np.asarray(inputs["L0_ffn_Wr"], np.float32)),
        "br": np.ascontiguousarray(np.asarray(inputs["L0_ffn_br"], np.float32)),
        "w1": np.ascontiguousarray(np.asarray(inputs["L0_ffn_W1"], np.float32)),
        "b1": np.ascontiguousarray(np.asarray(inputs["L0_ffn_b1"], np.float32)),
        "g1": np.ascontiguousarray(np.asarray(inputs["L0_ffn_g1"], np.float32)),
        "be1": np.ascontiguousarray(np.asarray(inputs["L0_ffn_be1"], np.float32)),
        "w2": np.ascontiguousarray(np.asarray(inputs["L0_ffn_W2"], np.float32)),
        "b2": np.ascontiguousarray(np.asarray(inputs["L0_ffn_b2"], np.float32)),
    }
    in_maps = []
    for c in range(8):
        b, hf = c // 2, c % 2
        m = dict(shared)
        m["xm"] = np.ascontiguousarray(x[b, hf * TPC:(hf + 1) * TPC])
        m["xp"] = np.ascontiguousarray(x[b, 0:TPC]) if hf == 1 else zeros
        in_maps.append(m)
    return in_maps


def kernel(**inputs):
    in_maps = make_inmaps(inputs)
    nc = build_nc()
    res = run_bass_kernel_spmd(nc, in_maps, core_ids=list(range(8)))
    outp = np.empty((B, L, D), np.float32)
    for c in range(8):
        b, hf = c // 2, c % 2
        outp[b, hf * TPC:(hf + 1) * TPC] = res.results[c]["out"]
    return outp
